# version 19
# baseline (speedup 1.0000x reference)
import contextlib
import numpy as np

import concourse.bass as bass
import concourse.mybir as mybir
from concourse.bass_utils import run_bass_kernel_spmd

F32 = mybir.dt.float32
BF16 = mybir.dt.bfloat16
I32 = mybir.dt.int32
AF = mybir.ActivationFunctionType
ALU = mybir.AluOpType
AX = mybir.AxisListType

B, SEQ, D, H, HD, FF = 4, 4096, 1024, 16, 64, 4096
TOK = 2048
NCORES = 8
CSHIFT = 40.0
EPS_LN = 1e-5
EPS_Q = 1e-8
SLOPES = [2.0 ** (-h) for h in range(H)]
NKT = SEQ // 128


def _e1_table():
    i = np.arange(128, dtype=np.float64)[:, None]
    s = np.asarray(SLOPES, dtype=np.float64)[None, :]
    return (s * (i - 127.0) - CSHIFT).astype(np.float32)


def _keep_sets():
    f0, f1 = _fb_raw(0), _fb_raw(1)
    return [[kt for kt in range(NKT) if f0[h][kt] != 0.0 or f1[h][kt] != 0.0]
            for h in range(H)]


def _fb_raw(half):
    t = np.zeros((H, NKT), dtype=np.float32)
    for h in range(H):
        for kt in range(NKT):
            k_true = (half * TOK + kt * 128 + 127) % SEQ
            t[h, kt] = np.float32(np.exp(np.float64(SLOPES[h] * (k_true - 4095))))
    return t


def _fb_table(half):
    t = np.zeros((H, NKT), dtype=np.float32)
    for h in range(H):
        for kt in range(NKT):
            k_true = (half * TOK + kt * 128 + 127) % SEQ
            t[h, kt] = np.float32(np.exp(np.float64(SLOPES[h] * (k_true - 4095))))
    t = t.reshape(1, H * NKT)
    return np.broadcast_to(t, (128, H * NKT)).copy()


class Prog:
    def __init__(self, nc, stack):
        self.nc = nc
        self.stack = stack
        self.ops = None
        self.sem = {}
        self.lastw = {}
        self.last_v = None
        self.last_a = None
        self.last_g = None
        self.hw = {}
        self.synced = {}

    def new_sem(self, name):
        h = self.stack.enter_context(self.nc.semaphore(name))
        self.sem[name] = [h, 0]

    def inc(self, name, n=1):
        s = self.sem[name]
        s[1] += n
        return (name, s[1])

    def now(self, name):
        return (name, self.sem[name][1])

    def handle(self, name):
        return self.sem[name][0]

    DMA_SEMS = ("qS", "qG", "qG2", "qG3", "qA", "qA2", "qO", "qV")

    def wait(self, eng, tok):
        if tok is None:
            return
        name, v = tok
        if name in self.DMA_SEMS:
            v = self.sem[name][1]
            if v > self.hw.get(name, 0):
                self.hw[name] = v
        if v <= 0:
            return
        key = (eng, name)
        if self.lastw.get(key, 0) >= v:
            return
        self.lastw[key] = v
        h = self.sem[name][0]
        self.ops[eng].append(lambda E, h=h, v=v: E.wait_ge(h, v))

    def drain(self, tok):
        for eng in ("sync", "gpsimd", "scalar", "vector", "tensor"):
            self.wait(eng, tok)

    @contextlib.contextmanager
    def phase(self, name):
        self.ops = {"sync": [], "gpsimd": [], "vector": [], "scalar": [], "tensor": []}
        yield
        ops = self.ops
        self.ops = None
        with self.nc.Block(name) as blk:
            for eng_name in ("sync", "gpsimd", "vector", "scalar", "tensor"):
                lst = ops[eng_name]
                if lst:
                    getattr(blk, eng_name)(lambda E, lst=lst: [f(E) for f in lst])


def _build(nc):
    stack = contextlib.ExitStack()
    p = Prog(nc, stack)

    def din(name, shape):
        return nc.dram_tensor(name, shape, F32, kind="ExternalInput").ap()

    xT = din("xT", [D, SEQ])
    wqT = din("wqT", [D, D]); wkT = din("wkT", [D, D])
    wvT = din("wvT", [D, D]); woT = din("woT", [D, D])
    bq_rs = din("bq_rs", [128, 8]); bk_rs = din("bk_rs", [128, 8])
    bo_rs = din("bo_rs", [128, 8]); bv_row = din("bv_row", [1, D])
    AT = din("AT", [D, 24]); AoT = din("AoT", [D, 8])
    BqT = din("BqT", [8, D]); BkT = din("BkT", [8, D])
    BvT = din("BvT", [8, D]); BoT = din("BoT", [8, D])
    rez = din("rez", [1, 1])
    w1 = din("w1", [FF, D]); w1T = din("w1T", [D, FF])
    w2 = din("w2", [D, FF]); w2T = din("w2T", [FF, D])
    b1_rs = din("b1_rs", [128, 32]); b2_rs = din("b2_rs", [128, 8])
    e1tab = din("e1tab", [128, H])
    ident = din("ident", [128, 128])
    fbtab = din("fbtab", [128, H * NKT])
    outT = nc.dram_tensor("outT", [D, TOK], F32, kind="ExternalOutput").ap()

    def dsc(name, shape, dt=F32):
        return nc.dram_tensor(name, shape, dt, kind="Internal").ap()

    xnbf_d = dsc("xnbf_d", [D, SEQ], BF16)
    xn32_d = dsc("xn32_d", [D, TOK])
    h_d = dsc("h_d", [FF, TOK])
    qw2_d = dsc("qw2_d", [FF, D], BF16)
    col_d = dsc("col_d", [128, 1])
    mx1_d = dsc("mx1_d", [128, 1])
    mx2_d = dsc("mx2_d", [128, 1])
    cc1_out = dsc("cc1_out", [128, 1])
    lr_d = [dsc(f"lr_d{i}", [1, 512]) for i in range(2)]
    rsw1_d = dsc("rsw1_d", [32, 128])
    rsw2_d = dsc("rsw2_d", [8, 128])
    r_d = dsc("r_d", [1, SEQ])
    mr_d = dsc("mr_d", [1, SEQ])
    cc2_out = dsc("cc2_out", [128, 1])

    for s in ("qS", "qG", "qG2", "qG3", "qA", "qA2", "qO", "qV", "sV", "sA", "sP", "sG", "sC"):
        p.new_sem(s)

    def sb(name, shape, dt=F32, S=None, side=None):
        return (S or stack).enter_context(nc.sbuf_tensor(name, shape, dt, side=side))

    def psum(name, shape, S=None):
        return (S or stack).enter_context(nc.psum_tensor(name, shape, F32))

    ident_sb = sb("ident_sb", [128, 128])
    ones_col_bf = sb("ones_col_bf", [128, 1], BF16)
    ones_row_f = sb("ones_row_f", [1, 128])
    e1_sb = sb("e1_sb", [128, H])
    fb_sb = sb("fb_sb", [128, H * NKT])

    runmax = sb("runmax", [128, 1])
    hmax = sb("hmax", [128, 1])
    rmx = sb("rmx", [128, 1])
    bqs_sb = sb("bqs_sb", [128, 8])
    bk_sb = sb("bk_sb", [128, 8])
    bo_sb = sb("bo_sb", [128, 8])
    bv_rsb = sb("bv_rsb", [1, D])
    b1i_sb = sb("b1i_sb", [128, 32])
    sc1_sb = sb("sc1_sb", [128, 32])
    sc1f_sb = sb("sc1f_sb", [128, 32])
    b2i_sb = sb("b2i_sb", [128, 8])
    sc2_sb = sb("sc2_sb", [128, 8])
    sw1c = sb("sw1c", [128, 32])
    rw1c = sb("rw1c", [128, 32])
    sw2c = sb("sw2c", [128, 8])
    rw2c = sb("rw2c", [128, 8])
    scl = sb("scl", [128, 8])
    grow = sb("grow", [1, 128])
    gsc = sb("gsc", [1, 8])
    t8a = sb("t8a", [128, 8])
    t8i = sb("t8i", [128, 8], I32)
    t8e = sb("t8e", [128, 8])
    PSB = contextlib.ExitStack()
    ps_b1 = psum("ps_b1", [128, 512], PSB)
    ps_b2 = psum("ps_b2", [128, 512], PSB)

    def v_op(fn_name, **kw):
        tok = p.inc("sV")
        p.wait("vector", p.last_v)
        def f(E, fn_name=fn_name, kw=kw, t=tok):
            getattr(E, fn_name)(**kw).then_inc(p.handle(t[0]), 1)
        p.ops["vector"].append(f)
        p.last_v = tok
        return tok

    def a_op_nochain(fn_name, **kw):
        tok = p.inc("sA")
        def f(E, fn_name=fn_name, kw=kw, t=tok):
            getattr(E, fn_name)(**kw).then_inc(p.handle(t[0]), 1)
        p.ops["scalar"].append(f)
        p.last_a = tok
        return tok

    def a_op(fn_name, **kw):
        tok = p.inc("sA")
        p.wait("scalar", p.last_a)
        def f(E, fn_name=fn_name, kw=kw, t=tok):
            getattr(E, fn_name)(**kw).then_inc(p.handle(t[0]), 1)
        p.ops["scalar"].append(f)
        p.last_a = tok
        return tok

    def dma(queue, out, in_, sem=None):
        semname = sem or {"sync": "qS", "gpsimd": "qG", "scalar": "qA", "vector": "qV"}[queue]
        if queue == "scalar":
            p.wait("scalar", p.last_a)
        elif queue == "gpsimd":
            p.wait("gpsimd", p.last_g)
        elif queue == "vector":
            p.wait("vector", p.last_v)
        hwv = p.hw.get(semname, 0)
        if hwv > p.synced.get(semname, 0):
            p.synced[semname] = hwv
            h = p.handle(semname)
            p.ops[queue].append(lambda E, h=h, v=hwv: E.wait_ge(h, v))
        tok = p.inc(semname, 16)
        p.ops[queue].append(
            lambda E, o=out, i=in_, t=tok: E.dma_start(out=o, in_=i).then_inc(p.handle(t[0]), 16))
        return tok

    def mmop(out, lhsT, rhs, start, stop, tp=None, inc=False):
        tok = p.inc("sP") if inc else None
        def f(E, o=out, l=lhsT, r=rhs, s0=start, s1=stop, tp=tp, t=tok):
            kw = {}
            if tp is not None:
                kw["tile_position"] = tp
            ins = E.matmul(out=o, lhsT=l, rhs=r, start=s0, stop=s1, **kw)
            if t is not None:
                ins.then_inc(p.handle(t[0]), 1)
        p.ops["tensor"].append(f)
        return tok

    def round_ge0(y_ap, i32_ap, e_ap, out_ap, in_tok):
        p.wait("gpsimd", in_tok)
        hop = dma("gpsimd", i32_ap, y_ap)
        p.wait("vector", in_tok)
        p.wait("vector", hop)
        v_op("tensor_sub", out=e_ap, in0=y_ap, in1=i32_ap)
        return v_op("scalar_tensor_tensor", out=out_ap, in0=e_ap, scalar=0.5,
                    in1=i32_ap, op0=ALU.is_ge, op1=ALU.add)

    def round_full(y_ap, i32_ap, e_ap, out_ap, in_tok):
        p.wait("gpsimd", in_tok)
        hop = dma("gpsimd", i32_ap, y_ap)
        p.wait("vector", in_tok)
        p.wait("vector", hop)
        v_op("tensor_sub", out=e_ap, in0=y_ap, in1=i32_ap)
        v_op("scalar_tensor_tensor", out=y_ap, in0=e_ap, scalar=0.5,
             in1=i32_ap, op0=ALU.is_ge, op1=ALU.add)
        v_op("tensor_single_scalar", out=e_ap, in_=e_ap, scalar=-0.5, op=ALU.is_le)
        return v_op("tensor_sub", out=out_ap, in0=y_ap, in1=e_ap)

    with contextlib.ExitStack() as S0:
        m_row = sb("m_row", [1, SEQ], F32, S0)
        r_row = sb("r_row", [1, SEQ], F32, S0)
        mr_row = sb("mr_row", [1, SEQ], F32, S0)
        xf = [sb(f"xf{i}", [128, 4096], F32, S0) for i in range(2)]
        xb = [sb(f"xb{i}", [128, 4096], BF16, S0) for i in range(2)]
        xq = [sb(f"xq{i}", [128, 4096], BF16, S0) for i in range(2)]
        xn32b = [sb(f"xn32b{i}", [128, 4096], F32, S0) for i in range(2)]
        xnbb = [sb(f"xnbb{i}", [128, 4096], BF16, S0) for i in range(2)]
        rbc = sb("rbc", [128, 512], F32, S0)
        mrbc = sb("mrbc", [128, 512], F32, S0)
        tmp32 = sb("tmp32", [128, 512], F32, S0)
        strow = sb("strow", [1, 512], F32, S0)
        strow2 = sb("strow2", [1, 512], F32, S0)
        ps_s = psum("ps_s", [128, 512], S0)
        ps_q = psum("ps_q", [128, 512], S0)

        with p.phase("P0_ln"):
            dma("sync", e1_sb[:, :], e1tab)
            dma("sync", ident_sb[:, :], ident)
            dma("sync", fb_sb[:, :], fbtab)
            t_bq = dma("sync", bqs_sb[:, :], bq_rs)
            dma("sync", bk_sb[:, :], bk_rs)
            dma("sync", bo_sb[:, :], bo_rs)
            dma("sync", bv_rsb[:, :], bv_row)
            t_rz = dma("sync", gsc[0:1, 2:3], rez)
            t_o1 = v_op("memset", ap=ones_col_bf[:, :], constant=1.0)
            t_o2 = v_op("memset", ap=ones_row_f[:, :], constant=1.0)
            v_op("memset", ap=runmax[:, :], constant=-1e30)
            v_op("memset", ap=hmax[:, :], constant=-1e30)
            p.wait("vector", t_bq)
            v_op("tensor_scalar_mul", out=bqs_sb[:, :], in0=bqs_sb[:, :], scalar1=0.125)

            iter_done = {}
            for tc in range(8):
                i = tc % 2
                tsl = slice(tc * 512, (tc + 1) * 512)
                war = iter_done.get(tc - 2, [])
                for w in war:
                    p.wait("sync", w)
                    p.wait("gpsimd", w)
                if tc >= 2:
                    stok = ("qA" if (tc - 2) % 2 == 0 else "qA2", 1)
                    p.wait("sync", stok)
                    p.wait("gpsimd", stok)
                ld, lb = [], []
                for dc in range(8):
                    dsl = slice(dc * 512, (dc + 1) * 512)
                    ld.append(dma("sync", xf[i][:, dsl], xT[dc * 128:(dc + 1) * 128, tsl]))
                    lb.append(dma("gpsimd", xb[i][:, dsl], xT[dc * 128:(dc + 1) * 128, tsl]))
                p.wait("scalar", ld[7])
                for w in war:
                    p.wait("scalar", w)
                t_sq = a_op("activation", out=xq[i][:, :], in_=xf[i][:, :],
                            func=AF.Square, bias=0.0, scale=1.0)
                sq = [t_sq] * 8
                p.wait("tensor", t_o1)
                if tc >= 1:
                    p.wait("tensor", iter_done[tc - 1][0])
                for dc in range(8):
                    p.wait("tensor", lb[dc])
                    mmop(ps_s[0:1, :], ones_col_bf[:, :], xb[i][:, dc * 512:(dc + 1) * 512],
                         dc == 0, dc == 7)
                t_stat = None
                for dc in range(8):
                    p.wait("tensor", sq[dc])
                    t_stat = mmop(ps_q[0:1, :], ones_col_bf[:, :], xq[i][:, dc * 512:(dc + 1) * 512],
                                  dc == 0, dc == 7, inc=(dc == 7))
                p.wait("vector", t_stat)
                v_op("tensor_scalar_mul", out=m_row[0:1, tsl], in0=ps_s[0:1, :], scalar1=1.0 / 1024.0)
                v_op("tensor_scalar_mul", out=strow[0:1, :], in0=ps_q[0:1, :], scalar1=1.0 / 1024.0)
                v_op("tensor_mul", out=strow2[0:1, :], in0=m_row[0:1, tsl], in1=m_row[0:1, tsl])
                v_op("tensor_sub", out=strow[0:1, :], in0=strow[0:1, :], in1=strow2[0:1, :])
                t_var = v_op("tensor_scalar_add", out=strow[0:1, :], in0=strow[0:1, :], scalar1=EPS_LN)
                p.wait("scalar", t_var)
                t_sd = a_op("activation", out=strow2[0:1, :], in_=strow[0:1, :],
                            func=AF.Sqrt, bias=0.0, scale=1.0)
                p.wait("vector", t_sd)
                t_r = v_op("reciprocal", out=r_row[0:1, tsl], in_=strow2[0:1, :])
                t_mr = v_op("tensor_mul", out=mr_row[0:1, tsl], in0=m_row[0:1, tsl], in1=r_row[0:1, tsl])
                p.wait("tensor", t_o2)
                p.wait("tensor", t_r)
                p.wait("tensor", t_mr)
                t_bm1 = mmop(ps_b1[:, :], ones_row_f[0:1, :], r_row[0:1, tsl], True, True, inc=True)
                t_bm2 = mmop(ps_b2[:, :], ones_row_f[0:1, :], mr_row[0:1, tsl], True, True, inc=True)
                p.wait("vector", t_bm1)
                t_rbc = v_op("tensor_copy", out=rbc[:, :], in_=ps_b1[:, :])
                p.wait("vector", t_bm2)
                t_mrbc = v_op("tensor_copy", out=mrbc[:, :], in_=ps_b2[:, :])
                dn = [t_mrbc, t_rbc, t_stat]
                for dc in range(8):
                    dsl = slice(dc * 512, (dc + 1) * 512)
                    v_op("tensor_mul", out=tmp32[:, :], in0=xf[i][:, dsl], in1=rbc[:, :])
                    t_xn = v_op("tensor_sub", out=xn32b[i][:, dsl], in0=tmp32[:, :], in1=mrbc[:, :])
                    v_op("reduce_max", out=rmx[:, :], in_=xn32b[i][:, dsl], axis=AX.X)
                    t_rmu = v_op("tensor_max", out=runmax[:, :], in0=runmax[:, :], in1=rmx[:, :])
                    p.wait("scalar", t_xn)
                    t_cp = a_op("copy", out=xnbb[i][:, dsl], in_=xn32b[i][:, dsl])
                    p.wait("scalar", t_cp)
                    ssem = "qA" if tc % 2 == 0 else "qA2"
                    dma("scalar", xnbf_d[dc * 128:(dc + 1) * 128, tsl], xnbb[i][:, dsl], sem=ssem)
                    dn.extend([t_rmu, t_cp])
                    if tc < 4:
                        dma("scalar", xn32_d[dc * 128:(dc + 1) * 128, tsl], xn32b[i][:, dsl], sem=ssem)
                iter_done[tc] = dn
        with p.phase("P0c_rows"):
            dma("sync", r_d, r_row[0:1, :])
            dma("sync", mr_d, mr_row[0:1, :])
        tok_xnbf = p.now("qA")
        tok_xnbf2 = p.now("qA2")

        with p.phase("P1_cc1"):
            p.wait("sync", p.last_v)
            t_mx = dma("sync", mx1_d, runmax[:, :])
            p.wait("gpsimd", t_mx)
            tok_c = p.inc("sC")
            p.ops["gpsimd"].append(lambda E, t=tok_c: E.collective_compute(
                "AllReduce", ALU.max, replica_groups=[list(range(NCORES))],
                ins=[mx1_d.rearrange("p o -> o p").opt()],
                outs=[cc1_out.rearrange("p o -> o p").opt()]).then_inc(p.handle(t[0]), 1))
            p.wait("sync", tok_c)
            t_g = dma("sync", grow[0:1, :], cc1_out.rearrange("p o -> o p"))
            p.wait("vector", t_g)
            v_op("reduce_max", out=gsc[0:1, 0:1], in_=grow[0:1, :], axis=AX.X)
            t_s1 = v_op("tensor_scalar", out=gsc[0:1, 0:1], in0=gsc[0:1, 0:1], scalar1=EPS_Q,
                        scalar2=1.0 / 255.0, op0=ALU.max, op1=ALU.mult)
            p.wait("tensor", t_s1)
            t_b = mmop(ps_b1[:, 0:1], ones_row_f[0:1, :], gsc[0:1, 0:1], True, True, inc=True)
            p.wait("vector", t_b)
            v_op("tensor_copy", out=scl[:, 1:2], in_=ps_b1[:, 0:1])
            v_op("reciprocal", out=scl[:, 0:1], in_=scl[:, 1:2])
            p.wait("tensor", t_rz)
            p.wait("tensor", p.last_v)
            t_b2 = mmop(ps_b2[:, 0:1], ones_row_f[0:1, :], gsc[0:1, 2:3], True, True, inc=True)
            p.wait("vector", t_b2)
            v_op("tensor_copy", out=scl[:, 2:3], in_=ps_b2[:, 0:1])

    S23 = contextlib.ExitStack()
    qw1T = sb("qw1T", [128, 8 * FF], BF16, S23)
    with contextlib.ExitStack() as S2:
        rr1 = sb("rr1", [1, FF], F32, S2)
        SWT = contextlib.ExitStack()
        wt = [sb(f"wt{i}", [128, 4096], F32, SWT, side="right") for i in range(2)]
        rbc1 = [sb(f"rbc1_{i}", [128, 1024], F32, S2) for i in range(4)]
        sm32 = sb("sm32", [128, 128], F32, S2)
        sm32b = sb("sm32b", [128, 128], F32, S2)
        sm32i = sb("sm32i", [128, 32], I32, S2)

        with p.phase("P2_wprep"):
            p.drain(tok_xnbf)
            p.drain(tok_xnbf2)
            w1r = w1.rearrange("(g a p) d -> g p a d", p=128, a=4)
            red1 = [None] * 8
            dma("sync", wt[0][:, :], w1r[0])
            dma("scalar", wt[1][:, :], w1r[1])
            for gld in range(8):
                i = gld % 2
                p.wait("vector", (("qS", p.sem["qS"][1]) if i == 0 else ("qA", p.sem["qA"][1])))
                for a in range(4):
                    red1[gld] = v_op("reduce_max", out=sw1c[:, gld * 4 + a: gld * 4 + a + 1],
                                     in_=wt[i][:, a * 1024:(a + 1) * 1024],
                                     axis=AX.X, apply_absolute_value=True)
                if gld + 2 < 8:
                    q = "sync" if i == 0 else "scalar"
                    p.wait(q, red1[gld])
                    dma(q, wt[i][:, :], w1r[gld + 2])
            v_op("tensor_scalar", out=sw1c[:, :], in0=sw1c[:, :], scalar1=EPS_Q,
                 scalar2=1.0 / 127.0, op0=ALU.max, op1=ALU.mult)
            t_rw1 = v_op("reciprocal", out=rw1c[:, :], in_=sw1c[:, :])
            p.wait("sync", p.last_v)
            p.wait("scalar", p.last_v)
            SWT.close()
            w4 = [sb(f"w4{i}", [128, 1024], F32, S2) for i in range(2)]
            ybuf = [sb(f"ybuf{i}", [128, 1024], F32, S2) for i in range(3)]
            ebuf = [sb(f"ebuf{i}", [128, 1024], F32, S2) for i in range(3)]
            ibuf = [sb(f"ibuf{i}", [128, 1024], I32, S2) for i in range(3)]
            def pe_transpose(out_ps, in_ap):
                tok = p.inc("sP")
                p.ops["tensor"].append(
                    lambda E, o=out_ps, i=in_ap, t=tok: E.transpose(
                        out=o, in_=i, identity=ident_sb[:, :]).then_inc(p.handle(t[0]), 1))
                return tok
            p.wait("tensor", t_rw1)
            t_t1 = pe_transpose(ps_b1[0:32, 0:128], rw1c[:, :])
            p.wait("vector", t_t1)
            t_e1t = v_op("tensor_copy", out=sm32[0:32, 0:128], in_=ps_b1[0:32, 0:128])
            p.wait("gpsimd", t_e1t)
            t_d1 = dma("gpsimd", rsw1_d, sm32[0:32, 0:128])
            p.wait("gpsimd", t_d1)
            t_r1 = dma("gpsimd", rr1[0:1, :], rsw1_d.rearrange("t p -> (t p)")[None, :])
            p.wait("tensor", t_r1)
            for hl in range(4):
                for j in range(2):
                    p.wait("tensor", p.last_v)
                    t_b = mmop(ps_b1[:, :], ones_row_f[0:1, :],
                               rr1[0:1, hl * 1024 + j * 512: hl * 1024 + (j + 1) * 512],
                               True, True, inc=True)
                    p.wait("vector", t_b)
                    v_op("tensor_copy", out=rbc1[hl][:, j * 512:(j + 1) * 512], in_=ps_b1[:, :])
            w1h = w1T.rearrange("(g p) (hl f) -> g hl p f", p=128, hl=4)
            nW = 32
            t_l = [None] * nW; t_mu = [None] * nW; t_hp = [None] * nW
            muls_done = {}
            def wq_load(c):
                q = "sync" if c % 2 == 0 else "scalar"
                if c >= 2:
                    p.wait(q, t_mu[c - 2])
                t_l[c] = dma(q, w4[c % 2][:, :], w1h[c // 4, c % 4])
            def wq_mul(c):
                i = c % 3
                p.wait("vector", t_l[c])
                if muls_done.get(i) is not None:
                    p.wait("vector", muls_done[i])
                t_mu[c] = v_op("tensor_mul", out=ybuf[i][:, :], in0=w4[c % 2][:, :],
                               in1=rbc1[c % 4][:, :])
                p.wait("gpsimd", t_mu[c])
                t_hp[c] = dma("gpsimd", ibuf[i][:, :], ybuf[i][:, :],
                              sem=("qG" if c % 2 == 0 else "qG2"))
                muls_done[i] = t_hp[c]
            def wq_fin(c):
                i = c % 3
                p.wait("vector", t_hp[c])
                v_op("tensor_sub", out=ebuf[i][:, :], in0=ybuf[i][:, :], in1=ibuf[i][:, :])
                v_op("scalar_tensor_tensor", out=ybuf[i][:, :], in0=ebuf[i][:, :], scalar=0.5,
                     in1=ibuf[i][:, :], op0=ALU.is_ge, op1=ALU.add)
                v_op("tensor_single_scalar", out=ebuf[i][:, :], in_=ebuf[i][:, :],
                     scalar=-0.5, op=ALU.is_le)
                dc, hl = c // 4, c % 4
                v_op("tensor_sub",
                     out=qw1T[:, dc * FF + hl * 1024: dc * FF + (hl + 1) * 1024],
                     in0=ybuf[i][:, :], in1=ebuf[i][:, :])
            wq_load(0); wq_mul(0); wq_load(1); wq_mul(1); wq_load(2)
            for c in range(nW):
                wq_fin(c)
                if c + 2 < nW:
                    wq_mul(c + 2)
                if c + 3 < nW:
                    wq_load(c + 3)
            t_b1 = dma("sync", sm32[:, 0:32], b1_rs)
            p.wait("vector", t_b1)
            v_op("tensor_mul", out=sm32b[:, 0:32], in0=sm32[:, 0:32], in1=rw1c[:, :])
            t_y0 = v_op("tensor_scalar_mul", out=sm32b[:, 0:32], in0=sm32b[:, 0:32], scalar1=scl[:, 0:1])
            round_full(sm32b[:, 0:32], sm32i[:, :], sm32[:, 0:32], b1i_sb[:, :], t_y0)
            v_op("tensor_scalar_mul", out=sc1_sb[:, :], in0=sw1c[:, :], scalar1=scl[:, 1:2])

    with contextlib.ExitStack() as S3:
        q1 = sb("q1", [128, 8 * TOK], BF16, S3)
        xq32 = [sb(f"xq32{i}", [128, 512], F32, S3) for i in range(2)]
        w4 = [sb(f"w4b{i}", [128, 1024], F32, S3) for i in range(2)]
        ybuf = [sb(f"ybuf_b{i}", [128, 1024], F32, S3) for i in range(3)]
        ebuf = [sb(f"ebuf_b{i}", [128, 1024], F32, S3) for i in range(3)]
        ibuf = [sb(f"ibuf_b{i}", [128, 1024], I32, S3) for i in range(3)]
        ps_h = [psum(f"ps_h{i}", [128, 512], S3) for i in range(6)]
        SWT3 = contextlib.ExitStack()
        wtb = [sb(f"wtb{i}", [128, 2048], F32, SWT3, side="right") for i in range(2)]
        swtmp = sb("swtmp", [128, 2], F32, S3)

        with p.phase("P3_ffn1"):
            for s_ in ("qG", "qG2", "qG3", "qA", "qS"):
                p.drain(p.now(s_))
            w2r = w2.rearrange("(g p) (hl d) -> g hl p d", p=128, hl=2)
            tokp = None
            t_lw2 = [None] * 16
            t_lw2[0] = dma("sync", wtb[0][:, :], w2r[0, 0])
            t_lw2[1] = dma("scalar", wtb[1][:, :], w2r[0, 1])
            for j in range(16):
                i, hl = j // 2, j % 2
                q = "sync" if j % 2 == 0 else "scalar"
                p.wait("vector", t_lw2[j])
                v_op("reduce_max", out=swtmp[:, hl:hl + 1], in_=wtb[j % 2][:, :],
                     axis=AX.X, apply_absolute_value=True)
                if hl == 1:
                    v_op("tensor_max", out=sw2c[:, i:i + 1], in0=swtmp[:, 0:1], in1=swtmp[:, 1:2])
                tokp = p.last_v
                if j + 2 < 16:
                    p.wait(q, tokp)
                    t_lw2[j + 2] = dma(q, wtb[j % 2][:, :], w2r[(j + 2) // 2, (j + 2) % 2])
            v_op("tensor_scalar", out=sw2c[:, :], in0=sw2c[:, :], scalar1=EPS_Q,
                 scalar2=1.0 / 127.0, op0=ALU.max, op1=ALU.mult)
            t_rw2 = v_op("reciprocal", out=rw2c[:, :], in_=sw2c[:, :])
            p.wait("sync", t_rw2)
            p.wait("scalar", t_rw2)
            SWT3.close()
            ht = [sb(f"ht{i}", [128, 512], F32, S3) for i in range(6)]
            rmxall = sb("rmxall", [128, 32], F32, S3)
            rr2 = sb("rr2", [1, D], F32, S3)
            rbc2 = sb("rbc2", [128, D], F32, S3)
            qc2 = [sb("qc2_0", [128, D], BF16, S3)] * 2
            def pe_transpose3(out_ps, in_ap):
                tok = p.inc("sP")
                p.ops["tensor"].append(
                    lambda E, o=out_ps, i=in_ap, t=tok: E.transpose(
                        out=o, in_=i, identity=ident_sb[:, :]).then_inc(p.handle(t[0]), 1))
                return tok
            p.wait("tensor", t_rw2)
            t_t2 = pe_transpose3(ps_b2[0:8, 0:128], rw2c[:, :])
            p.wait("vector", t_t2)
            t_e2t = v_op("tensor_copy", out=ht[0][0:8, 0:128], in_=ps_b2[0:8, 0:128])
            p.wait("gpsimd", t_e2t)
            t_d2 = dma("gpsimd", rsw2_d, ht[0][0:8, 0:128])
            p.wait("gpsimd", t_d2)
            t_r2 = dma("gpsimd", rr2[0:1, :], rsw2_d.rearrange("t p -> (t p)")[None, :])
            p.wait("tensor", t_r2)
            for j in range(2):
                p.wait("tensor", p.last_v)
                t_b = mmop(ps_b2[:, :], ones_row_f[0:1, :], rr2[0:1, j * 512:(j + 1) * 512],
                           True, True, inc=True)
                p.wait("vector", t_b)
                v_op("tensor_copy", out=rbc2[:, j * 512:(j + 1) * 512], in_=ps_b2[:, :])
            t_rbc2_done = p.last_v
            nq = 32
            t_yq = [None] * nq
            t_hopq = [None] * nq
            t_lq = [None] * nq
            def q1_load(j):
                tc, dc = j // 8, j % 8
                t_lq[j] = dma("sync", xq32[j % 2][:, :],
                              xn32_d[dc * 128:(dc + 1) * 128, tc * 512:(tc + 1) * 512])
            def q1_mul(j):
                i = j % 3
                p.wait("vector", t_lq[j])
                t_yq[j] = v_op("tensor_scalar", out=ybuf[i][:, 0:512], in0=xq32[j % 2][:, :],
                               scalar1=scl[:, 0:1], scalar2=0.0, op0=ALU.mult, op1=ALU.max)
                p.wait("gpsimd", t_yq[j])
                t_hopq[j] = dma("gpsimd", ibuf[i][:, 0:512], ybuf[i][:, 0:512],
                                sem=("qG" if j % 2 == 0 else "qG2"))
            def q1_fin(j):
                i = j % 3
                tc, dc = j // 8, j % 8
                p.wait("vector", t_hopq[j])
                v_op("tensor_sub", out=ebuf[i][:, 0:512], in0=ybuf[i][:, 0:512], in1=ibuf[i][:, 0:512])
                v_op("scalar_tensor_tensor",
                     out=q1[:, dc * TOK + tc * 512: dc * TOK + (tc + 1) * 512],
                     in0=ebuf[i][:, 0:512], scalar=0.5,
                     in1=ibuf[i][:, 0:512], op0=ALU.is_ge, op1=ALU.add)
            q1_load(0); q1_mul(0); q1_load(1); q1_mul(1)
            p.wait("sync", t_yq[0])
            q1_load(2)
            for j in range(nq):
                q1_fin(j)
                if j + 2 < nq:
                    q1_mul(j + 2)
                if j + 3 < nq:
                    p.wait("sync", t_yq[j + 1])
                    q1_load(j + 3)
            t_l2 = [None] * 32; t_mu2 = [None] * 32; t_hp2 = [None] * 32; t_st2 = [None] * 32
            muls2 = {}
            def w2_load(c):
                q = "sync" if c % 2 == 0 else "scalar"
                if c >= 2:
                    p.wait(q, t_mu2[c - 2])
                t_l2[c] = dma(q, w4[c % 2][:, :], w2T[c * 128:(c + 1) * 128, :])
            def w2_mul(c):
                i = c % 3
                p.wait("vector", t_l2[c])
                if muls2.get(i) is not None:
                    p.wait("vector", muls2[i])
                t_mu2[c] = v_op("tensor_mul", out=ybuf[i][:, :], in0=w4[c % 2][:, :],
                                in1=rbc2[:, :])
                p.wait("gpsimd", t_mu2[c])
                t_hp2[c] = dma("gpsimd", ibuf[i][:, :], ybuf[i][:, :],
                               sem=("qG" if c % 2 == 0 else "qG2"))
                muls2[i] = t_hp2[c]
            def w2_fin(c):
                i = c % 3
                p.wait("vector", t_hp2[c])
                v_op("tensor_sub", out=ebuf[i][:, :], in0=ybuf[i][:, :], in1=ibuf[i][:, :])
                v_op("scalar_tensor_tensor", out=ybuf[i][:, :], in0=ebuf[i][:, :], scalar=0.5,
                     in1=ibuf[i][:, :], op0=ALU.is_ge, op1=ALU.add)
                v_op("tensor_single_scalar", out=ebuf[i][:, :], in_=ebuf[i][:, :],
                     scalar=-0.5, op=ALU.is_le)
                if c >= 1 and t_st2[c - 1] is not None:
                    p.wait("vector", t_st2[c - 1])
                v_op("tensor_sub", out=qc2[c % 2][:, :], in0=ybuf[i][:, :], in1=ebuf[i][:, :])
                p.wait("gpsimd", p.last_v)
                t_st2[c] = dma("gpsimd", qw2_d[c * 128:(c + 1) * 128, :], qc2[c % 2][:, :],
                               sem="qG3")
            def w2_gen():
                w2_load(0); w2_mul(0); w2_load(1); w2_mul(1); w2_load(2)
                yield
                for c in range(32):
                    w2_fin(c)
                    if c + 2 < 32:
                        w2_mul(c + 2)
                    if c + 3 < 32:
                        w2_load(c + 3)
                    yield
            gen = w2_gen()
            p.wait("tensor", p.last_v)
            p.wait("vector", p.now("qG"))
            war = {}
            ev_tok = {}
            red_tok = {}
            for tc in range(4):
                for fc in range(32):
                    k = tc * 32 + fc
                    pb = k % 6
                    if k - 6 in war:
                        p.wait("tensor", war[k - 6])
                    t_mm = None
                    for dc in range(8):
                        t_mm = mmop(ps_h[pb][:, :],
                                    qw1T[:, dc * FF + fc * 128: dc * FF + (fc + 1) * 128],
                                    q1[:, dc * TOK + tc * 512: dc * TOK + (tc + 1) * 512],
                                    dc == 0, dc == 7, inc=(dc == 7))
                    p.wait("scalar", t_mm)
                    if k >= 6:
                        par6 = ((k - 6) // 4) % 2
                        p.wait("scalar", ("qA" if par6 == 0 else "qA2", 1))
                        p.wait("scalar", red_tok[k - 6])
                    t_e1 = a_op("activation", out=ht[k % 6][:, :], in_=ps_h[pb][:, :],
                                func=AF.Relu, bias=b1i_sb[:, fc:fc + 1], scale=1.0)
                    p.wait("vector", t_e1)
                    red_tok[k] = v_op("reduce_max", out=rmxall[:, fc:fc + 1], in_=ht[k % 6][:, :], axis=AX.X)
                    war[k] = t_e1
                    ev_tok[k] = t_e1
                    if k % 4 == 3:
                        bsem = "qA" if ((k // 4) % 2 == 0) else "qA2"
                        for kk in range(k - 3, k + 1):
                            p.wait("scalar", ev_tok[kk])
                            dma("scalar",
                                h_d[(kk % 32) * 128:(kk % 32 + 1) * 128,
                                    (kk // 32) * 512:(kk // 32 + 1) * 512],
                                ht[kk % 6][:, :], sem=bsem)
                        next(gen, None)
                p.wait("vector", p.last_v)
                v_op("tensor_mul", out=rmxall[:, :], in0=rmxall[:, :], in1=sc1_sb[:, :])
                v_op("reduce_max", out=rmx[:, :], in_=rmxall[:, :], axis=AX.X)
                v_op("tensor_max", out=hmax[:, :], in0=hmax[:, :], in1=rmx[:, :])
            for _ in range(40):
                next(gen, None)
    tok_qw2 = p.now("qG3")
    tok_h2 = p.now("qA2")
    tok_h = p.now("qA")

    with p.phase("P3b_cc2"):
        p.wait("sync", p.last_v)
        t_mx = dma("sync", mx2_d, hmax[:, :])
        p.wait("gpsimd", t_mx)
        tok_c = p.inc("sC")
        p.ops["gpsimd"].append(lambda E, t=tok_c: E.collective_compute(
            "AllReduce", ALU.max, replica_groups=[list(range(NCORES))],
            ins=[mx2_d.rearrange("p o -> o p").opt()],
            outs=[cc2_out.rearrange("p o -> o p").opt()]).then_inc(p.handle(t[0]), 1))
        p.wait("sync", tok_c)
        t_g = dma("sync", grow[0:1, :], cc2_out.rearrange("p o -> o p"))
        p.wait("vector", t_g)
        v_op("reduce_max", out=gsc[0:1, 4:5], in_=grow[0:1, :], axis=AX.X)
        t_s2 = v_op("tensor_scalar", out=gsc[0:1, 4:5], in0=gsc[0:1, 4:5], scalar1=EPS_Q,
                    scalar2=1.0 / 255.0, op0=ALU.max, op1=ALU.mult)
        p.wait("tensor", t_s2)
        t_b = mmop(ps_b1[:, 0:1], ones_row_f[0:1, :], gsc[0:1, 4:5], True, True, inc=True)
        p.wait("vector", t_b)
        v_op("tensor_copy", out=scl[:, 4:5], in_=ps_b1[:, 0:1])
        v_op("reciprocal", out=scl[:, 3:4], in_=scl[:, 4:5])
        t_b2l = dma("sync", t8a[:, :], b2_rs)
        p.wait("vector", t_b2l)
        v_op("tensor_mul", out=t8a[:, :], in0=t8a[:, :], in1=rw2c[:, :])
        t_y = v_op("tensor_scalar_mul", out=t8a[:, :], in0=t8a[:, :], scalar1=scl[:, 3:4])
        round_full(t8a[:, :], t8i[:, :], t8e[:, :], b2i_sb[:, :], t_y)
        v_op("tensor_scalar_mul", out=sc2_sb[:, :], in0=sw2c[:, :], scalar1=scl[:, 4:5])
        v_op("tensor_scalar_mul", out=sc1f_sb[:, :], in0=sc1_sb[:, :], scalar1=scl[:, 3:4])
    S23.close()
    PSB.close()

    SAtt = contextlib.ExitStack()
    OT = sb("OT", [128, 8 * TOK], BF16, SAtt)
    u_all = sb("u_all", [72, SEQ], BF16, SAtt)
    u_o = sb("u_o", [8, 512], BF16, SAtt)

    KEEPS_ALL = _keep_sets()
    with contextlib.ExitStack() as S4:
        xnt = [sb(f"xnt{i}", [128, 4096], BF16, S4) for i in range(2)]
        QT = sb("QT", [128, 4 * TOK], BF16, S4)
        KT = sb("KT", [128, 4 * SEQ], BF16, S4)
        vt32 = sb("vt32", [128, 512], F32, S4)
        bvbc = sb("bvbc", [128, 512], F32, S4)
        BqT_bf = sb("BqT_bf", [8, D], BF16, S4)
        BkT_bf = sb("BkT_bf", [40, D], BF16, S4)
        BvT_bf = sb("BvT_bf", [72, D], BF16, S4)
        AT_bf = sb("AT_bf", [128, 8 * 24], BF16, S4)

        t_Bv = None
        for g in range(2):
            gsl = slice(g * 512, (g + 1) * 512)
            SWg = contextlib.ExitStack()
            keeps_g = [KEEPS_ALL[g * 8 + h] for h in range(8)]
            posof = [{kt: i for i, kt in enumerate(kp)} for kp in keeps_g]
            Vp = [sb(f"Vp{g}_{h}", [128, len(keeps_g[h]) * 65 + 128], BF16, SWg) for h in range(8)]
            SWa = contextlib.ExitStack()
            wq_bf = sb(f"wq_bf{g}", [128, 8 * 512], BF16, SWa)
            wk_bf = sb(f"wk_bf{g}", [128, 8 * 512], BF16, SWa)
            wv_bf = sb(f"wv_bf{g}", [128, 8 * 512], BF16, SWa)
            ps_p = [psum(f"ps_p{g}{i}", [128, 512], SWa) for i in range(4)]
            ps_u4 = psum(f"ps_u4{g}", [24, 512], SWa)
            ps_bv = psum(f"ps_bv{g}", [128, 512], SWa)
            with p.phase(f"P4a_proj{g}"):
                p.drain(tok_h)
                p.drain(tok_h2)
                p.drain(tok_xnbf2)
                t_wv = None
                for dc in range(8):
                    rows = slice(dc * 128, (dc + 1) * 128)
                    dma("gpsimd", wq_bf[:, dc * 512:(dc + 1) * 512], wqT[rows, gsl])
                    dma("gpsimd", wk_bf[:, dc * 512:(dc + 1) * 512], wkT[rows, gsl])
                    t_wv = dma("gpsimd", wv_bf[:, dc * 512:(dc + 1) * 512], wvT[rows, gsl])
                    if g == 0:
                        dma("gpsimd", AT_bf[:, dc * 24:(dc + 1) * 24], AT[rows, :])
                if g == 0:
                    dma("gpsimd", BqT_bf[0:8, :], BqT)
                    dma("gpsimd", BkT_bf[32:40, :], BkT)
                    t_Bv = dma("gpsimd", BvT_bf[64:72, :], BvT)
                t_wload = p.now("qG")
                p.wait("tensor", p.last_v)
                t_b = mmop(ps_bv[:, :], ones_row_f[0:1, :], bv_rsb[0:1, gsl], True, True, inc=True)
                p.wait("vector", t_b)
                t_bvbc = v_op("tensor_copy", out=bvbc[:, :], in_=ps_bv[:, :])
                p.wait("vector", t_wload)
                for h in range(8):
                    nk_h = len(keeps_g[h])
                    v_op("memset", ap=Vp[h].ap()[:, nk_h * 65: nk_h * 65 + 128], constant=0.0)
                for h in range(8):
                    hh = g * 8 + h
                    nk_h = len(keeps_g[h])
                    fcol = Vp[h].ap()[:, 0:nk_h * 65].rearrange("p (k c) -> p k c", c=65)[:, :, 64]
                    for pos, kt in enumerate(keeps_g[h]):
                        v_op("tensor_copy", out=fcol[:, pos:pos + 1],
                             in_=fb_sb[:, hh * NKT + kt: hh * NKT + kt + 1])
                ev_war = [None, None, None, None]
                rotc = [0]
                uo_war = None
                prev_iter = {}
                for tc in range(8):
                    i4 = tc % 2
                    tsl = slice(tc * 512, (tc + 1) * 512)
                    if tc - 2 in prev_iter:
                        p.wait("sync", prev_iter[tc - 2])
                    xl = []
                    for dc in range(8):
                        xl.append(dma("sync", xnt[i4][:, dc * 512:(dc + 1) * 512],
                                      xnbf_d[dc * 128:(dc + 1) * 128, tsl]))
                    p.wait("tensor", t_wload)
                    g_dma1 = g_dma2 = None
                    if g == 0:
                        if uo_war is not None:
                            p.wait("tensor", uo_war)
                        t_u = None
                        for dc in range(8):
                            p.wait("tensor", xl[dc])
                            t_u = mmop(ps_u4[:, :], AT_bf[:, dc * 24:(dc + 1) * 24],
                                       xnt[i4][:, dc * 512:(dc + 1) * 512],
                                       dc == 0, dc == 7, inc=(dc == 7))
                        p.wait("scalar", t_u)
                        t_uc = a_op("activation", out=u_all[0:24, tsl], in_=ps_u4[:, :],
                                    func=AF.Copy, bias=0.0, scale=0.125)
                        uo_war = t_uc
                        p.wait("gpsimd", t_uc)
                        g_dma1 = dma("gpsimd", u_all[32:40, tsl], u_all[8:16, tsl])
                        g_dma2 = dma("gpsimd", u_all[64:72, tsl], u_all[16:24, tsl])
                    for mc in range(4):
                        pb = rotc[0] % 4; rotc[0] += 1
                        if ev_war[pb] is not None:
                            p.wait("tensor", ev_war[pb])
                        for dc in range(8):
                            p.wait("tensor", xl[dc])
                            mmop(ps_p[pb][:, :],
                                 wk_bf[:, dc * 512 + mc * 128: dc * 512 + (mc + 1) * 128],
                                 xnt[i4][:, dc * 512:(dc + 1) * 512], dc == 0, False)
                        if g == 0:
                            p.wait("tensor", g_dma1)
                        t_k = mmop(ps_p[pb][:, :],
                                   BkT_bf[32:40, g * 512 + mc * 128: g * 512 + (mc + 1) * 128],
                                   u_all[32:40, tsl], False, True, inc=True)
                        p.wait("scalar", t_k)
                        ev_war[pb] = a_op("activation",
                                          out=KT[:, mc * SEQ + tc * 512: mc * SEQ + (tc + 1) * 512],
                                          in_=ps_p[pb][:, :], func=AF.Identity,
                                          bias=bk_sb[:, g * 4 + mc: g * 4 + mc + 1], scale=1.0)
                    if tc < 4:
                        for mc in range(4):
                            pb = rotc[0] % 4; rotc[0] += 1
                            if ev_war[pb] is not None:
                                p.wait("tensor", ev_war[pb])
                            for dc in range(8):
                                mmop(ps_p[pb][:, :],
                                     wq_bf[:, dc * 512 + mc * 128: dc * 512 + (mc + 1) * 128],
                                     xnt[i4][:, dc * 512:(dc + 1) * 512], dc == 0, False)
                            t_q = mmop(ps_p[pb][:, :],
                                       BqT_bf[0:8, g * 512 + mc * 128: g * 512 + (mc + 1) * 128],
                                       u_all[0:8, tsl], False, True, inc=True)
                            p.wait("scalar", t_q)
                            ev_war[pb] = a_op("activation",
                                              out=QT[:, mc * TOK + tc * 512: mc * TOK + (tc + 1) * 512],
                                              in_=ps_p[pb][:, :], func=AF.Identity,
                                              bias=bqs_sb[:, g * 4 + mc: g * 4 + mc + 1], scale=0.125)
                    for j in range(4):
                        tt = tc * 4 + j
                        pb = rotc[0] % 4; rotc[0] += 1
                        if ev_war[pb] is not None:
                            p.wait("tensor", ev_war[pb])
                        for dc in range(8):
                            mmop(ps_p[pb][:, :],
                                 xnt[i4][:, dc * 512 + j * 128: dc * 512 + (j + 1) * 128],
                                 wv_bf[:, dc * 512:(dc + 1) * 512], dc == 0, False)
                        if g == 0:
                            p.wait("tensor", g_dma2)
                            p.wait("tensor", t_Bv)
                        t_v = mmop(ps_p[pb][:, :], u_all[64:72, tt * 128:(tt + 1) * 128],
                                   BvT_bf[64:72, gsl], False, True, inc=True)
                        p.wait("vector", t_v)
                        p.wait("vector", t_bvbc)
                        t_vt = v_op("tensor_add", out=vt32[:, :], in0=ps_p[pb][:, :], in1=bvbc[:, :])
                        ev_war[pb] = t_vt
                        for h in range(8):
                            if tt not in posof[h]:
                                continue
                            pos = posof[h][tt]
                            v_op("tensor_scalar_mul",
                                 out=Vp[h].ap()[:, pos * 65: pos * 65 + 64],
                                 in0=vt32[:, h * 64:(h + 1) * 64],
                                 scalar1=fb_sb[:, (g * 8 + h) * NKT + tt:(g * 8 + h) * NKT + tt + 1])
                    prev_iter[tc] = t_v
            SWa.close()

            SWb = contextlib.ExitStack()
            Pt = [sb(f"Pt{g}{i}", [128, 1536], BF16, SWb) for i in range(2)]
            lrow = [sb(f"lrow{g}{i}", [128, 512], F32, SWb) for i in range(2)]
            rbx = [sb(f"rbx{g}{i}", [64, 512], F32, SWb) for i in range(2)]
            otmp = [sb(f"otmp{g}{i}", [64, 512], BF16, SWb) for i in range(2)]
            ps_S = [psum(f"ps_S{g}{i}", [128, 1536], SWb) for i in range(2)]
            ps_PV = [psum(f"ps_PV{g}{i}", [128, 512], SWb) for i in range(2)]
            with p.phase(f"P4b_attn{g}"):
                pv_ev = [None, None]
                s_free = [None, None]
                pt_free = [None, None]
                pend = None

                def emit_epi(hq, qq, pvbq, t_pvq):
                    pjq, subq = hq // 2, hq % 2
                    ot_base = (g * 4 + pjq) * TOK + qq * 512
                    p.wait("vector", t_pvq)
                    v_op("tensor_copy", out=lrow[pvbq][64:65, :], in_=ps_PV[pvbq][64:65, :])
                    t_rc = v_op("reciprocal", out=lrow[pvbq][64:65, :], in_=lrow[pvbq][64:65, :])
                    p.wait("gpsimd", t_rc)
                    t_b1 = dma("gpsimd", lr_d[pvbq], lrow[pvbq][64:65, :], sem="qG3")
                    p.wait("gpsimd", t_b1)
                    t_bc = dma("gpsimd", rbx[pvbq][:, :], lr_d[pvbq].broadcast_to((64, 512)), sem="qG3")
                    p.wait("vector", t_bc)
                    if subq == 0:
                        t_o = v_op("tensor_mul",
                                   out=OT.ap()[0:64, ot_base:ot_base + 512],
                                   in0=ps_PV[pvbq][0:64, :], in1=rbx[pvbq][:, :])
                    else:
                        t_o = v_op("tensor_mul", out=otmp[pvbq][:, :],
                                   in0=ps_PV[pvbq][0:64, :], in1=rbx[pvbq][:, :])
                        p.wait("gpsimd", t_o)
                        t_mv2 = dma("gpsimd", OT.ap()[64:128, ot_base:ot_base + 512],
                                    otmp[pvbq][:, :])
                        p.wait("vector", t_mv2)
                    pv_ev[pvbq] = t_o

                for h in range(8):
                    hh = g * 8 + h
                    pj, sub = h // 2, h % 2
                    rows = slice(64 * sub, 64 * sub + 64)
                    tp = (64 * sub, 0)
                    kept = keeps_g[h]
                    grps = [kept[i:i + 3] for i in range(0, len(kept), 3)]
                    ng = len(grps)
                    vH = Vp[h]
                    for qc in range(4):
                        qsl = slice(pj * TOK + qc * 512, pj * TOK + (qc + 1) * 512)
                        pvb = qc % 2
                        t_qk = [None] * ng

                        def emit_qk(gi, first):
                            sbk = gi % 2
                            p.wait("tensor", s_free[sbk])
                            if first:
                                p.wait("tensor", pv_ev[pvb])
                            tq = None
                            for idx, kt in enumerate(grps[gi]):
                                ksl = slice(pj * SEQ + kt * 128, pj * SEQ + (kt + 1) * 128)
                                tq = mmop(ps_S[sbk][:, idx * 512:(idx + 1) * 512],
                                          KT.ap()[rows, ksl], QT.ap()[rows, qsl],
                                          True, True, tp=tp, inc=(idx == len(grps[gi]) - 1))
                            t_qk[gi] = tq

                        emit_qk(0, True)
                        if ng > 1:
                            emit_qk(1, False)
                        t_pv = None
                        for gi, grp in enumerate(grps):
                            sbk = gi % 2
                            p.wait("scalar", t_qk[gi])
                            p.wait("scalar", pt_free[sbk])
                            n = len(grp) * 512
                            t_act = a_op_nochain("activation", out=Pt[sbk][:, 0:n], in_=ps_S[sbk][:, 0:n],
                                                 func=AF.Exp, bias=e1_sb[:, hh:hh + 1], scale=1.0)
                            s_free[sbk] = t_act
                            p.wait("tensor", t_act)
                            if gi + 2 < ng:
                                emit_qk(gi + 2, False)
                            for idx, kt in enumerate(grp):
                                pos = posof[h][kt]
                                t_pv = mmop(ps_PV[pvb][:, :], vH.ap()[:, pos * 65: pos * 65 + 128],
                                            Pt[sbk][:, idx * 512:(idx + 1) * 512],
                                            (gi == 0 and idx == 0),
                                            (gi == ng - 1 and idx == len(grp) - 1),
                                            inc=(idx == len(grp) - 1))
                            pt_free[sbk] = t_pv
                            if gi == 1 and pend is not None:
                                emit_epi(*pend)
                                pend = None
                        if pend is not None:
                            emit_epi(*pend)
                            pend = None
                        pend = (h, qc, pvb, t_pv)
                if pend is not None:
                    emit_epi(*pend)
                    pend = None
            SWb.close()
            SWg.close()
    tok_ot = p.now("qG")

    s_all = sb("s_all", [128, 8 * TOK], F32, side="right")
    with contextlib.ExitStack() as S5:
        wo_bf = sb("wo_bf", [128, 8 * 1024], BF16, S5)
        AoT_bf = sb("AoT_bf", [128, 8 * 8], BF16, S5)
        BoT_bf = sb("BoT_bf", [8, D], BF16, S5)
        x5 = [sb(f"x5{i}", [128, 512], F32, S5) for i in range(2)]
        r5 = sb("r5", [1, SEQ], F32, S5)
        mr5 = sb("mr5", [1, SEQ], F32, S5)
        rbc5 = sb("rbc5", [128, 512], F32, S5)
        mrbc5 = sb("mrbc5", [128, 512], F32, S5)
        stmp = sb("stmp", [128, 512], F32, S5)
        stmp2 = sb("stmp2", [128, 512], F32, S5)
        ps_o = [psum(f"ps_o{i}", [128, 512], S5) for i in range(2)]
        ps_uo = psum("ps_uo", [24, 512], S5)
        ps_b51 = psum("ps_b51", [128, 512], S5)
        ps_b52 = psum("ps_b52", [128, 512], S5)

        with p.phase("P5_oproj"):
            p.drain(tok_ot)
            for dc in range(8):
                rows = slice(dc * 128, (dc + 1) * 128)
                dma("gpsimd", wo_bf[:, dc * 1024:(dc + 1) * 1024], woT[rows, :])
                dma("gpsimd", AoT_bf[:, dc * 8:(dc + 1) * 8], AoT[rows, :])
            t_Bo = dma("gpsimd", BoT_bf[:, :], BoT)
            t_rr = dma("sync", r5[0:1, :], r_d)
            dma("sync", mr5[0:1, :], mr_d)
            t_wo = p.now("qG")
            p.wait("tensor", tok_ot)
            ev_war = [None, None]
            uo_war = None
            for tc in range(4):
                tsl = slice(tc * 512, (tc + 1) * 512)
                p.wait("tensor", p.last_v)
                p.wait("tensor", t_rr)
                t_b1m = mmop(ps_b51[:, :], ones_row_f[0:1, :], r5[0:1, tsl], True, True, inc=True)
                t_b2m = mmop(ps_b52[:, :], ones_row_f[0:1, :], mr5[0:1, tsl], True, True, inc=True)
                p.wait("vector", t_b1m)
                v_op("tensor_copy", out=rbc5[:, :], in_=ps_b51[:, :])
                p.wait("vector", t_b2m)
                v_op("tensor_copy", out=mrbc5[:, :], in_=ps_b52[:, :])
                p.wait("tensor", t_wo)
                if uo_war is not None:
                    p.wait("tensor", uo_war)
                t_u = None
                for dc in range(8):
                    t_u = mmop(ps_uo[0:8, :], AoT_bf[:, dc * 8:(dc + 1) * 8],
                               OT.ap()[:, dc * TOK + tc * 512: dc * TOK + (tc + 1) * 512],
                               dc == 0, dc == 7, inc=(dc == 7))
                p.wait("scalar", t_u)
                t_uo = a_op("activation", out=u_o[0:8, :], in_=ps_uo[0:8, :],
                            func=AF.Copy, bias=0.0, scale=0.125)
                uo_war = t_uo
                for mc in range(8):
                    pb = mc % 2
                    t_lx = dma("sync", x5[pb][:, :], xT[mc * 128:(mc + 1) * 128, tsl])
                    if ev_war[pb] is not None:
                        p.wait("tensor", ev_war[pb])
                    for dc in range(8):
                        mmop(ps_o[pb][:, :],
                             wo_bf[:, dc * 1024 + mc * 128: dc * 1024 + (mc + 1) * 128],
                             OT.ap()[:, dc * TOK + tc * 512: dc * TOK + (tc + 1) * 512],
                             dc == 0, False)
                    p.wait("tensor", t_uo)
                    p.wait("tensor", t_Bo)
                    t_om = mmop(ps_o[pb][:, :], BoT_bf[0:8, mc * 128:(mc + 1) * 128],
                                u_o[0:8, :], False, True, inc=True)
                    p.wait("vector", t_om)
                    v_op("tensor_scalar", out=stmp[:, :], in0=ps_o[pb][:, :],
                         scalar1=bo_sb[:, mc:mc + 1], scalar2=scl[:, 2:3],
                         op0=ALU.add, op1=ALU.mult)
                    ev_war[pb] = p.last_v
                    p.wait("vector", t_lx)
                    v_op("tensor_mul", out=stmp2[:, :], in0=x5[pb][:, :], in1=rbc5[:, :])
                    v_op("tensor_sub", out=stmp2[:, :], in0=stmp2[:, :], in1=mrbc5[:, :])
                    t_s = v_op("tensor_add",
                               out=s_all[:, mc * TOK + tc * 512: mc * TOK + (tc + 1) * 512],
                               in0=stmp[:, :], in1=stmp2[:, :])
                    p.wait("sync", t_s)
    SAtt.close()

    with contextlib.ExitStack() as S7:
        hb = [sb(f"hb{i}", [128, 512], F32, S7) for i in range(6)]
        q2b = [sb(f"q2b{i}", [128, 512], BF16, S7) for i in range(4)]
        w2b = [sb(f"w2b{i}", [128, 1024], BF16, S7) for i in range(6)]
        x7 = [sb(f"x7{i}", [128, 512], F32, S7) for i in range(8)]
        yb7 = [sb(f"yb7{i}", [128, 512], F32, S7) for i in range(4)]
        eb7 = [sb(f"eb7{i}", [128, 512], F32, S7) for i in range(4)]
        ib7 = [sb(f"ib7{i}", [128, 512], I32, S7) for i in range(4)]
        yt7 = [sb(f"yt7{i}", [128, 512], F32, S7) for i in range(8)]
        gt7 = [sb(f"gt7{i}", [128, 512], F32, S7) for i in range(8)]
        ot7 = [sb(f"ot7{i}", [128, 512], F32, S7) for i in range(8)]
        ps_y = [psum(f"ps_y{i}", [128, 512], S7) for i in range(8)]

        with p.phase("P7_ffn2"):
            p.wait("sync", tok_h)
            p.wait("sync", tok_h2)
            p.wait("scalar", tok_h)
            p.wait("scalar", tok_h2)
            p.wait("sync", tok_qw2)
            t_out_final = None
            last_ev = None
            prev_tail = None
            for tc in range(4):
                tsl = slice(tc * 512, (tc + 1) * 512)
                nf = 32
                t_lh = [None] * nf; t_lw = [None] * nf
                t_y = [None] * nf; t_hop = [None] * nf; t_q = [None] * nf
                t_fc = [None] * nf
                def f_load(fc):
                    i = fc % 6
                    q = "sync" if fc % 2 == 0 else "scalar"
                    if fc >= 6:
                        p.wait(q, t_y[fc - 6])
                        p.wait(q, t_fc[fc - 6] if t_fc[fc - 6] is not None else None)
                    t_lh[fc] = dma(q, hb[i][:, :], h_d[fc * 128:(fc + 1) * 128, tsl])
                    t_lw[fc] = dma(q, w2b[i][:, :], qw2_d[fc * 128:(fc + 1) * 128, :])
                def f_mul(fc):
                    i = fc % 4
                    p.wait("scalar", t_lh[fc])
                    if fc >= 4:
                        p.wait("scalar", t_hop[fc - 4])
                        p.wait("scalar", t_q[fc - 4])
                    t_y[fc] = a_op("mul", out=yb7[i][:, :], in_=hb[fc % 6][:, :],
                                   mul=sc1f_sb[:, fc:fc + 1])
                    p.wait("gpsimd", t_y[fc])
                    t_hop[fc] = dma("gpsimd", ib7[i][:, :], yb7[i][:, :],
                                    sem=("qG" if fc % 2 == 0 else "qG2"))
                def f_fin(fc):
                    i = fc % 4
                    p.wait("vector", t_hop[fc])
                    if fc >= 4:
                        p.wait("vector", t_fc[fc - 4])
                    v_op("tensor_sub", out=eb7[i][:, :], in0=yb7[i][:, :], in1=ib7[i][:, :])
                    t_q[fc] = v_op("scalar_tensor_tensor", out=q2b[i][:, :],
                                   in0=eb7[i][:, :], scalar=0.5,
                                   in1=ib7[i][:, :], op0=ALU.is_ge, op1=ALU.add)
                def f_mm(fc):
                    i = fc % 4
                    p.wait("tensor", t_q[fc])
                    p.wait("tensor", t_lw[fc])
                    if fc == 0 and last_ev is not None:
                        p.wait("tensor", last_ev)
                    tl = None
                    for mc in range(8):
                        tl = mmop(ps_y[mc][:, :], w2b[fc % 6][:, mc * 128:(mc + 1) * 128],
                                  q2b[i][:, :], fc == 0, fc == 31,
                                  inc=(mc == 7))
                    t_fc[fc] = tl
                if tc > 0:
                    p.wait("sync", prev_fc_last)
                    p.wait("scalar", prev_fc_last)
                    p.wait("sync", prev_q_last)
                    p.wait("scalar", prev_q_last)
                    p.wait("sync", prev_y_last)
                    p.wait("scalar", prev_y_last)
                f_load(0); f_mul(0); f_load(1); f_mul(1); f_load(2); f_fin(0)
                f_mul(2); f_load(3); f_fin(1); f_mul(3); f_load(4); f_load(5)
                t_lx = [None] * 8
                for fc in range(nf):
                    f_mm(fc)
                    if fc + 2 < nf:
                        f_fin(fc + 2)
                    if fc + 4 < nf:
                        f_mul(fc + 4)
                    if fc + 6 < nf:
                        f_load(fc + 6)
                    if fc == 24:
                        for mc in range(8):
                            if prev_tail is not None:
                                p.wait("sync", prev_tail)
                            t_lx[mc] = dma("sync", x7[mc][:, :], xT[mc * 128:(mc + 1) * 128, tsl])
                prev_fc_last = t_fc[31]
                prev_q_last = t_q[31]
                prev_y_last = t_y[31]
                t_mm = p.now("sP")
                ts_tok = [None] * 8
                si_tok = [None] * 8
                p.wait("vector", t_mm)
                for mc in range(8):
                    ts_tok[mc] = v_op("tensor_scalar", out=yt7[mc][:, :], in0=ps_y[mc][:, :],
                                      scalar1=b2i_sb[:, mc:mc + 1], scalar2=sc2_sb[:, mc:mc + 1],
                                      op0=ALU.add, op1=ALU.mult)
                for mc in range(8):
                    p.wait("scalar", ts_tok[mc])
                    si_tok[mc] = a_op("activation", out=gt7[mc][:, :], in_=yt7[mc][:, :],
                                      func=AF.Silu, bias=0.0, scale=1.0)
                p.wait("vector", t_lx[0])
                p.wait("vector", p.now("qO"))
                for mc in range(8):
                    p.wait("vector", si_tok[mc])
                    v_op("tensor_mul", out=ot7[mc][:, :], in0=gt7[mc][:, :],
                         in1=s_all[:, mc * TOK + tc * 512: mc * TOK + (tc + 1) * 512])
                    t_o = v_op("tensor_add", out=ot7[mc][:, :], in0=ot7[mc][:, :],
                               in1=x7[mc][:, :])
                    p.wait("sync", t_o)
                    t_out_final = dma("sync", outT[mc * 128:(mc + 1) * 128, tsl], ot7[mc][:, :],
                                      sem="qO")
                prev_tail = t_o
                last_ev = ("sV", ts_tok[7][1])
            p.wait("sync", p.now("qO"))

    stack.close()
    return nc


def kernel(**inputs):
    x = np.ascontiguousarray(inputs["L0_attn_x"]).astype(np.float32, copy=False)
    Wq, Wk, Wv, Wo = (inputs[f"L0_attn_W{n}"] for n in "qkvo")
    bq, bk, bv, bo = (inputs[f"L0_attn_b{n}"] for n in "qkvo")
    Aq, Ak, Av, Ao = (inputs[f"L0_attn_A{n}"] for n in "qkvo")
    Bq, Bk, Bv, Bo = (inputs[f"L0_attn_B{n}"] for n in "qkvo")
    rez = inputs["L0_attn_rezero"]
    w1, b1 = inputs["L0_ffn_w1"], inputs["L0_ffn_b1"]
    w2, b2 = inputs["L0_ffn_w2"], inputs["L0_ffn_b2"]

    C = np.ascontiguousarray
    shared = {
        "wqT": C(Wq.T), "wkT": C(Wk.T), "wvT": C(Wv.T), "woT": C(Wo.T),
        "bq_rs": C(np.asarray(bq).reshape(8, 128).T), "bk_rs": C(np.asarray(bk).reshape(8, 128).T),
        "bo_rs": C(np.asarray(bo).reshape(8, 128).T), "bv_row": C(np.asarray(bv).reshape(1, D)),
        "AT": C(np.concatenate([np.asarray(Aq).T, np.asarray(Ak).T, np.asarray(Av).T], axis=1)),
        "AoT": C(np.asarray(Ao).T),
        "BqT": C(np.asarray(Bq).T), "BkT": C(np.asarray(Bk).T),
        "BvT": C(np.asarray(Bv).T), "BoT": C(np.asarray(Bo).T),
        "rez": C(np.asarray(rez).reshape(1, 1)),
        "w1": C(w1), "w1T": C(np.asarray(w1).T), "w2": C(w2), "w2T": C(np.asarray(w2).T),
        "b1_rs": C(np.asarray(b1).reshape(32, 128).T), "b2_rs": C(np.asarray(b2).reshape(8, 128).T),
        "e1tab": _e1_table(),
        "ident": np.eye(128, dtype=np.float32),
    }
    fb = [_fb_table(0), _fb_table(1)]
    in_maps = []
    for c in range(NCORES):
        b, half = c // 2, c % 2
        xb_ = x[b]
        xrot = np.concatenate([xb_[half * TOK:(half + 1) * TOK],
                               xb_[(1 - half) * TOK:(2 - half) * TOK]], axis=0)
        m = dict(shared)
        m["xT"] = C(xrot.T)
        m["fbtab"] = fb[half]
        in_maps.append(m)

    nc = bass.Bass("TRN2", target_bir_lowering=False, debug=False)
    _build(nc)
    res = run_bass_kernel_spmd(nc, in_maps, core_ids=list(range(NCORES)))
    out = np.empty((B, SEQ, D), dtype=np.float32)
    for c in range(NCORES):
        b, half = c // 2, c % 2
        out[b, half * TOK:(half + 1) * TOK, :] = np.asarray(res.results[c]["outT"]).T
    return out


# version 20
# speedup vs baseline: 1.0204x; 1.0204x over previous
import contextlib
import numpy as np

import concourse.bass as bass
import concourse.mybir as mybir
from concourse.bass_utils import run_bass_kernel_spmd

F32 = mybir.dt.float32
BF16 = mybir.dt.bfloat16
I32 = mybir.dt.int32
AF = mybir.ActivationFunctionType
ALU = mybir.AluOpType
AX = mybir.AxisListType

B, SEQ, D, H, HD, FF = 4, 4096, 1024, 16, 64, 4096
TOK = 2048
NCORES = 8
CSHIFT = 40.0
EPS_LN = 1e-5
EPS_Q = 1e-8
SLOPES = [2.0 ** (-h) for h in range(H)]
NKT = SEQ // 128


def _e1_table():
    i = np.arange(128, dtype=np.float64)[:, None]
    s = np.asarray(SLOPES, dtype=np.float64)[None, :]
    return (s * (i - 127.0) - CSHIFT).astype(np.float32)


def _keep_sets():
    f0, f1 = _fb_raw(0), _fb_raw(1)
    return [[kt for kt in range(NKT) if f0[h][kt] != 0.0 or f1[h][kt] != 0.0]
            for h in range(H)]


def _fb_raw(half):
    t = np.zeros((H, NKT), dtype=np.float32)
    for h in range(H):
        for kt in range(NKT):
            k_true = (half * TOK + kt * 128 + 127) % SEQ
            t[h, kt] = np.float32(np.exp(np.float64(SLOPES[h] * (k_true - 4095))))
    return t


def _fb_table(half):
    t = np.zeros((H, NKT), dtype=np.float32)
    for h in range(H):
        for kt in range(NKT):
            k_true = (half * TOK + kt * 128 + 127) % SEQ
            t[h, kt] = np.float32(np.exp(np.float64(SLOPES[h] * (k_true - 4095))))
    t = t.reshape(1, H * NKT)
    return np.broadcast_to(t, (128, H * NKT)).copy()


class Prog:
    def __init__(self, nc, stack):
        self.nc = nc
        self.stack = stack
        self.ops = None
        self.sem = {}
        self.lastw = {}
        self.last_v = None
        self.last_a = None
        self.last_g = None
        self.hw = {}
        self.synced = {}

    def new_sem(self, name):
        h = self.stack.enter_context(self.nc.semaphore(name))
        self.sem[name] = [h, 0]

    def inc(self, name, n=1):
        s = self.sem[name]
        s[1] += n
        return (name, s[1])

    def now(self, name):
        return (name, self.sem[name][1])

    def handle(self, name):
        return self.sem[name][0]

    DMA_SEMS = ("qS", "qG", "qG2", "qG3", "qA", "qA2", "qO", "qV")

    def wait(self, eng, tok):
        if tok is None:
            return
        name, v = tok
        if name in self.DMA_SEMS:
            v = self.sem[name][1]
            if v > self.hw.get(name, 0):
                self.hw[name] = v
        if v <= 0:
            return
        key = (eng, name)
        if self.lastw.get(key, 0) >= v:
            return
        self.lastw[key] = v
        h = self.sem[name][0]
        self.ops[eng].append(lambda E, h=h, v=v: E.wait_ge(h, v))

    def drain(self, tok):
        for eng in ("sync", "gpsimd", "scalar", "vector", "tensor"):
            self.wait(eng, tok)

    @contextlib.contextmanager
    def phase(self, name):
        self.ops = {"sync": [], "gpsimd": [], "vector": [], "scalar": [], "tensor": []}
        yield
        ops = self.ops
        self.ops = None
        with self.nc.Block(name) as blk:
            for eng_name in ("sync", "gpsimd", "vector", "scalar", "tensor"):
                lst = ops[eng_name]
                if lst:
                    getattr(blk, eng_name)(lambda E, lst=lst: [f(E) for f in lst])


def _build(nc):
    stack = contextlib.ExitStack()
    p = Prog(nc, stack)

    def din(name, shape):
        return nc.dram_tensor(name, shape, F32, kind="ExternalInput").ap()

    xT = din("xT", [D, SEQ])
    wqT = din("wqT", [D, D]); wkT = din("wkT", [D, D])
    wvT = din("wvT", [D, D]); woT = din("woT", [D, D])
    bq_rs = din("bq_rs", [128, 8]); bk_rs = din("bk_rs", [128, 8])
    bo_rs = din("bo_rs", [128, 8]); bv_row = din("bv_row", [1, D])
    AT = din("AT", [D, 24]); AoT = din("AoT", [D, 8])
    BqT = din("BqT", [8, D]); BkT = din("BkT", [8, D])
    BvT = din("BvT", [8, D]); BoT = din("BoT", [8, D])
    rez = din("rez", [1, 1])
    w1 = din("w1", [FF, D]); w1T = din("w1T", [D, FF])
    w2 = din("w2", [D, FF]); w2T = din("w2T", [FF, D])
    b1_rs = din("b1_rs", [128, 32]); b2_rs = din("b2_rs", [128, 8])
    e1tab = din("e1tab", [128, H])
    ident = din("ident", [128, 128])
    fbtab = din("fbtab", [128, H * NKT])
    outT = nc.dram_tensor("outT", [D, TOK], F32, kind="ExternalOutput").ap()

    def dsc(name, shape, dt=F32):
        return nc.dram_tensor(name, shape, dt, kind="Internal").ap()

    xnbf_d = dsc("xnbf_d", [D, SEQ], BF16)
    xn32_d = dsc("xn32_d", [D, TOK])
    h_d = dsc("h_d", [FF, TOK])
    qw2_d = dsc("qw2_d", [FF, D], BF16)
    col_d = dsc("col_d", [128, 1])
    mx1_d = dsc("mx1_d", [128, 1])
    mx2_d = dsc("mx2_d", [128, 1])
    cc1_out = dsc("cc1_out", [128, 1])
    lr_d = [dsc(f"lr_d{i}", [1, 512]) for i in range(2)]
    rsw1_d = dsc("rsw1_d", [32, 128])
    rsw2_d = dsc("rsw2_d", [8, 128])
    r_d = dsc("r_d", [1, SEQ])
    mr_d = dsc("mr_d", [1, SEQ])
    cc2_out = dsc("cc2_out", [128, 1])

    for s in ("qS", "qG", "qG2", "qG3", "qA", "qA2", "qO", "qV", "sV", "sA", "sP", "sG", "sC"):
        p.new_sem(s)

    def sb(name, shape, dt=F32, S=None, side=None):
        return (S or stack).enter_context(nc.sbuf_tensor(name, shape, dt, side=side))

    def psum(name, shape, S=None):
        return (S or stack).enter_context(nc.psum_tensor(name, shape, F32))

    ident_sb = sb("ident_sb", [128, 128])
    ones_col_bf = sb("ones_col_bf", [128, 1], BF16)
    ones_row_f = sb("ones_row_f", [1, 128])
    e1_sb = sb("e1_sb", [128, H])
    fb_sb = sb("fb_sb", [128, H * NKT])

    runmax = sb("runmax", [128, 1])
    hmax = sb("hmax", [128, 1])
    rmx = sb("rmx", [128, 1])
    bqs_sb = sb("bqs_sb", [128, 8])
    bk_sb = sb("bk_sb", [128, 8])
    bo_sb = sb("bo_sb", [128, 8])
    bv_rsb = sb("bv_rsb", [1, D])
    b1i_sb = sb("b1i_sb", [128, 32])
    sc1_sb = sb("sc1_sb", [128, 32])
    sc1f_sb = sb("sc1f_sb", [128, 32])
    b2i_sb = sb("b2i_sb", [128, 8])
    sc2_sb = sb("sc2_sb", [128, 8])
    sw1c = sb("sw1c", [128, 32])
    rw1c = sb("rw1c", [128, 32])
    sw2c = sb("sw2c", [128, 8])
    rw2c = sb("rw2c", [128, 8])
    scl = sb("scl", [128, 8])
    grow = sb("grow", [1, 128])
    gsc = sb("gsc", [1, 8])
    t8a = sb("t8a", [128, 8])
    t8i = sb("t8i", [128, 8], I32)
    t8e = sb("t8e", [128, 8])
    PSB = contextlib.ExitStack()
    ps_b1 = psum("ps_b1", [128, 512], PSB)
    ps_b2 = psum("ps_b2", [128, 512], PSB)

    def v_op(fn_name, **kw):
        tok = p.inc("sV")
        p.wait("vector", p.last_v)
        def f(E, fn_name=fn_name, kw=kw, t=tok):
            getattr(E, fn_name)(**kw).then_inc(p.handle(t[0]), 1)
        p.ops["vector"].append(f)
        p.last_v = tok
        return tok

    def a_op_nochain(fn_name, **kw):
        tok = p.inc("sA")
        def f(E, fn_name=fn_name, kw=kw, t=tok):
            getattr(E, fn_name)(**kw).then_inc(p.handle(t[0]), 1)
        p.ops["scalar"].append(f)
        p.last_a = tok
        return tok

    def a_op(fn_name, **kw):
        tok = p.inc("sA")
        p.wait("scalar", p.last_a)
        def f(E, fn_name=fn_name, kw=kw, t=tok):
            getattr(E, fn_name)(**kw).then_inc(p.handle(t[0]), 1)
        p.ops["scalar"].append(f)
        p.last_a = tok
        return tok

    def dma(queue, out, in_, sem=None):
        semname = sem or {"sync": "qS", "gpsimd": "qG", "scalar": "qA", "vector": "qV"}[queue]
        if queue == "scalar":
            p.wait("scalar", p.last_a)
        elif queue == "gpsimd":
            p.wait("gpsimd", p.last_g)
        elif queue == "vector":
            p.wait("vector", p.last_v)
        hwv = p.hw.get(semname, 0)
        if hwv > p.synced.get(semname, 0):
            p.synced[semname] = hwv
            h = p.handle(semname)
            p.ops[queue].append(lambda E, h=h, v=hwv: E.wait_ge(h, v))
        tok = p.inc(semname, 16)
        p.ops[queue].append(
            lambda E, o=out, i=in_, t=tok: E.dma_start(out=o, in_=i).then_inc(p.handle(t[0]), 16))
        return tok

    def mmop(out, lhsT, rhs, start, stop, tp=None, inc=False):
        tok = p.inc("sP") if inc else None
        def f(E, o=out, l=lhsT, r=rhs, s0=start, s1=stop, tp=tp, t=tok):
            kw = {}
            if tp is not None:
                kw["tile_position"] = tp
            ins = E.matmul(out=o, lhsT=l, rhs=r, start=s0, stop=s1, **kw)
            if t is not None:
                ins.then_inc(p.handle(t[0]), 1)
        p.ops["tensor"].append(f)
        return tok

    def round_ge0(y_ap, i32_ap, e_ap, out_ap, in_tok):
        p.wait("gpsimd", in_tok)
        hop = dma("gpsimd", i32_ap, y_ap)
        p.wait("vector", in_tok)
        p.wait("vector", hop)
        v_op("tensor_sub", out=e_ap, in0=y_ap, in1=i32_ap)
        return v_op("scalar_tensor_tensor", out=out_ap, in0=e_ap, scalar=0.5,
                    in1=i32_ap, op0=ALU.is_ge, op1=ALU.add)

    def round_full(y_ap, i32_ap, e_ap, out_ap, in_tok):
        p.wait("gpsimd", in_tok)
        hop = dma("gpsimd", i32_ap, y_ap)
        p.wait("vector", in_tok)
        p.wait("vector", hop)
        v_op("tensor_sub", out=e_ap, in0=y_ap, in1=i32_ap)
        v_op("scalar_tensor_tensor", out=y_ap, in0=e_ap, scalar=0.5,
             in1=i32_ap, op0=ALU.is_ge, op1=ALU.add)
        v_op("tensor_single_scalar", out=e_ap, in_=e_ap, scalar=-0.5, op=ALU.is_le)
        return v_op("tensor_sub", out=out_ap, in0=y_ap, in1=e_ap)

    with contextlib.ExitStack() as S0:
        m_row = sb("m_row", [1, SEQ], F32, S0)
        r_row = sb("r_row", [1, SEQ], F32, S0)
        mr_row = sb("mr_row", [1, SEQ], F32, S0)
        xf = [sb(f"xf{i}", [128, 4096], F32, S0) for i in range(2)]
        xb = [sb(f"xb{i}", [128, 4096], BF16, S0) for i in range(2)]
        xq = [sb(f"xq{i}", [128, 4096], BF16, S0) for i in range(2)]
        xn32b = [sb(f"xn32b{i}", [128, 4096], F32, S0) for i in range(2)]
        xnbb = [sb(f"xnbb{i}", [128, 4096], BF16, S0) for i in range(2)]
        rbc = sb("rbc", [128, 512], F32, S0)
        mrbc = sb("mrbc", [128, 512], F32, S0)
        tmp32 = sb("tmp32", [128, 512], F32, S0)
        strow = sb("strow", [1, 512], F32, S0)
        strow2 = sb("strow2", [1, 512], F32, S0)
        ps_s = psum("ps_s", [128, 512], S0)
        ps_q = psum("ps_q", [128, 512], S0)

        with p.phase("P0_ln"):
            dma("sync", e1_sb[:, :], e1tab)
            dma("sync", ident_sb[:, :], ident)
            dma("sync", fb_sb[:, :], fbtab)
            t_bq = dma("sync", bqs_sb[:, :], bq_rs)
            dma("sync", bk_sb[:, :], bk_rs)
            dma("sync", bo_sb[:, :], bo_rs)
            dma("sync", bv_rsb[:, :], bv_row)
            t_rz = dma("sync", gsc[0:1, 2:3], rez)
            t_o1 = v_op("memset", ap=ones_col_bf[:, :], constant=1.0)
            t_o2 = v_op("memset", ap=ones_row_f[:, :], constant=1.0)
            v_op("memset", ap=runmax[:, :], constant=-1e30)
            v_op("memset", ap=hmax[:, :], constant=-1e30)
            p.wait("vector", t_bq)
            v_op("tensor_scalar_mul", out=bqs_sb[:, :], in0=bqs_sb[:, :], scalar1=0.125)

            iter_done = {}
            for tc in range(8):
                i = tc % 2
                tsl = slice(tc * 512, (tc + 1) * 512)
                war = iter_done.get(tc - 2, [])
                for w in war:
                    p.wait("sync", w)
                    p.wait("gpsimd", w)
                if tc >= 2:
                    stok = ("qA" if (tc - 2) % 2 == 0 else "qA2", 1)
                    p.wait("sync", stok)
                    p.wait("gpsimd", stok)
                ld, lb = [], []
                for dc in range(8):
                    dsl = slice(dc * 512, (dc + 1) * 512)
                    ld.append(dma("sync", xf[i][:, dsl], xT[dc * 128:(dc + 1) * 128, tsl]))
                    lb.append(dma("gpsimd", xb[i][:, dsl], xT[dc * 128:(dc + 1) * 128, tsl]))
                p.wait("scalar", ld[7])
                for w in war:
                    p.wait("scalar", w)
                t_sq = a_op("activation", out=xq[i][:, :], in_=xf[i][:, :],
                            func=AF.Square, bias=0.0, scale=1.0)
                sq = [t_sq] * 8
                p.wait("tensor", t_o1)
                if tc >= 1:
                    p.wait("tensor", iter_done[tc - 1][0])
                for dc in range(8):
                    p.wait("tensor", lb[dc])
                    mmop(ps_s[0:1, :], ones_col_bf[:, :], xb[i][:, dc * 512:(dc + 1) * 512],
                         dc == 0, dc == 7)
                t_stat = None
                for dc in range(8):
                    p.wait("tensor", sq[dc])
                    t_stat = mmop(ps_q[0:1, :], ones_col_bf[:, :], xq[i][:, dc * 512:(dc + 1) * 512],
                                  dc == 0, dc == 7, inc=(dc == 7))
                p.wait("vector", t_stat)
                v_op("tensor_scalar_mul", out=m_row[0:1, tsl], in0=ps_s[0:1, :], scalar1=1.0 / 1024.0)
                v_op("tensor_scalar_mul", out=strow[0:1, :], in0=ps_q[0:1, :], scalar1=1.0 / 1024.0)
                v_op("tensor_mul", out=strow2[0:1, :], in0=m_row[0:1, tsl], in1=m_row[0:1, tsl])
                v_op("tensor_sub", out=strow[0:1, :], in0=strow[0:1, :], in1=strow2[0:1, :])
                t_var = v_op("tensor_scalar_add", out=strow[0:1, :], in0=strow[0:1, :], scalar1=EPS_LN)
                p.wait("scalar", t_var)
                t_sd = a_op("activation", out=strow2[0:1, :], in_=strow[0:1, :],
                            func=AF.Sqrt, bias=0.0, scale=1.0)
                p.wait("vector", t_sd)
                t_r = v_op("reciprocal", out=r_row[0:1, tsl], in_=strow2[0:1, :])
                t_mr = v_op("tensor_mul", out=mr_row[0:1, tsl], in0=m_row[0:1, tsl], in1=r_row[0:1, tsl])
                p.wait("tensor", t_o2)
                p.wait("tensor", t_r)
                p.wait("tensor", t_mr)
                t_bm1 = mmop(ps_b1[:, :], ones_row_f[0:1, :], r_row[0:1, tsl], True, True, inc=True)
                t_bm2 = mmop(ps_b2[:, :], ones_row_f[0:1, :], mr_row[0:1, tsl], True, True, inc=True)
                p.wait("vector", t_bm1)
                t_rbc = v_op("tensor_copy", out=rbc[:, :], in_=ps_b1[:, :])
                p.wait("vector", t_bm2)
                t_mrbc = v_op("tensor_copy", out=mrbc[:, :], in_=ps_b2[:, :])
                dn = [t_mrbc, t_rbc, t_stat]
                for dc in range(8):
                    dsl = slice(dc * 512, (dc + 1) * 512)
                    v_op("tensor_mul", out=tmp32[:, :], in0=xf[i][:, dsl], in1=rbc[:, :])
                    t_xn = v_op("tensor_sub", out=xn32b[i][:, dsl], in0=tmp32[:, :], in1=mrbc[:, :])
                    v_op("reduce_max", out=rmx[:, :], in_=xn32b[i][:, dsl], axis=AX.X)
                    t_rmu = v_op("tensor_max", out=runmax[:, :], in0=runmax[:, :], in1=rmx[:, :])
                    p.wait("scalar", t_xn)
                    t_cp = a_op("copy", out=xnbb[i][:, dsl], in_=xn32b[i][:, dsl])
                    p.wait("scalar", t_cp)
                    ssem = "qA" if tc % 2 == 0 else "qA2"
                    dma("scalar", xnbf_d[dc * 128:(dc + 1) * 128, tsl], xnbb[i][:, dsl], sem=ssem)
                    dn.extend([t_rmu, t_cp])
                    if tc < 4:
                        dma("scalar", xn32_d[dc * 128:(dc + 1) * 128, tsl], xn32b[i][:, dsl], sem=ssem)
                iter_done[tc] = dn
        with p.phase("P0c_rows"):
            dma("sync", r_d, r_row[0:1, :])
            dma("sync", mr_d, mr_row[0:1, :])
        tok_xnbf = p.now("qA")
        tok_xnbf2 = p.now("qA2")

        with p.phase("P1_cc1"):
            p.wait("sync", p.last_v)
            t_mx = dma("sync", mx1_d, runmax[:, :])
            p.wait("gpsimd", t_mx)
            tok_c = p.inc("sC")
            p.ops["gpsimd"].append(lambda E, t=tok_c: E.collective_compute(
                "AllReduce", ALU.max, replica_groups=[list(range(NCORES))],
                ins=[mx1_d.rearrange("p o -> o p").opt()],
                outs=[cc1_out.rearrange("p o -> o p").opt()]).then_inc(p.handle(t[0]), 1))
            p.wait("sync", tok_c)
            t_g = dma("sync", grow[0:1, :], cc1_out.rearrange("p o -> o p"))
            p.wait("vector", t_g)
            v_op("reduce_max", out=gsc[0:1, 0:1], in_=grow[0:1, :], axis=AX.X)
            t_s1 = v_op("tensor_scalar", out=gsc[0:1, 0:1], in0=gsc[0:1, 0:1], scalar1=EPS_Q,
                        scalar2=1.0 / 255.0, op0=ALU.max, op1=ALU.mult)
            p.wait("tensor", t_s1)
            t_b = mmop(ps_b1[:, 0:1], ones_row_f[0:1, :], gsc[0:1, 0:1], True, True, inc=True)
            p.wait("vector", t_b)
            v_op("tensor_copy", out=scl[:, 1:2], in_=ps_b1[:, 0:1])
            v_op("reciprocal", out=scl[:, 0:1], in_=scl[:, 1:2])
            p.wait("tensor", t_rz)
            p.wait("tensor", p.last_v)
            t_b2 = mmop(ps_b2[:, 0:1], ones_row_f[0:1, :], gsc[0:1, 2:3], True, True, inc=True)
            p.wait("vector", t_b2)
            v_op("tensor_copy", out=scl[:, 2:3], in_=ps_b2[:, 0:1])

    S23 = contextlib.ExitStack()
    qw1T = sb("qw1T", [128, 8 * FF], BF16, S23)
    with contextlib.ExitStack() as S2:
        rr1 = sb("rr1", [1, FF], F32, S2)
        SWT = contextlib.ExitStack()
        wt = [sb(f"wt{i}", [128, 4096], F32, SWT, side="right") for i in range(2)]
        rbc1 = [sb(f"rbc1_{i}", [128, 1024], F32, S2) for i in range(4)]
        sm32 = sb("sm32", [128, 128], F32, S2)
        sm32b = sb("sm32b", [128, 128], F32, S2)
        sm32i = sb("sm32i", [128, 32], I32, S2)

        with p.phase("P2_wprep"):
            p.drain(tok_xnbf)
            p.drain(tok_xnbf2)
            w1r = w1.rearrange("(g a p) d -> g p a d", p=128, a=4)
            red1 = [None] * 8
            dma("sync", wt[0][:, :], w1r[0])
            dma("scalar", wt[1][:, :], w1r[1])
            for gld in range(8):
                i = gld % 2
                p.wait("vector", (("qS", p.sem["qS"][1]) if i == 0 else ("qA", p.sem["qA"][1])))
                for a in range(4):
                    red1[gld] = v_op("reduce_max", out=sw1c[:, gld * 4 + a: gld * 4 + a + 1],
                                     in_=wt[i][:, a * 1024:(a + 1) * 1024],
                                     axis=AX.X, apply_absolute_value=True)
                if gld + 2 < 8:
                    q = "sync" if i == 0 else "scalar"
                    p.wait(q, red1[gld])
                    dma(q, wt[i][:, :], w1r[gld + 2])
            v_op("tensor_scalar", out=sw1c[:, :], in0=sw1c[:, :], scalar1=EPS_Q,
                 scalar2=1.0 / 127.0, op0=ALU.max, op1=ALU.mult)
            t_rw1 = v_op("reciprocal", out=rw1c[:, :], in_=sw1c[:, :])
            p.wait("sync", p.last_v)
            p.wait("scalar", p.last_v)
            SWT.close()
            w4 = [sb(f"w4{i}", [128, 1024], F32, S2) for i in range(2)]
            ybuf = [sb(f"ybuf{i}", [128, 1024], F32, S2) for i in range(3)]
            ebuf = [sb(f"ebuf{i}", [128, 1024], F32, S2) for i in range(3)]
            ibuf = [sb(f"ibuf{i}", [128, 1024], I32, S2) for i in range(3)]
            def pe_transpose(out_ps, in_ap):
                tok = p.inc("sP")
                p.ops["tensor"].append(
                    lambda E, o=out_ps, i=in_ap, t=tok: E.transpose(
                        out=o, in_=i, identity=ident_sb[:, :]).then_inc(p.handle(t[0]), 1))
                return tok
            p.wait("tensor", t_rw1)
            t_t1 = pe_transpose(ps_b1[0:32, 0:128], rw1c[:, :])
            p.wait("vector", t_t1)
            t_e1t = v_op("tensor_copy", out=sm32[0:32, 0:128], in_=ps_b1[0:32, 0:128])
            p.wait("gpsimd", t_e1t)
            t_d1 = dma("gpsimd", rsw1_d, sm32[0:32, 0:128])
            p.wait("gpsimd", t_d1)
            t_r1 = dma("gpsimd", rr1[0:1, :], rsw1_d.rearrange("t p -> (t p)")[None, :])
            p.wait("tensor", t_r1)
            for hl in range(4):
                for j in range(2):
                    p.wait("tensor", p.last_v)
                    t_b = mmop(ps_b1[:, :], ones_row_f[0:1, :],
                               rr1[0:1, hl * 1024 + j * 512: hl * 1024 + (j + 1) * 512],
                               True, True, inc=True)
                    p.wait("vector", t_b)
                    v_op("tensor_copy", out=rbc1[hl][:, j * 512:(j + 1) * 512], in_=ps_b1[:, :])
            w1h = w1T.rearrange("(g p) (hl f) -> g hl p f", p=128, hl=4)
            nW = 32
            t_l = [None] * nW; t_mu = [None] * nW; t_hp = [None] * nW
            muls_done = {}
            def wq_load(c):
                q = "sync" if c % 2 == 0 else "scalar"
                if c >= 2:
                    p.wait(q, t_mu[c - 2])
                t_l[c] = dma(q, w4[c % 2][:, :], w1h[c // 4, c % 4])
            def wq_mul(c):
                i = c % 3
                p.wait("vector", t_l[c])
                if muls_done.get(i) is not None:
                    p.wait("vector", muls_done[i])
                t_mu[c] = v_op("tensor_mul", out=ybuf[i][:, :], in0=w4[c % 2][:, :],
                               in1=rbc1[c % 4][:, :])
                p.wait("gpsimd", t_mu[c])
                t_hp[c] = dma("gpsimd", ibuf[i][:, :], ybuf[i][:, :],
                              sem=("qG" if c % 2 == 0 else "qG2"))
                muls_done[i] = t_hp[c]
            def wq_fin(c):
                i = c % 3
                p.wait("vector", t_hp[c])
                v_op("tensor_sub", out=ebuf[i][:, :], in0=ybuf[i][:, :], in1=ibuf[i][:, :])
                v_op("scalar_tensor_tensor", out=ybuf[i][:, :], in0=ebuf[i][:, :], scalar=0.5,
                     in1=ibuf[i][:, :], op0=ALU.is_ge, op1=ALU.add)
                v_op("tensor_single_scalar", out=ebuf[i][:, :], in_=ebuf[i][:, :],
                     scalar=-0.5, op=ALU.is_le)
                dc, hl = c // 4, c % 4
                v_op("tensor_sub",
                     out=qw1T[:, dc * FF + hl * 1024: dc * FF + (hl + 1) * 1024],
                     in0=ybuf[i][:, :], in1=ebuf[i][:, :])
            wq_load(0); wq_mul(0); wq_load(1); wq_mul(1); wq_load(2)
            for c in range(nW):
                wq_fin(c)
                if c + 2 < nW:
                    wq_mul(c + 2)
                if c + 3 < nW:
                    wq_load(c + 3)
            t_b1 = dma("sync", sm32[:, 0:32], b1_rs)
            p.wait("vector", t_b1)
            v_op("tensor_mul", out=sm32b[:, 0:32], in0=sm32[:, 0:32], in1=rw1c[:, :])
            t_y0 = v_op("tensor_scalar_mul", out=sm32b[:, 0:32], in0=sm32b[:, 0:32], scalar1=scl[:, 0:1])
            round_full(sm32b[:, 0:32], sm32i[:, :], sm32[:, 0:32], b1i_sb[:, :], t_y0)
            v_op("tensor_scalar_mul", out=sc1_sb[:, :], in0=sw1c[:, :], scalar1=scl[:, 1:2])

    with contextlib.ExitStack() as S3:
        q1 = sb("q1", [128, 8 * TOK], BF16, S3)
        xq32 = [sb(f"xq32{i}", [128, 512], F32, S3) for i in range(2)]
        w4 = [sb(f"w4b{i}", [128, 1024], F32, S3) for i in range(2)]
        ybuf = [sb(f"ybuf_b{i}", [128, 1024], F32, S3) for i in range(3)]
        ebuf = [sb(f"ebuf_b{i}", [128, 1024], F32, S3) for i in range(3)]
        ibuf = [sb(f"ibuf_b{i}", [128, 1024], I32, S3) for i in range(3)]
        ps_h = [psum(f"ps_h{i}", [128, 512], S3) for i in range(6)]
        SWT3 = contextlib.ExitStack()
        wtb = [sb(f"wtb{i}", [128, 2048], F32, SWT3, side="right") for i in range(2)]
        swtmp = sb("swtmp", [128, 2], F32, S3)

        with p.phase("P3_ffn1"):
            for s_ in ("qG", "qG2", "qG3", "qA", "qS"):
                p.drain(p.now(s_))
            w2r = w2.rearrange("(g p) (hl d) -> g hl p d", p=128, hl=2)
            tokp = None
            t_lw2 = [None] * 16
            t_lw2[0] = dma("sync", wtb[0][:, :], w2r[0, 0])
            t_lw2[1] = dma("scalar", wtb[1][:, :], w2r[0, 1])
            for j in range(16):
                i, hl = j // 2, j % 2
                q = "sync" if j % 2 == 0 else "scalar"
                p.wait("vector", t_lw2[j])
                v_op("reduce_max", out=swtmp[:, hl:hl + 1], in_=wtb[j % 2][:, :],
                     axis=AX.X, apply_absolute_value=True)
                if hl == 1:
                    v_op("tensor_max", out=sw2c[:, i:i + 1], in0=swtmp[:, 0:1], in1=swtmp[:, 1:2])
                tokp = p.last_v
                if j + 2 < 16:
                    p.wait(q, tokp)
                    t_lw2[j + 2] = dma(q, wtb[j % 2][:, :], w2r[(j + 2) // 2, (j + 2) % 2])
            v_op("tensor_scalar", out=sw2c[:, :], in0=sw2c[:, :], scalar1=EPS_Q,
                 scalar2=1.0 / 127.0, op0=ALU.max, op1=ALU.mult)
            t_rw2 = v_op("reciprocal", out=rw2c[:, :], in_=sw2c[:, :])
            p.wait("sync", t_rw2)
            p.wait("scalar", t_rw2)
            SWT3.close()
            ht = [sb(f"ht{i}", [128, 512], F32, S3) for i in range(6)]
            rmxall = sb("rmxall", [128, 32], F32, S3)
            rr2 = sb("rr2", [1, D], F32, S3)
            rbc2 = sb("rbc2", [128, D], F32, S3)
            qc2 = [sb("qc2_0", [128, D], BF16, S3)] * 2
            def pe_transpose3(out_ps, in_ap):
                tok = p.inc("sP")
                p.ops["tensor"].append(
                    lambda E, o=out_ps, i=in_ap, t=tok: E.transpose(
                        out=o, in_=i, identity=ident_sb[:, :]).then_inc(p.handle(t[0]), 1))
                return tok
            p.wait("tensor", t_rw2)
            t_t2 = pe_transpose3(ps_b2[0:8, 0:128], rw2c[:, :])
            p.wait("vector", t_t2)
            t_e2t = v_op("tensor_copy", out=ht[0][0:8, 0:128], in_=ps_b2[0:8, 0:128])
            p.wait("gpsimd", t_e2t)
            t_d2 = dma("gpsimd", rsw2_d, ht[0][0:8, 0:128])
            p.wait("gpsimd", t_d2)
            t_r2 = dma("gpsimd", rr2[0:1, :], rsw2_d.rearrange("t p -> (t p)")[None, :])
            p.wait("tensor", t_r2)
            for j in range(2):
                p.wait("tensor", p.last_v)
                t_b = mmop(ps_b2[:, :], ones_row_f[0:1, :], rr2[0:1, j * 512:(j + 1) * 512],
                           True, True, inc=True)
                p.wait("vector", t_b)
                v_op("tensor_copy", out=rbc2[:, j * 512:(j + 1) * 512], in_=ps_b2[:, :])
            t_rbc2_done = p.last_v
            nq = 32
            t_yq = [None] * nq
            t_hopq = [None] * nq
            t_lq = [None] * nq
            def q1_load(j):
                tc, dc = j // 8, j % 8
                t_lq[j] = dma("sync", xq32[j % 2][:, :],
                              xn32_d[dc * 128:(dc + 1) * 128, tc * 512:(tc + 1) * 512])
            def q1_mul(j):
                i = j % 3
                p.wait("vector", t_lq[j])
                t_yq[j] = v_op("tensor_scalar", out=ybuf[i][:, 0:512], in0=xq32[j % 2][:, :],
                               scalar1=scl[:, 0:1], scalar2=0.0, op0=ALU.mult, op1=ALU.max)
                p.wait("gpsimd", t_yq[j])
                t_hopq[j] = dma("gpsimd", ibuf[i][:, 0:512], ybuf[i][:, 0:512],
                                sem=("qG" if j % 2 == 0 else "qG2"))
            def q1_fin(j):
                i = j % 3
                tc, dc = j // 8, j % 8
                p.wait("vector", t_hopq[j])
                v_op("tensor_sub", out=ebuf[i][:, 0:512], in0=ybuf[i][:, 0:512], in1=ibuf[i][:, 0:512])
                v_op("scalar_tensor_tensor",
                     out=q1[:, dc * TOK + tc * 512: dc * TOK + (tc + 1) * 512],
                     in0=ebuf[i][:, 0:512], scalar=0.5,
                     in1=ibuf[i][:, 0:512], op0=ALU.is_ge, op1=ALU.add)
            q1_load(0); q1_mul(0); q1_load(1); q1_mul(1)
            p.wait("sync", t_yq[0])
            q1_load(2)
            for j in range(nq):
                q1_fin(j)
                if j + 2 < nq:
                    q1_mul(j + 2)
                if j + 3 < nq:
                    p.wait("sync", t_yq[j + 1])
                    q1_load(j + 3)
            t_l2 = [None] * 32; t_mu2 = [None] * 32; t_hp2 = [None] * 32; t_st2 = [None] * 32
            muls2 = {}
            def w2_load(c):
                q = "sync" if c % 2 == 0 else "scalar"
                if c >= 2:
                    p.wait(q, t_mu2[c - 2])
                t_l2[c] = dma(q, w4[c % 2][:, :], w2T[c * 128:(c + 1) * 128, :])
            def w2_mul(c):
                i = c % 3
                p.wait("vector", t_l2[c])
                if muls2.get(i) is not None:
                    p.wait("vector", muls2[i])
                t_mu2[c] = v_op("tensor_mul", out=ybuf[i][:, :], in0=w4[c % 2][:, :],
                                in1=rbc2[:, :])
                p.wait("gpsimd", t_mu2[c])
                t_hp2[c] = dma("gpsimd", ibuf[i][:, :], ybuf[i][:, :],
                               sem=("qG" if c % 2 == 0 else "qG2"))
                muls2[i] = t_hp2[c]
            def w2_fin(c):
                i = c % 3
                p.wait("vector", t_hp2[c])
                v_op("tensor_sub", out=ebuf[i][:, :], in0=ybuf[i][:, :], in1=ibuf[i][:, :])
                v_op("scalar_tensor_tensor", out=ybuf[i][:, :], in0=ebuf[i][:, :], scalar=0.5,
                     in1=ibuf[i][:, :], op0=ALU.is_ge, op1=ALU.add)
                v_op("tensor_single_scalar", out=ebuf[i][:, :], in_=ebuf[i][:, :],
                     scalar=-0.5, op=ALU.is_le)
                if c >= 1 and t_st2[c - 1] is not None:
                    p.wait("vector", t_st2[c - 1])
                v_op("tensor_sub", out=qc2[c % 2][:, :], in0=ybuf[i][:, :], in1=ebuf[i][:, :])
                p.wait("gpsimd", p.last_v)
                t_st2[c] = dma("gpsimd", qw2_d[c * 128:(c + 1) * 128, :], qc2[c % 2][:, :],
                               sem="qG3")
            def w2_gen():
                w2_load(0); w2_mul(0); w2_load(1); w2_mul(1); w2_load(2)
                yield
                for c in range(32):
                    w2_fin(c)
                    if c + 2 < 32:
                        w2_mul(c + 2)
                    if c + 3 < 32:
                        w2_load(c + 3)
                    yield
            gen = w2_gen()
            p.wait("tensor", p.last_v)
            p.wait("vector", p.now("qG"))
            war = {}
            ev_tok = {}
            red_tok = {}
            for tc in range(4):
                for fc in range(32):
                    k = tc * 32 + fc
                    pb = k % 6
                    if k - 6 in war:
                        p.wait("tensor", war[k - 6])
                    t_mm = None
                    for dc in range(8):
                        t_mm = mmop(ps_h[pb][:, :],
                                    qw1T[:, dc * FF + fc * 128: dc * FF + (fc + 1) * 128],
                                    q1[:, dc * TOK + tc * 512: dc * TOK + (tc + 1) * 512],
                                    dc == 0, dc == 7, inc=(dc == 7))
                    p.wait("scalar", t_mm)
                    if k >= 6:
                        p.wait("scalar", ("qA", 1))
                        p.wait("scalar", red_tok[k - 6])
                    t_e1 = a_op_nochain("activation", out=ht[k % 6][:, :], in_=ps_h[pb][:, :],
                                        func=AF.Relu, bias=b1i_sb[:, fc:fc + 1], scale=1.0)
                    p.wait("vector", t_e1)
                    red_tok[k] = v_op("reduce_max", out=rmxall[:, fc:fc + 1], in_=ht[k % 6][:, :], axis=AX.X)
                    war[k] = t_e1
                    ev_tok[k] = t_e1
                    if k % 4 == 3:
                        for kk in range(k - 3, k + 1):
                            p.wait("scalar", ev_tok[kk])
                            dma("scalar",
                                h_d[(kk % 32) * 128:(kk % 32 + 1) * 128,
                                    (kk // 32) * 512:(kk // 32 + 1) * 512],
                                ht[kk % 6][:, :])
                        next(gen, None)
                p.wait("vector", p.last_v)
                v_op("tensor_mul", out=rmxall[:, :], in0=rmxall[:, :], in1=sc1_sb[:, :])
                v_op("reduce_max", out=rmx[:, :], in_=rmxall[:, :], axis=AX.X)
                v_op("tensor_max", out=hmax[:, :], in0=hmax[:, :], in1=rmx[:, :])
            for _ in range(40):
                next(gen, None)
    tok_qw2 = p.now("qG3")
    tok_h = p.now("qA")

    with p.phase("P3b_cc2"):
        p.wait("sync", p.last_v)
        t_mx = dma("sync", mx2_d, hmax[:, :])
        p.wait("gpsimd", t_mx)
        tok_c = p.inc("sC")
        p.ops["gpsimd"].append(lambda E, t=tok_c: E.collective_compute(
            "AllReduce", ALU.max, replica_groups=[list(range(NCORES))],
            ins=[mx2_d.rearrange("p o -> o p").opt()],
            outs=[cc2_out.rearrange("p o -> o p").opt()]).then_inc(p.handle(t[0]), 1))
        p.wait("sync", tok_c)
        t_g = dma("sync", grow[0:1, :], cc2_out.rearrange("p o -> o p"))
        p.wait("vector", t_g)
        v_op("reduce_max", out=gsc[0:1, 4:5], in_=grow[0:1, :], axis=AX.X)
        t_s2 = v_op("tensor_scalar", out=gsc[0:1, 4:5], in0=gsc[0:1, 4:5], scalar1=EPS_Q,
                    scalar2=1.0 / 255.0, op0=ALU.max, op1=ALU.mult)
        p.wait("tensor", t_s2)
        t_b = mmop(ps_b1[:, 0:1], ones_row_f[0:1, :], gsc[0:1, 4:5], True, True, inc=True)
        p.wait("vector", t_b)
        v_op("tensor_copy", out=scl[:, 4:5], in_=ps_b1[:, 0:1])
        v_op("reciprocal", out=scl[:, 3:4], in_=scl[:, 4:5])
        t_b2l = dma("sync", t8a[:, :], b2_rs)
        p.wait("vector", t_b2l)
        v_op("tensor_mul", out=t8a[:, :], in0=t8a[:, :], in1=rw2c[:, :])
        t_y = v_op("tensor_scalar_mul", out=t8a[:, :], in0=t8a[:, :], scalar1=scl[:, 3:4])
        round_full(t8a[:, :], t8i[:, :], t8e[:, :], b2i_sb[:, :], t_y)
        v_op("tensor_scalar_mul", out=sc2_sb[:, :], in0=sw2c[:, :], scalar1=scl[:, 4:5])
        v_op("tensor_scalar_mul", out=sc1f_sb[:, :], in0=sc1_sb[:, :], scalar1=scl[:, 3:4])
    S23.close()
    PSB.close()

    SAtt = contextlib.ExitStack()
    OT = sb("OT", [128, 8 * TOK], BF16, SAtt)
    u_all = sb("u_all", [72, SEQ], BF16, SAtt)
    u_o = sb("u_o", [8, 512], BF16, SAtt)

    KEEPS_ALL = _keep_sets()
    with contextlib.ExitStack() as S4:
        xnt = [sb(f"xnt{i}", [128, 4096], BF16, S4) for i in range(2)]
        QT = sb("QT", [128, 4 * TOK], BF16, S4)
        KT = sb("KT", [128, 4 * SEQ], BF16, S4)
        vt32 = sb("vt32", [128, 512], F32, S4)
        bvbc = sb("bvbc", [128, 512], F32, S4)
        BqT_bf = sb("BqT_bf", [8, D], BF16, S4)
        BkT_bf = sb("BkT_bf", [40, D], BF16, S4)
        BvT_bf = sb("BvT_bf", [72, D], BF16, S4)
        AT_bf = sb("AT_bf", [128, 8 * 24], BF16, S4)

        t_Bv = None
        for g in range(2):
            gsl = slice(g * 512, (g + 1) * 512)
            SWg = contextlib.ExitStack()
            keeps_g = [KEEPS_ALL[g * 8 + h] for h in range(8)]
            posof = [{kt: i for i, kt in enumerate(kp)} for kp in keeps_g]
            Vp = [sb(f"Vp{g}_{h}", [128, len(keeps_g[h]) * 65 + 128], BF16, SWg) for h in range(8)]
            SWa = contextlib.ExitStack()
            wq_bf = sb(f"wq_bf{g}", [128, 8 * 512], BF16, SWa)
            wk_bf = sb(f"wk_bf{g}", [128, 8 * 512], BF16, SWa)
            wv_bf = sb(f"wv_bf{g}", [128, 8 * 512], BF16, SWa)
            ps_p = [psum(f"ps_p{g}{i}", [128, 512], SWa) for i in range(4)]
            ps_u4 = psum(f"ps_u4{g}", [24, 512], SWa)
            ps_bv = psum(f"ps_bv{g}", [128, 512], SWa)
            with p.phase(f"P4a_proj{g}"):
                p.drain(tok_h)
                p.drain(tok_xnbf2)
                t_wv = None
                for dc in range(8):
                    rows = slice(dc * 128, (dc + 1) * 128)
                    dma("gpsimd", wq_bf[:, dc * 512:(dc + 1) * 512], wqT[rows, gsl])
                    dma("gpsimd", wk_bf[:, dc * 512:(dc + 1) * 512], wkT[rows, gsl])
                    t_wv = dma("gpsimd", wv_bf[:, dc * 512:(dc + 1) * 512], wvT[rows, gsl])
                    if g == 0:
                        dma("gpsimd", AT_bf[:, dc * 24:(dc + 1) * 24], AT[rows, :])
                if g == 0:
                    dma("gpsimd", BqT_bf[0:8, :], BqT)
                    dma("gpsimd", BkT_bf[32:40, :], BkT)
                    t_Bv = dma("gpsimd", BvT_bf[64:72, :], BvT)
                t_wload = p.now("qG")
                p.wait("tensor", p.last_v)
                t_b = mmop(ps_bv[:, :], ones_row_f[0:1, :], bv_rsb[0:1, gsl], True, True, inc=True)
                p.wait("vector", t_b)
                t_bvbc = v_op("tensor_copy", out=bvbc[:, :], in_=ps_bv[:, :])
                p.wait("vector", t_wload)
                for h in range(8):
                    nk_h = len(keeps_g[h])
                    v_op("memset", ap=Vp[h].ap()[:, nk_h * 65: nk_h * 65 + 128], constant=0.0)
                for h in range(8):
                    hh = g * 8 + h
                    nk_h = len(keeps_g[h])
                    fcol = Vp[h].ap()[:, 0:nk_h * 65].rearrange("p (k c) -> p k c", c=65)[:, :, 64]
                    for pos, kt in enumerate(keeps_g[h]):
                        v_op("tensor_copy", out=fcol[:, pos:pos + 1],
                             in_=fb_sb[:, hh * NKT + kt: hh * NKT + kt + 1])
                ev_war = [None, None, None, None]
                rotc = [0]
                uo_war = None
                prev_iter = {}
                for tc in range(8):
                    i4 = tc % 2
                    tsl = slice(tc * 512, (tc + 1) * 512)
                    if tc - 2 in prev_iter:
                        p.wait("sync", prev_iter[tc - 2])
                    xl = []
                    for dc in range(8):
                        xl.append(dma("sync", xnt[i4][:, dc * 512:(dc + 1) * 512],
                                      xnbf_d[dc * 128:(dc + 1) * 128, tsl]))
                    p.wait("tensor", t_wload)
                    g_dma1 = g_dma2 = None
                    if g == 0:
                        if uo_war is not None:
                            p.wait("tensor", uo_war)
                        t_u = None
                        for dc in range(8):
                            p.wait("tensor", xl[dc])
                            t_u = mmop(ps_u4[:, :], AT_bf[:, dc * 24:(dc + 1) * 24],
                                       xnt[i4][:, dc * 512:(dc + 1) * 512],
                                       dc == 0, dc == 7, inc=(dc == 7))
                        p.wait("scalar", t_u)
                        t_uc = a_op("activation", out=u_all[0:24, tsl], in_=ps_u4[:, :],
                                    func=AF.Copy, bias=0.0, scale=0.125)
                        uo_war = t_uc
                        p.wait("gpsimd", t_uc)
                        g_dma1 = dma("gpsimd", u_all[32:40, tsl], u_all[8:16, tsl])
                        g_dma2 = dma("gpsimd", u_all[64:72, tsl], u_all[16:24, tsl])
                    for mc in range(4):
                        pb = rotc[0] % 4; rotc[0] += 1
                        if ev_war[pb] is not None:
                            p.wait("tensor", ev_war[pb])
                        for dc in range(8):
                            p.wait("tensor", xl[dc])
                            mmop(ps_p[pb][:, :],
                                 wk_bf[:, dc * 512 + mc * 128: dc * 512 + (mc + 1) * 128],
                                 xnt[i4][:, dc * 512:(dc + 1) * 512], dc == 0, False)
                        if g == 0:
                            p.wait("tensor", g_dma1)
                        t_k = mmop(ps_p[pb][:, :],
                                   BkT_bf[32:40, g * 512 + mc * 128: g * 512 + (mc + 1) * 128],
                                   u_all[32:40, tsl], False, True, inc=True)
                        p.wait("scalar", t_k)
                        ev_war[pb] = a_op_nochain("activation",
                                          out=KT[:, mc * SEQ + tc * 512: mc * SEQ + (tc + 1) * 512],
                                          in_=ps_p[pb][:, :], func=AF.Identity,
                                          bias=bk_sb[:, g * 4 + mc: g * 4 + mc + 1], scale=1.0)
                    if tc < 4:
                        for mc in range(4):
                            pb = rotc[0] % 4; rotc[0] += 1
                            if ev_war[pb] is not None:
                                p.wait("tensor", ev_war[pb])
                            for dc in range(8):
                                mmop(ps_p[pb][:, :],
                                     wq_bf[:, dc * 512 + mc * 128: dc * 512 + (mc + 1) * 128],
                                     xnt[i4][:, dc * 512:(dc + 1) * 512], dc == 0, False)
                            t_q = mmop(ps_p[pb][:, :],
                                       BqT_bf[0:8, g * 512 + mc * 128: g * 512 + (mc + 1) * 128],
                                       u_all[0:8, tsl], False, True, inc=True)
                            p.wait("scalar", t_q)
                            ev_war[pb] = a_op_nochain("activation",
                                              out=QT[:, mc * TOK + tc * 512: mc * TOK + (tc + 1) * 512],
                                              in_=ps_p[pb][:, :], func=AF.Identity,
                                              bias=bqs_sb[:, g * 4 + mc: g * 4 + mc + 1], scale=0.125)
                    for j in range(4):
                        tt = tc * 4 + j
                        pb = rotc[0] % 4; rotc[0] += 1
                        if ev_war[pb] is not None:
                            p.wait("tensor", ev_war[pb])
                        for dc in range(8):
                            mmop(ps_p[pb][:, :],
                                 xnt[i4][:, dc * 512 + j * 128: dc * 512 + (j + 1) * 128],
                                 wv_bf[:, dc * 512:(dc + 1) * 512], dc == 0, False)
                        if g == 0:
                            p.wait("tensor", g_dma2)
                            p.wait("tensor", t_Bv)
                        t_v = mmop(ps_p[pb][:, :], u_all[64:72, tt * 128:(tt + 1) * 128],
                                   BvT_bf[64:72, gsl], False, True, inc=True)
                        p.wait("vector", t_v)
                        p.wait("vector", t_bvbc)
                        t_vt = v_op("tensor_add", out=vt32[:, :], in0=ps_p[pb][:, :], in1=bvbc[:, :])
                        ev_war[pb] = t_vt
                        for h in range(8):
                            if tt not in posof[h]:
                                continue
                            pos = posof[h][tt]
                            v_op("tensor_scalar_mul",
                                 out=Vp[h].ap()[:, pos * 65: pos * 65 + 64],
                                 in0=vt32[:, h * 64:(h + 1) * 64],
                                 scalar1=fb_sb[:, (g * 8 + h) * NKT + tt:(g * 8 + h) * NKT + tt + 1])
                    prev_iter[tc] = t_v
            SWa.close()

            SWb = contextlib.ExitStack()
            Pt = [sb(f"Pt{g}{i}", [128, 1536], BF16, SWb) for i in range(2)]
            lrow = [sb(f"lrow{g}{i}", [128, 512], F32, SWb) for i in range(2)]
            rbx = [sb(f"rbx{g}{i}", [64, 512], F32, SWb) for i in range(2)]
            otmp = [sb(f"otmp{g}{i}", [64, 512], BF16, SWb) for i in range(2)]
            ps_S = [psum(f"ps_S{g}{i}", [128, 1536], SWb) for i in range(2)]
            ps_PV = [psum(f"ps_PV{g}{i}", [128, 512], SWb) for i in range(2)]
            with p.phase(f"P4b_attn{g}"):
                pv_ev = [None, None]
                s_free = [None, None]
                pt_free = [None, None]
                pend = None

                def emit_epi(hq, qq, pvbq, t_pvq):
                    pjq, subq = hq // 2, hq % 2
                    ot_base = (g * 4 + pjq) * TOK + qq * 512
                    p.wait("vector", t_pvq)
                    v_op("tensor_copy", out=lrow[pvbq][64:65, :], in_=ps_PV[pvbq][64:65, :])
                    t_rc = v_op("reciprocal", out=lrow[pvbq][64:65, :], in_=lrow[pvbq][64:65, :])
                    p.wait("gpsimd", t_rc)
                    t_b1 = dma("gpsimd", lr_d[pvbq], lrow[pvbq][64:65, :], sem="qG3")
                    p.wait("gpsimd", t_b1)
                    t_bc = dma("gpsimd", rbx[pvbq][:, :], lr_d[pvbq].broadcast_to((64, 512)), sem="qG3")
                    p.wait("vector", t_bc)
                    if subq == 0:
                        t_o = v_op("tensor_mul",
                                   out=OT.ap()[0:64, ot_base:ot_base + 512],
                                   in0=ps_PV[pvbq][0:64, :], in1=rbx[pvbq][:, :])
                    else:
                        t_o = v_op("tensor_mul", out=otmp[pvbq][:, :],
                                   in0=ps_PV[pvbq][0:64, :], in1=rbx[pvbq][:, :])
                        p.wait("gpsimd", t_o)
                        t_mv2 = dma("gpsimd", OT.ap()[64:128, ot_base:ot_base + 512],
                                    otmp[pvbq][:, :])
                        p.wait("vector", t_mv2)
                    pv_ev[pvbq] = t_o

                for h in range(8):
                    hh = g * 8 + h
                    pj, sub = h // 2, h % 2
                    rows = slice(64 * sub, 64 * sub + 64)
                    tp = (64 * sub, 0)
                    kept = keeps_g[h]
                    grps = [kept[i:i + 3] for i in range(0, len(kept), 3)]
                    ng = len(grps)
                    vH = Vp[h]
                    for qc in range(4):
                        qsl = slice(pj * TOK + qc * 512, pj * TOK + (qc + 1) * 512)
                        pvb = qc % 2
                        t_qk = [None] * ng

                        def emit_qk(gi, first):
                            sbk = gi % 2
                            p.wait("tensor", s_free[sbk])
                            if first:
                                p.wait("tensor", pv_ev[pvb])
                            tq = None
                            for idx, kt in enumerate(grps[gi]):
                                ksl = slice(pj * SEQ + kt * 128, pj * SEQ + (kt + 1) * 128)
                                tq = mmop(ps_S[sbk][:, idx * 512:(idx + 1) * 512],
                                          KT.ap()[rows, ksl], QT.ap()[rows, qsl],
                                          True, True, tp=tp, inc=(idx == len(grps[gi]) - 1))
                            t_qk[gi] = tq

                        emit_qk(0, True)
                        if ng > 1:
                            emit_qk(1, False)
                        t_pv = None
                        for gi, grp in enumerate(grps):
                            sbk = gi % 2
                            p.wait("scalar", t_qk[gi])
                            p.wait("scalar", pt_free[sbk])
                            n = len(grp) * 512
                            t_act = a_op_nochain("activation", out=Pt[sbk][:, 0:n], in_=ps_S[sbk][:, 0:n],
                                                 func=AF.Exp, bias=e1_sb[:, hh:hh + 1], scale=1.0)
                            s_free[sbk] = t_act
                            p.wait("tensor", t_act)
                            if gi + 2 < ng:
                                emit_qk(gi + 2, False)
                            for idx, kt in enumerate(grp):
                                pos = posof[h][kt]
                                t_pv = mmop(ps_PV[pvb][:, :], vH.ap()[:, pos * 65: pos * 65 + 128],
                                            Pt[sbk][:, idx * 512:(idx + 1) * 512],
                                            (gi == 0 and idx == 0),
                                            (gi == ng - 1 and idx == len(grp) - 1),
                                            inc=(idx == len(grp) - 1))
                            pt_free[sbk] = t_pv
                            if gi == 1 and pend is not None:
                                emit_epi(*pend)
                                pend = None
                        if pend is not None:
                            emit_epi(*pend)
                            pend = None
                        pend = (h, qc, pvb, t_pv)
                if pend is not None:
                    emit_epi(*pend)
                    pend = None
            SWb.close()
            SWg.close()
    tok_ot = p.now("qG")

    s_all = sb("s_all", [128, 8 * TOK], F32, side="right")
    with contextlib.ExitStack() as S5:
        wo_bf = sb("wo_bf", [128, 8 * 1024], BF16, S5)
        AoT_bf = sb("AoT_bf", [128, 8 * 8], BF16, S5)
        BoT_bf = sb("BoT_bf", [8, D], BF16, S5)
        x5 = [sb(f"x5{i}", [128, 512], F32, S5) for i in range(2)]
        r5 = sb("r5", [1, SEQ], F32, S5)
        mr5 = sb("mr5", [1, SEQ], F32, S5)
        rbc5 = sb("rbc5", [128, 512], F32, S5)
        mrbc5 = sb("mrbc5", [128, 512], F32, S5)
        stmp = sb("stmp", [128, 512], F32, S5)
        stmp2 = sb("stmp2", [128, 512], F32, S5)
        ps_o = [psum(f"ps_o{i}", [128, 512], S5) for i in range(2)]
        ps_uo = psum("ps_uo", [24, 512], S5)
        ps_b51 = psum("ps_b51", [128, 512], S5)
        ps_b52 = psum("ps_b52", [128, 512], S5)

        with p.phase("P5_oproj"):
            p.drain(tok_ot)
            for dc in range(8):
                rows = slice(dc * 128, (dc + 1) * 128)
                dma("gpsimd", wo_bf[:, dc * 1024:(dc + 1) * 1024], woT[rows, :])
                dma("gpsimd", AoT_bf[:, dc * 8:(dc + 1) * 8], AoT[rows, :])
            t_Bo = dma("gpsimd", BoT_bf[:, :], BoT)
            t_rr = dma("sync", r5[0:1, :], r_d)
            dma("sync", mr5[0:1, :], mr_d)
            t_wo = p.now("qG")
            p.wait("tensor", tok_ot)
            ev_war = [None, None]
            uo_war = None
            for tc in range(4):
                tsl = slice(tc * 512, (tc + 1) * 512)
                p.wait("tensor", p.last_v)
                p.wait("tensor", t_rr)
                t_b1m = mmop(ps_b51[:, :], ones_row_f[0:1, :], r5[0:1, tsl], True, True, inc=True)
                t_b2m = mmop(ps_b52[:, :], ones_row_f[0:1, :], mr5[0:1, tsl], True, True, inc=True)
                p.wait("vector", t_b1m)
                v_op("tensor_copy", out=rbc5[:, :], in_=ps_b51[:, :])
                p.wait("vector", t_b2m)
                v_op("tensor_copy", out=mrbc5[:, :], in_=ps_b52[:, :])
                p.wait("tensor", t_wo)
                if uo_war is not None:
                    p.wait("tensor", uo_war)
                t_u = None
                for dc in range(8):
                    t_u = mmop(ps_uo[0:8, :], AoT_bf[:, dc * 8:(dc + 1) * 8],
                               OT.ap()[:, dc * TOK + tc * 512: dc * TOK + (tc + 1) * 512],
                               dc == 0, dc == 7, inc=(dc == 7))
                p.wait("scalar", t_u)
                t_uo = a_op("activation", out=u_o[0:8, :], in_=ps_uo[0:8, :],
                            func=AF.Copy, bias=0.0, scale=0.125)
                uo_war = t_uo
                for mc in range(8):
                    pb = mc % 2
                    t_lx = dma("sync", x5[pb][:, :], xT[mc * 128:(mc + 1) * 128, tsl])
                    if ev_war[pb] is not None:
                        p.wait("tensor", ev_war[pb])
                    for dc in range(8):
                        mmop(ps_o[pb][:, :],
                             wo_bf[:, dc * 1024 + mc * 128: dc * 1024 + (mc + 1) * 128],
                             OT.ap()[:, dc * TOK + tc * 512: dc * TOK + (tc + 1) * 512],
                             dc == 0, False)
                    p.wait("tensor", t_uo)
                    p.wait("tensor", t_Bo)
                    t_om = mmop(ps_o[pb][:, :], BoT_bf[0:8, mc * 128:(mc + 1) * 128],
                                u_o[0:8, :], False, True, inc=True)
                    p.wait("vector", t_om)
                    v_op("tensor_scalar", out=stmp[:, :], in0=ps_o[pb][:, :],
                         scalar1=bo_sb[:, mc:mc + 1], scalar2=scl[:, 2:3],
                         op0=ALU.add, op1=ALU.mult)
                    ev_war[pb] = p.last_v
                    p.wait("vector", t_lx)
                    v_op("tensor_mul", out=stmp2[:, :], in0=x5[pb][:, :], in1=rbc5[:, :])
                    v_op("tensor_sub", out=stmp2[:, :], in0=stmp2[:, :], in1=mrbc5[:, :])
                    t_s = v_op("tensor_add",
                               out=s_all[:, mc * TOK + tc * 512: mc * TOK + (tc + 1) * 512],
                               in0=stmp[:, :], in1=stmp2[:, :])
                    p.wait("sync", t_s)
    SAtt.close()

    with contextlib.ExitStack() as S7:
        hb = [sb(f"hb{i}", [128, 512], F32, S7) for i in range(6)]
        q2b = [sb(f"q2b{i}", [128, 512], BF16, S7) for i in range(4)]
        w2b = [sb(f"w2b{i}", [128, 1024], BF16, S7) for i in range(6)]
        x7 = [sb(f"x7{i}", [128, 512], F32, S7) for i in range(8)]
        yb7 = [sb(f"yb7{i}", [128, 512], F32, S7) for i in range(4)]
        eb7 = [sb(f"eb7{i}", [128, 512], F32, S7) for i in range(4)]
        ib7 = [sb(f"ib7{i}", [128, 512], I32, S7) for i in range(4)]
        yt7 = [sb(f"yt7{i}", [128, 512], F32, S7) for i in range(8)]
        gt7 = [sb(f"gt7{i}", [128, 512], F32, S7) for i in range(8)]
        ot7 = [sb(f"ot7{i}", [128, 512], F32, S7) for i in range(8)]
        ps_y = [psum(f"ps_y{i}", [128, 512], S7) for i in range(8)]

        with p.phase("P7_ffn2"):
            p.wait("sync", tok_h)
            p.wait("sync", tok_qw2)
            t_out_final = None
            last_ev = None
            prev_tail = None
            for tc in range(4):
                tsl = slice(tc * 512, (tc + 1) * 512)
                nf = 32
                t_lh = [None] * nf; t_lw = [None] * nf
                t_y = [None] * nf; t_hop = [None] * nf; t_q = [None] * nf
                t_fc = [None] * nf
                def f_load(fc):
                    i = fc % 6
                    q = "sync" if fc % 2 == 0 else "scalar"
                    if fc >= 6:
                        p.wait(q, t_y[fc - 6])
                        p.wait(q, t_fc[fc - 6] if t_fc[fc - 6] is not None else None)
                    t_lh[fc] = dma(q, hb[i][:, :], h_d[fc * 128:(fc + 1) * 128, tsl])
                    t_lw[fc] = dma(q, w2b[i][:, :], qw2_d[fc * 128:(fc + 1) * 128, :])
                def f_mul(fc):
                    i = fc % 4
                    p.wait("scalar", t_lh[fc])
                    if fc >= 4:
                        p.wait("scalar", t_hop[fc - 4])
                        p.wait("scalar", t_q[fc - 4])
                    t_y[fc] = a_op_nochain("mul", out=yb7[i][:, :], in_=hb[fc % 6][:, :],
                                           mul=sc1f_sb[:, fc:fc + 1])
                    p.wait("gpsimd", t_y[fc])
                    t_hop[fc] = dma("gpsimd", ib7[i][:, :], yb7[i][:, :],
                                    sem=("qG" if fc % 2 == 0 else "qG2"))
                def f_fin(fc):
                    i = fc % 4
                    p.wait("vector", t_hop[fc])
                    if fc >= 4:
                        p.wait("vector", t_fc[fc - 4])
                    v_op("tensor_sub", out=eb7[i][:, :], in0=yb7[i][:, :], in1=ib7[i][:, :])
                    t_q[fc] = v_op("scalar_tensor_tensor", out=q2b[i][:, :],
                                   in0=eb7[i][:, :], scalar=0.5,
                                   in1=ib7[i][:, :], op0=ALU.is_ge, op1=ALU.add)
                def f_mm(fc):
                    i = fc % 4
                    p.wait("tensor", t_q[fc])
                    p.wait("tensor", t_lw[fc])
                    if fc == 0 and last_ev is not None:
                        p.wait("tensor", last_ev)
                    tl = None
                    for mc in range(8):
                        tl = mmop(ps_y[mc][:, :], w2b[fc % 6][:, mc * 128:(mc + 1) * 128],
                                  q2b[i][:, :], fc == 0, fc == 31,
                                  inc=(mc == 7))
                    t_fc[fc] = tl
                if tc > 0:
                    p.wait("sync", prev_fc_last)
                    p.wait("scalar", prev_fc_last)
                    p.wait("sync", prev_q_last)
                    p.wait("scalar", prev_q_last)
                    p.wait("sync", prev_y_last)
                    p.wait("scalar", prev_y_last)
                f_load(0); f_mul(0); f_load(1); f_mul(1); f_load(2); f_fin(0)
                f_mul(2); f_load(3); f_fin(1); f_mul(3); f_load(4); f_load(5)
                t_lx = [None] * 8
                for fc in range(nf):
                    f_mm(fc)
                    if fc + 2 < nf:
                        f_fin(fc + 2)
                    if fc + 4 < nf:
                        f_mul(fc + 4)
                    if fc + 6 < nf:
                        f_load(fc + 6)
                    if fc == 24:
                        for mc in range(8):
                            if prev_tail is not None:
                                p.wait("sync", prev_tail)
                            t_lx[mc] = dma("sync", x7[mc][:, :], xT[mc * 128:(mc + 1) * 128, tsl])
                prev_fc_last = t_fc[31]
                prev_q_last = t_q[31]
                prev_y_last = t_y[31]
                t_mm = p.now("sP")
                ts_tok = [None] * 8
                si_tok = [None] * 8
                p.wait("vector", t_mm)
                for mc in range(8):
                    ts_tok[mc] = v_op("tensor_scalar", out=yt7[mc][:, :], in0=ps_y[mc][:, :],
                                      scalar1=b2i_sb[:, mc:mc + 1], scalar2=sc2_sb[:, mc:mc + 1],
                                      op0=ALU.add, op1=ALU.mult)
                for mc in range(8):
                    p.wait("scalar", ts_tok[mc])
                    si_tok[mc] = a_op_nochain("activation", out=gt7[mc][:, :], in_=yt7[mc][:, :],
                                              func=AF.Silu, bias=0.0, scale=1.0)
                p.wait("vector", t_lx[0])
                p.wait("vector", p.now("qO"))
                for mc in range(8):
                    p.wait("vector", si_tok[mc])
                    v_op("tensor_mul", out=ot7[mc][:, :], in0=gt7[mc][:, :],
                         in1=s_all[:, mc * TOK + tc * 512: mc * TOK + (tc + 1) * 512])
                    t_o = v_op("tensor_add", out=ot7[mc][:, :], in0=ot7[mc][:, :],
                               in1=x7[mc][:, :])
                    p.wait("sync", t_o)
                    t_out_final = dma("sync", outT[mc * 128:(mc + 1) * 128, tsl], ot7[mc][:, :],
                                      sem="qO")
                prev_tail = t_o
                last_ev = ("sV", ts_tok[7][1])
            p.wait("sync", p.now("qO"))

    stack.close()
    return nc


def kernel(**inputs):
    x = np.ascontiguousarray(inputs["L0_attn_x"]).astype(np.float32, copy=False)
    Wq, Wk, Wv, Wo = (inputs[f"L0_attn_W{n}"] for n in "qkvo")
    bq, bk, bv, bo = (inputs[f"L0_attn_b{n}"] for n in "qkvo")
    Aq, Ak, Av, Ao = (inputs[f"L0_attn_A{n}"] for n in "qkvo")
    Bq, Bk, Bv, Bo = (inputs[f"L0_attn_B{n}"] for n in "qkvo")
    rez = inputs["L0_attn_rezero"]
    w1, b1 = inputs["L0_ffn_w1"], inputs["L0_ffn_b1"]
    w2, b2 = inputs["L0_ffn_w2"], inputs["L0_ffn_b2"]

    C = np.ascontiguousarray
    shared = {
        "wqT": C(Wq.T), "wkT": C(Wk.T), "wvT": C(Wv.T), "woT": C(Wo.T),
        "bq_rs": C(np.asarray(bq).reshape(8, 128).T), "bk_rs": C(np.asarray(bk).reshape(8, 128).T),
        "bo_rs": C(np.asarray(bo).reshape(8, 128).T), "bv_row": C(np.asarray(bv).reshape(1, D)),
        "AT": C(np.concatenate([np.asarray(Aq).T, np.asarray(Ak).T, np.asarray(Av).T], axis=1)),
        "AoT": C(np.asarray(Ao).T),
        "BqT": C(np.asarray(Bq).T), "BkT": C(np.asarray(Bk).T),
        "BvT": C(np.asarray(Bv).T), "BoT": C(np.asarray(Bo).T),
        "rez": C(np.asarray(rez).reshape(1, 1)),
        "w1": C(w1), "w1T": C(np.asarray(w1).T), "w2": C(w2), "w2T": C(np.asarray(w2).T),
        "b1_rs": C(np.asarray(b1).reshape(32, 128).T), "b2_rs": C(np.asarray(b2).reshape(8, 128).T),
        "e1tab": _e1_table(),
        "ident": np.eye(128, dtype=np.float32),
    }
    fb = [_fb_table(0), _fb_table(1)]
    in_maps = []
    for c in range(NCORES):
        b, half = c // 2, c % 2
        xb_ = x[b]
        xrot = np.concatenate([xb_[half * TOK:(half + 1) * TOK],
                               xb_[(1 - half) * TOK:(2 - half) * TOK]], axis=0)
        m = dict(shared)
        m["xT"] = C(xrot.T)
        m["fbtab"] = fb[half]
        in_maps.append(m)

    nc = bass.Bass("TRN2", target_bir_lowering=False, debug=False)
    _build(nc)
    res = run_bass_kernel_spmd(nc, in_maps, core_ids=list(range(NCORES)))
    out = np.empty((B, SEQ, D), dtype=np.float32)
    for c in range(NCORES):
        b, half = c // 2, c % 2
        out[b, half * TOK:(half + 1) * TOK, :] = np.asarray(res.results[c]["outT"]).T
    return out


# version 21
# speedup vs baseline: 1.0219x; 1.0015x over previous
import contextlib
import numpy as np

import concourse.bass as bass
import concourse.mybir as mybir
from concourse.bass_utils import run_bass_kernel_spmd

F32 = mybir.dt.float32
BF16 = mybir.dt.bfloat16
I32 = mybir.dt.int32
AF = mybir.ActivationFunctionType
ALU = mybir.AluOpType
AX = mybir.AxisListType

B, SEQ, D, H, HD, FF = 4, 4096, 1024, 16, 64, 4096
TOK = 2048
NCORES = 8
CSHIFT = 40.0
EPS_LN = 1e-5
EPS_Q = 1e-8
SLOPES = [2.0 ** (-h) for h in range(H)]
NKT = SEQ // 128


def _e1_table():
    i = np.arange(128, dtype=np.float64)[:, None]
    s = np.asarray(SLOPES, dtype=np.float64)[None, :]
    return (s * (i - 127.0) - CSHIFT).astype(np.float32)


def _keep_sets():
    f0, f1 = _fb_raw(0), _fb_raw(1)
    return [[kt for kt in range(NKT) if f0[h][kt] != 0.0 or f1[h][kt] != 0.0]
            for h in range(H)]


def _fb_raw(half):
    t = np.zeros((H, NKT), dtype=np.float32)
    for h in range(H):
        for kt in range(NKT):
            k_true = (half * TOK + kt * 128 + 127) % SEQ
            t[h, kt] = np.float32(np.exp(np.float64(SLOPES[h] * (k_true - 4095))))
    return t


def _fb_table(half):
    t = np.zeros((H, NKT), dtype=np.float32)
    for h in range(H):
        for kt in range(NKT):
            k_true = (half * TOK + kt * 128 + 127) % SEQ
            t[h, kt] = np.float32(np.exp(np.float64(SLOPES[h] * (k_true - 4095))))
    t = t.reshape(1, H * NKT)
    return np.broadcast_to(t, (128, H * NKT)).copy()


class Prog:
    def __init__(self, nc, stack):
        self.nc = nc
        self.stack = stack
        self.ops = None
        self.sem = {}
        self.lastw = {}
        self.last_v = None
        self.last_a = None
        self.last_g = None
        self.hw = {}
        self.synced = {}

    def new_sem(self, name):
        h = self.stack.enter_context(self.nc.semaphore(name))
        self.sem[name] = [h, 0]

    def inc(self, name, n=1):
        s = self.sem[name]
        s[1] += n
        return (name, s[1])

    def now(self, name):
        return (name, self.sem[name][1])

    def handle(self, name):
        return self.sem[name][0]

    DMA_SEMS = ("qS", "qG", "qG2", "qG3", "qA", "qA2", "qO", "qV")

    def wait(self, eng, tok):
        if tok is None:
            return
        name, v = tok
        if name in self.DMA_SEMS:
            v = self.sem[name][1]
            if v > self.hw.get(name, 0):
                self.hw[name] = v
        if v <= 0:
            return
        key = (eng, name)
        if self.lastw.get(key, 0) >= v:
            return
        self.lastw[key] = v
        h = self.sem[name][0]
        self.ops[eng].append(lambda E, h=h, v=v: E.wait_ge(h, v))

    def drain(self, tok):
        for eng in ("sync", "gpsimd", "scalar", "vector", "tensor"):
            self.wait(eng, tok)

    @contextlib.contextmanager
    def phase(self, name):
        self.ops = {"sync": [], "gpsimd": [], "vector": [], "scalar": [], "tensor": []}
        yield
        ops = self.ops
        self.ops = None
        with self.nc.Block(name) as blk:
            for eng_name in ("sync", "gpsimd", "vector", "scalar", "tensor"):
                lst = ops[eng_name]
                if lst:
                    getattr(blk, eng_name)(lambda E, lst=lst: [f(E) for f in lst])


def _build(nc):
    stack = contextlib.ExitStack()
    p = Prog(nc, stack)

    def din(name, shape):
        return nc.dram_tensor(name, shape, F32, kind="ExternalInput").ap()

    xT = din("xT", [D, SEQ])
    wqT = din("wqT", [D, D]); wkT = din("wkT", [D, D])
    wvT = din("wvT", [D, D]); woT = din("woT", [D, D])
    bq_rs = din("bq_rs", [128, 8]); bk_rs = din("bk_rs", [128, 8])
    bo_rs = din("bo_rs", [128, 8]); bv_row = din("bv_row", [1, D])
    AT = din("AT", [D, 24]); AoT = din("AoT", [D, 8])
    BqT = din("BqT", [8, D]); BkT = din("BkT", [8, D])
    BvT = din("BvT", [8, D]); BoT = din("BoT", [8, D])
    rez = din("rez", [1, 1])
    w1 = din("w1", [FF, D]); w1T = din("w1T", [D, FF])
    w2 = din("w2", [D, FF]); w2T = din("w2T", [FF, D])
    b1_rs = din("b1_rs", [128, 32]); b2_rs = din("b2_rs", [128, 8])
    e1tab = din("e1tab", [128, H])
    ident = din("ident", [128, 128])
    fbtab = din("fbtab", [128, H * NKT])
    outT = nc.dram_tensor("outT", [D, TOK], F32, kind="ExternalOutput").ap()

    def dsc(name, shape, dt=F32):
        return nc.dram_tensor(name, shape, dt, kind="Internal").ap()

    xnbf_d = dsc("xnbf_d", [D, SEQ], BF16)
    xn32_d = dsc("xn32_d", [D, TOK])
    h_d = dsc("h_d", [FF, TOK])
    qw2_d = dsc("qw2_d", [FF, D], BF16)
    col_d = dsc("col_d", [128, 1])
    mx1_d = dsc("mx1_d", [128, 1])
    mx2_d = dsc("mx2_d", [128, 1])
    cc1_out = dsc("cc1_out", [128, 1])
    lr_d = [dsc(f"lr_d{i}", [1, 512]) for i in range(2)]
    rsw1_d = dsc("rsw1_d", [32, 128])
    rsw2_d = dsc("rsw2_d", [8, 128])
    r_d = dsc("r_d", [1, SEQ])
    mr_d = dsc("mr_d", [1, SEQ])
    cc2_out = dsc("cc2_out", [128, 1])

    for s in ("qS", "qG", "qG2", "qG3", "qA", "qA2", "qO", "qV", "sV", "sA", "sP", "sG", "sC"):
        p.new_sem(s)

    def sb(name, shape, dt=F32, S=None, side=None):
        return (S or stack).enter_context(nc.sbuf_tensor(name, shape, dt, side=side))

    def psum(name, shape, S=None):
        return (S or stack).enter_context(nc.psum_tensor(name, shape, F32))

    ident_sb = sb("ident_sb", [128, 128])
    ones_col_bf = sb("ones_col_bf", [128, 1], BF16)
    ones_row_f = sb("ones_row_f", [1, 128])
    e1_sb = sb("e1_sb", [128, H])
    fb_sb = sb("fb_sb", [128, H * NKT])

    runmax = sb("runmax", [128, 1])
    hmax = sb("hmax", [128, 1])
    rmx = sb("rmx", [128, 1])
    bqs_sb = sb("bqs_sb", [128, 8])
    bk_sb = sb("bk_sb", [128, 8])
    bo_sb = sb("bo_sb", [128, 8])
    bv_rsb = sb("bv_rsb", [1, D])
    b1i_sb = sb("b1i_sb", [128, 32])
    sc1_sb = sb("sc1_sb", [128, 32])
    sc1f_sb = sb("sc1f_sb", [128, 32])
    b2i_sb = sb("b2i_sb", [128, 8])
    sc2_sb = sb("sc2_sb", [128, 8])
    sw1c = sb("sw1c", [128, 32])
    rw1c = sb("rw1c", [128, 32])
    sw2c = sb("sw2c", [128, 8])
    rw2c = sb("rw2c", [128, 8])
    scl = sb("scl", [128, 8])
    grow = sb("grow", [1, 128])
    gsc = sb("gsc", [1, 8])
    t8a = sb("t8a", [128, 8])
    t8i = sb("t8i", [128, 8], I32)
    t8e = sb("t8e", [128, 8])
    PSB = contextlib.ExitStack()
    ps_b1 = psum("ps_b1", [128, 512], PSB)
    ps_b2 = psum("ps_b2", [128, 512], PSB)

    def v_op(fn_name, **kw):
        tok = p.inc("sV")
        p.wait("vector", p.last_v)
        def f(E, fn_name=fn_name, kw=kw, t=tok):
            getattr(E, fn_name)(**kw).then_inc(p.handle(t[0]), 1)
        p.ops["vector"].append(f)
        p.last_v = tok
        return tok

    def a_op_nochain(fn_name, **kw):
        tok = p.inc("sA")
        def f(E, fn_name=fn_name, kw=kw, t=tok):
            getattr(E, fn_name)(**kw).then_inc(p.handle(t[0]), 1)
        p.ops["scalar"].append(f)
        p.last_a = tok
        return tok

    def a_op(fn_name, **kw):
        tok = p.inc("sA")
        p.wait("scalar", p.last_a)
        def f(E, fn_name=fn_name, kw=kw, t=tok):
            getattr(E, fn_name)(**kw).then_inc(p.handle(t[0]), 1)
        p.ops["scalar"].append(f)
        p.last_a = tok
        return tok

    def dma(queue, out, in_, sem=None):
        semname = sem or {"sync": "qS", "gpsimd": "qG", "scalar": "qA", "vector": "qV"}[queue]
        if queue == "scalar":
            p.wait("scalar", p.last_a)
        elif queue == "gpsimd":
            p.wait("gpsimd", p.last_g)
        elif queue == "vector":
            p.wait("vector", p.last_v)
        hwv = p.hw.get(semname, 0)
        if hwv > p.synced.get(semname, 0):
            p.synced[semname] = hwv
            h = p.handle(semname)
            p.ops[queue].append(lambda E, h=h, v=hwv: E.wait_ge(h, v))
        tok = p.inc(semname, 16)
        p.ops[queue].append(
            lambda E, o=out, i=in_, t=tok: E.dma_start(out=o, in_=i).then_inc(p.handle(t[0]), 16))
        return tok

    def mmop(out, lhsT, rhs, start, stop, tp=None, inc=False):
        tok = p.inc("sP") if inc else None
        def f(E, o=out, l=lhsT, r=rhs, s0=start, s1=stop, tp=tp, t=tok):
            kw = {}
            if tp is not None:
                kw["tile_position"] = tp
            ins = E.matmul(out=o, lhsT=l, rhs=r, start=s0, stop=s1, **kw)
            if t is not None:
                ins.then_inc(p.handle(t[0]), 1)
        p.ops["tensor"].append(f)
        return tok

    def round_ge0(y_ap, i32_ap, e_ap, out_ap, in_tok):
        p.wait("gpsimd", in_tok)
        hop = dma("gpsimd", i32_ap, y_ap)
        p.wait("vector", in_tok)
        p.wait("vector", hop)
        v_op("tensor_sub", out=e_ap, in0=y_ap, in1=i32_ap)
        return v_op("scalar_tensor_tensor", out=out_ap, in0=e_ap, scalar=0.5,
                    in1=i32_ap, op0=ALU.is_ge, op1=ALU.add)

    def round_full(y_ap, i32_ap, e_ap, out_ap, in_tok):
        p.wait("gpsimd", in_tok)
        hop = dma("gpsimd", i32_ap, y_ap)
        p.wait("vector", in_tok)
        p.wait("vector", hop)
        v_op("tensor_sub", out=e_ap, in0=y_ap, in1=i32_ap)
        v_op("scalar_tensor_tensor", out=y_ap, in0=e_ap, scalar=0.5,
             in1=i32_ap, op0=ALU.is_ge, op1=ALU.add)
        v_op("tensor_single_scalar", out=e_ap, in_=e_ap, scalar=-0.5, op=ALU.is_le)
        return v_op("tensor_sub", out=out_ap, in0=y_ap, in1=e_ap)

    with contextlib.ExitStack() as S0:
        m_row = sb("m_row", [1, SEQ], F32, S0)
        r_row = sb("r_row", [1, SEQ], F32, S0)
        mr_row = sb("mr_row", [1, SEQ], F32, S0)
        xf = [sb(f"xf{i}", [128, 4096], F32, S0) for i in range(2)]
        xb = [sb(f"xb{i}", [128, 4096], BF16, S0) for i in range(2)]
        xq = [sb(f"xq{i}", [128, 4096], BF16, S0) for i in range(2)]
        xn32b = [sb(f"xn32b{i}", [128, 4096], F32, S0) for i in range(2)]
        xnbb = [sb(f"xnbb{i}", [128, 4096], BF16, S0) for i in range(2)]
        rbc = sb("rbc", [128, 512], F32, S0)
        mrbc = sb("mrbc", [128, 512], F32, S0)
        tmp32 = sb("tmp32", [128, 512], F32, S0)
        strow = sb("strow", [1, 512], F32, S0)
        strow2 = sb("strow2", [1, 512], F32, S0)
        ps_s = psum("ps_s", [128, 512], S0)
        ps_q = psum("ps_q", [128, 512], S0)

        with p.phase("P0_ln"):
            dma("sync", e1_sb[:, :], e1tab)
            dma("sync", ident_sb[:, :], ident)
            dma("sync", fb_sb[:, :], fbtab)
            t_bq = dma("sync", bqs_sb[:, :], bq_rs)
            dma("sync", bk_sb[:, :], bk_rs)
            dma("sync", bo_sb[:, :], bo_rs)
            dma("sync", bv_rsb[:, :], bv_row)
            t_rz = dma("sync", gsc[0:1, 2:3], rez)
            t_o1 = v_op("memset", ap=ones_col_bf[:, :], constant=1.0)
            t_o2 = v_op("memset", ap=ones_row_f[:, :], constant=1.0)
            v_op("memset", ap=runmax[:, :], constant=-1e30)
            v_op("memset", ap=hmax[:, :], constant=-1e30)
            p.wait("vector", t_bq)
            v_op("tensor_scalar_mul", out=bqs_sb[:, :], in0=bqs_sb[:, :], scalar1=0.125)

            iter_done = {}
            for tc in range(8):
                i = tc % 2
                tsl = slice(tc * 512, (tc + 1) * 512)
                war = iter_done.get(tc - 2, [])
                for w in war:
                    p.wait("sync", w)
                    p.wait("gpsimd", w)
                if tc >= 2:
                    stok = ("qA" if (tc - 2) % 2 == 0 else "qA2", 1)
                    p.wait("sync", stok)
                    p.wait("gpsimd", stok)
                xTr = xT.rearrange("(dc p) t -> p dc t", p=128)
                t_ldf = dma("sync", xf[i][:, :], xTr[:, :, tsl])
                t_ldb = dma("gpsimd", xb[i][:, :], xTr[:, :, tsl])
                ld = [t_ldf] * 8
                lb = [t_ldb] * 8
                p.wait("scalar", ld[0])
                for w in war:
                    p.wait("scalar", w)
                t_sq = a_op("activation", out=xq[i][:, :], in_=xf[i][:, :],
                            func=AF.Square, bias=0.0, scale=1.0)
                sq = [t_sq] * 8
                p.wait("tensor", t_o1)
                if tc >= 1:
                    p.wait("tensor", iter_done[tc - 1][0])
                for dc in range(8):
                    p.wait("tensor", lb[dc])
                    mmop(ps_s[0:1, :], ones_col_bf[:, :], xb[i][:, dc * 512:(dc + 1) * 512],
                         dc == 0, dc == 7)
                t_stat = None
                for dc in range(8):
                    p.wait("tensor", sq[dc])
                    t_stat = mmop(ps_q[0:1, :], ones_col_bf[:, :], xq[i][:, dc * 512:(dc + 1) * 512],
                                  dc == 0, dc == 7, inc=(dc == 7))
                p.wait("vector", t_stat)
                v_op("tensor_scalar_mul", out=m_row[0:1, tsl], in0=ps_s[0:1, :], scalar1=1.0 / 1024.0)
                v_op("tensor_scalar_mul", out=strow[0:1, :], in0=ps_q[0:1, :], scalar1=1.0 / 1024.0)
                v_op("tensor_mul", out=strow2[0:1, :], in0=m_row[0:1, tsl], in1=m_row[0:1, tsl])
                v_op("tensor_sub", out=strow[0:1, :], in0=strow[0:1, :], in1=strow2[0:1, :])
                t_var = v_op("tensor_scalar_add", out=strow[0:1, :], in0=strow[0:1, :], scalar1=EPS_LN)
                p.wait("scalar", t_var)
                t_sd = a_op("activation", out=strow2[0:1, :], in_=strow[0:1, :],
                            func=AF.Sqrt, bias=0.0, scale=1.0)
                p.wait("vector", t_sd)
                t_r = v_op("reciprocal", out=r_row[0:1, tsl], in_=strow2[0:1, :])
                t_mr = v_op("tensor_mul", out=mr_row[0:1, tsl], in0=m_row[0:1, tsl], in1=r_row[0:1, tsl])
                p.wait("tensor", t_o2)
                p.wait("tensor", t_r)
                p.wait("tensor", t_mr)
                t_bm1 = mmop(ps_b1[:, :], ones_row_f[0:1, :], r_row[0:1, tsl], True, True, inc=True)
                t_bm2 = mmop(ps_b2[:, :], ones_row_f[0:1, :], mr_row[0:1, tsl], True, True, inc=True)
                p.wait("vector", t_bm1)
                t_rbc = v_op("tensor_copy", out=rbc[:, :], in_=ps_b1[:, :])
                p.wait("vector", t_bm2)
                t_mrbc = v_op("tensor_copy", out=mrbc[:, :], in_=ps_b2[:, :])
                dn = [t_mrbc, t_rbc, t_stat]
                for dc in range(8):
                    dsl = slice(dc * 512, (dc + 1) * 512)
                    v_op("tensor_mul", out=tmp32[:, :], in0=xf[i][:, dsl], in1=rbc[:, :])
                    t_xn = v_op("tensor_sub", out=xn32b[i][:, dsl], in0=tmp32[:, :], in1=mrbc[:, :])
                    v_op("reduce_max", out=rmx[:, :], in_=xn32b[i][:, dsl], axis=AX.X)
                    t_rmu = v_op("tensor_max", out=runmax[:, :], in0=runmax[:, :], in1=rmx[:, :])
                    p.wait("scalar", t_xn)
                    t_cp = a_op("copy", out=xnbb[i][:, dsl], in_=xn32b[i][:, dsl])
                    p.wait("scalar", t_cp)
                    ssem = "qA" if tc % 2 == 0 else "qA2"
                    dma("scalar", xnbf_d[dc * 128:(dc + 1) * 128, tsl], xnbb[i][:, dsl], sem=ssem)
                    dn.extend([t_rmu, t_cp])
                    if tc < 4:
                        dma("scalar", xn32_d[dc * 128:(dc + 1) * 128, tsl], xn32b[i][:, dsl], sem=ssem)
                iter_done[tc] = dn
        with p.phase("P0c_rows"):
            dma("sync", r_d, r_row[0:1, :])
            dma("sync", mr_d, mr_row[0:1, :])
        tok_xnbf = p.now("qA")
        tok_xnbf2 = p.now("qA2")

        with p.phase("P1_cc1"):
            p.wait("sync", p.last_v)
            t_mx = dma("sync", mx1_d, runmax[:, :])
            p.wait("gpsimd", t_mx)
            tok_c = p.inc("sC")
            p.ops["gpsimd"].append(lambda E, t=tok_c: E.collective_compute(
                "AllReduce", ALU.max, replica_groups=[list(range(NCORES))],
                ins=[mx1_d.rearrange("p o -> o p").opt()],
                outs=[cc1_out.rearrange("p o -> o p").opt()]).then_inc(p.handle(t[0]), 1))
            p.wait("sync", tok_c)
            t_g = dma("sync", grow[0:1, :], cc1_out.rearrange("p o -> o p"))
            p.wait("vector", t_g)
            v_op("reduce_max", out=gsc[0:1, 0:1], in_=grow[0:1, :], axis=AX.X)
            t_s1 = v_op("tensor_scalar", out=gsc[0:1, 0:1], in0=gsc[0:1, 0:1], scalar1=EPS_Q,
                        scalar2=1.0 / 255.0, op0=ALU.max, op1=ALU.mult)
            p.wait("tensor", t_s1)
            t_b = mmop(ps_b1[:, 0:1], ones_row_f[0:1, :], gsc[0:1, 0:1], True, True, inc=True)
            p.wait("vector", t_b)
            v_op("tensor_copy", out=scl[:, 1:2], in_=ps_b1[:, 0:1])
            v_op("reciprocal", out=scl[:, 0:1], in_=scl[:, 1:2])
            p.wait("tensor", t_rz)
            p.wait("tensor", p.last_v)
            t_b2 = mmop(ps_b2[:, 0:1], ones_row_f[0:1, :], gsc[0:1, 2:3], True, True, inc=True)
            p.wait("vector", t_b2)
            v_op("tensor_copy", out=scl[:, 2:3], in_=ps_b2[:, 0:1])

    S23 = contextlib.ExitStack()
    qw1T = sb("qw1T", [128, 8 * FF], BF16, S23)
    with contextlib.ExitStack() as S2:
        rr1 = sb("rr1", [1, FF], F32, S2)
        SWT = contextlib.ExitStack()
        wt = [sb(f"wt{i}", [128, 4096], F32, SWT, side="right") for i in range(2)]
        rbc1 = [sb(f"rbc1_{i}", [128, 1024], F32, S2) for i in range(4)]
        sm32 = sb("sm32", [128, 128], F32, S2)
        sm32b = sb("sm32b", [128, 128], F32, S2)
        sm32i = sb("sm32i", [128, 32], I32, S2)

        with p.phase("P2_wprep"):
            p.drain(tok_xnbf)
            p.drain(tok_xnbf2)
            w1r = w1.rearrange("(g a p) d -> g p a d", p=128, a=4)
            red1 = [None] * 8
            dma("sync", wt[0][:, :], w1r[0])
            dma("scalar", wt[1][:, :], w1r[1])
            for gld in range(8):
                i = gld % 2
                p.wait("vector", (("qS", p.sem["qS"][1]) if i == 0 else ("qA", p.sem["qA"][1])))
                for a in range(4):
                    red1[gld] = v_op("reduce_max", out=sw1c[:, gld * 4 + a: gld * 4 + a + 1],
                                     in_=wt[i][:, a * 1024:(a + 1) * 1024],
                                     axis=AX.X, apply_absolute_value=True)
                if gld + 2 < 8:
                    q = "sync" if i == 0 else "scalar"
                    p.wait(q, red1[gld])
                    dma(q, wt[i][:, :], w1r[gld + 2])
            v_op("tensor_scalar", out=sw1c[:, :], in0=sw1c[:, :], scalar1=EPS_Q,
                 scalar2=1.0 / 127.0, op0=ALU.max, op1=ALU.mult)
            t_rw1 = v_op("reciprocal", out=rw1c[:, :], in_=sw1c[:, :])
            p.wait("sync", p.last_v)
            p.wait("scalar", p.last_v)
            SWT.close()
            w4 = [sb(f"w4{i}", [128, 1024], F32, S2) for i in range(2)]
            ybuf = [sb(f"ybuf{i}", [128, 1024], F32, S2) for i in range(3)]
            ebuf = [sb(f"ebuf{i}", [128, 1024], F32, S2) for i in range(3)]
            ibuf = [sb(f"ibuf{i}", [128, 1024], I32, S2) for i in range(3)]
            def pe_transpose(out_ps, in_ap):
                tok = p.inc("sP")
                p.ops["tensor"].append(
                    lambda E, o=out_ps, i=in_ap, t=tok: E.transpose(
                        out=o, in_=i, identity=ident_sb[:, :]).then_inc(p.handle(t[0]), 1))
                return tok
            p.wait("tensor", t_rw1)
            t_t1 = pe_transpose(ps_b1[0:32, 0:128], rw1c[:, :])
            p.wait("vector", t_t1)
            t_e1t = v_op("tensor_copy", out=sm32[0:32, 0:128], in_=ps_b1[0:32, 0:128])
            p.wait("gpsimd", t_e1t)
            t_d1 = dma("gpsimd", rsw1_d, sm32[0:32, 0:128])
            p.wait("gpsimd", t_d1)
            t_r1 = dma("gpsimd", rr1[0:1, :], rsw1_d.rearrange("t p -> (t p)")[None, :])
            p.wait("tensor", t_r1)
            for hl in range(4):
                for j in range(2):
                    p.wait("tensor", p.last_v)
                    t_b = mmop(ps_b1[:, :], ones_row_f[0:1, :],
                               rr1[0:1, hl * 1024 + j * 512: hl * 1024 + (j + 1) * 512],
                               True, True, inc=True)
                    p.wait("vector", t_b)
                    v_op("tensor_copy", out=rbc1[hl][:, j * 512:(j + 1) * 512], in_=ps_b1[:, :])
            w1h = w1T.rearrange("(g p) (hl f) -> g hl p f", p=128, hl=4)
            nW = 32
            t_l = [None] * nW; t_mu = [None] * nW; t_hp = [None] * nW
            muls_done = {}
            def wq_load(c):
                q = "sync" if c % 2 == 0 else "scalar"
                if c >= 2:
                    p.wait(q, t_mu[c - 2])
                t_l[c] = dma(q, w4[c % 2][:, :], w1h[c // 4, c % 4])
            def wq_mul(c):
                i = c % 3
                p.wait("vector", t_l[c])
                if muls_done.get(i) is not None:
                    p.wait("vector", muls_done[i])
                t_mu[c] = v_op("tensor_mul", out=ybuf[i][:, :], in0=w4[c % 2][:, :],
                               in1=rbc1[c % 4][:, :])
                p.wait("gpsimd", t_mu[c])
                t_hp[c] = dma("gpsimd", ibuf[i][:, :], ybuf[i][:, :],
                              sem=("qG" if c % 2 == 0 else "qG2"))
                muls_done[i] = t_hp[c]
            def wq_fin(c):
                i = c % 3
                p.wait("vector", t_hp[c])
                v_op("tensor_sub", out=ebuf[i][:, :], in0=ybuf[i][:, :], in1=ibuf[i][:, :])
                v_op("scalar_tensor_tensor", out=ybuf[i][:, :], in0=ebuf[i][:, :], scalar=0.5,
                     in1=ibuf[i][:, :], op0=ALU.is_ge, op1=ALU.add)
                v_op("tensor_single_scalar", out=ebuf[i][:, :], in_=ebuf[i][:, :],
                     scalar=-0.5, op=ALU.is_le)
                dc, hl = c // 4, c % 4
                v_op("tensor_sub",
                     out=qw1T[:, dc * FF + hl * 1024: dc * FF + (hl + 1) * 1024],
                     in0=ybuf[i][:, :], in1=ebuf[i][:, :])
            wq_load(0); wq_mul(0); wq_load(1); wq_mul(1); wq_load(2)
            for c in range(nW):
                wq_fin(c)
                if c + 2 < nW:
                    wq_mul(c + 2)
                if c + 3 < nW:
                    wq_load(c + 3)
            t_b1 = dma("sync", sm32[:, 0:32], b1_rs)
            p.wait("vector", t_b1)
            v_op("tensor_mul", out=sm32b[:, 0:32], in0=sm32[:, 0:32], in1=rw1c[:, :])
            t_y0 = v_op("tensor_scalar_mul", out=sm32b[:, 0:32], in0=sm32b[:, 0:32], scalar1=scl[:, 0:1])
            round_full(sm32b[:, 0:32], sm32i[:, :], sm32[:, 0:32], b1i_sb[:, :], t_y0)
            v_op("tensor_scalar_mul", out=sc1_sb[:, :], in0=sw1c[:, :], scalar1=scl[:, 1:2])

    with contextlib.ExitStack() as S3:
        q1 = sb("q1", [128, 8 * TOK], BF16, S3)
        xq32 = [sb(f"xq32{i}", [128, 512], F32, S3) for i in range(2)]
        w4 = [sb(f"w4b{i}", [128, 1024], F32, S3) for i in range(2)]
        ybuf = [sb(f"ybuf_b{i}", [128, 1024], F32, S3) for i in range(3)]
        ebuf = [sb(f"ebuf_b{i}", [128, 1024], F32, S3) for i in range(3)]
        ibuf = [sb(f"ibuf_b{i}", [128, 1024], I32, S3) for i in range(3)]
        ps_h = [psum(f"ps_h{i}", [128, 512], S3) for i in range(6)]
        SWT3 = contextlib.ExitStack()
        wtb = [sb(f"wtb{i}", [128, 2048], F32, SWT3, side="right") for i in range(2)]
        swtmp = sb("swtmp", [128, 2], F32, S3)

        with p.phase("P3_ffn1"):
            for s_ in ("qG", "qG2", "qG3", "qA", "qS"):
                p.drain(p.now(s_))
            w2r = w2.rearrange("(g p) (hl d) -> g hl p d", p=128, hl=2)
            tokp = None
            t_lw2 = [None] * 16
            t_lw2[0] = dma("sync", wtb[0][:, :], w2r[0, 0])
            t_lw2[1] = dma("scalar", wtb[1][:, :], w2r[0, 1])
            for j in range(16):
                i, hl = j // 2, j % 2
                q = "sync" if j % 2 == 0 else "scalar"
                p.wait("vector", t_lw2[j])
                v_op("reduce_max", out=swtmp[:, hl:hl + 1], in_=wtb[j % 2][:, :],
                     axis=AX.X, apply_absolute_value=True)
                if hl == 1:
                    v_op("tensor_max", out=sw2c[:, i:i + 1], in0=swtmp[:, 0:1], in1=swtmp[:, 1:2])
                tokp = p.last_v
                if j + 2 < 16:
                    p.wait(q, tokp)
                    t_lw2[j + 2] = dma(q, wtb[j % 2][:, :], w2r[(j + 2) // 2, (j + 2) % 2])
            v_op("tensor_scalar", out=sw2c[:, :], in0=sw2c[:, :], scalar1=EPS_Q,
                 scalar2=1.0 / 127.0, op0=ALU.max, op1=ALU.mult)
            t_rw2 = v_op("reciprocal", out=rw2c[:, :], in_=sw2c[:, :])
            p.wait("sync", t_rw2)
            p.wait("scalar", t_rw2)
            SWT3.close()
            ht = [sb(f"ht{i}", [128, 512], F32, S3) for i in range(6)]
            rmxall = sb("rmxall", [128, 32], F32, S3)
            rr2 = sb("rr2", [1, D], F32, S3)
            rbc2 = sb("rbc2", [128, D], F32, S3)
            qc2 = [sb("qc2_0", [128, D], BF16, S3)] * 2
            def pe_transpose3(out_ps, in_ap):
                tok = p.inc("sP")
                p.ops["tensor"].append(
                    lambda E, o=out_ps, i=in_ap, t=tok: E.transpose(
                        out=o, in_=i, identity=ident_sb[:, :]).then_inc(p.handle(t[0]), 1))
                return tok
            p.wait("tensor", t_rw2)
            t_t2 = pe_transpose3(ps_b2[0:8, 0:128], rw2c[:, :])
            p.wait("vector", t_t2)
            t_e2t = v_op("tensor_copy", out=ht[0][0:8, 0:128], in_=ps_b2[0:8, 0:128])
            p.wait("gpsimd", t_e2t)
            t_d2 = dma("gpsimd", rsw2_d, ht[0][0:8, 0:128])
            p.wait("gpsimd", t_d2)
            t_r2 = dma("gpsimd", rr2[0:1, :], rsw2_d.rearrange("t p -> (t p)")[None, :])
            p.wait("tensor", t_r2)
            for j in range(2):
                p.wait("tensor", p.last_v)
                t_b = mmop(ps_b2[:, :], ones_row_f[0:1, :], rr2[0:1, j * 512:(j + 1) * 512],
                           True, True, inc=True)
                p.wait("vector", t_b)
                v_op("tensor_copy", out=rbc2[:, j * 512:(j + 1) * 512], in_=ps_b2[:, :])
            t_rbc2_done = p.last_v
            nq = 32
            t_yq = [None] * nq
            t_hopq = [None] * nq
            t_lq = [None] * nq
            def q1_load(j):
                tc, dc = j // 8, j % 8
                t_lq[j] = dma("sync", xq32[j % 2][:, :],
                              xn32_d[dc * 128:(dc + 1) * 128, tc * 512:(tc + 1) * 512])
            def q1_mul(j):
                i = j % 3
                p.wait("vector", t_lq[j])
                t_yq[j] = v_op("tensor_scalar", out=ybuf[i][:, 0:512], in0=xq32[j % 2][:, :],
                               scalar1=scl[:, 0:1], scalar2=0.0, op0=ALU.mult, op1=ALU.max)
                p.wait("gpsimd", t_yq[j])
                t_hopq[j] = dma("gpsimd", ibuf[i][:, 0:512], ybuf[i][:, 0:512],
                                sem=("qG" if j % 2 == 0 else "qG2"))
            def q1_fin(j):
                i = j % 3
                tc, dc = j // 8, j % 8
                p.wait("vector", t_hopq[j])
                v_op("tensor_sub", out=ebuf[i][:, 0:512], in0=ybuf[i][:, 0:512], in1=ibuf[i][:, 0:512])
                v_op("scalar_tensor_tensor",
                     out=q1[:, dc * TOK + tc * 512: dc * TOK + (tc + 1) * 512],
                     in0=ebuf[i][:, 0:512], scalar=0.5,
                     in1=ibuf[i][:, 0:512], op0=ALU.is_ge, op1=ALU.add)
            q1_load(0); q1_mul(0); q1_load(1); q1_mul(1)
            p.wait("sync", t_yq[0])
            q1_load(2)
            for j in range(nq):
                q1_fin(j)
                if j + 2 < nq:
                    q1_mul(j + 2)
                if j + 3 < nq:
                    p.wait("sync", t_yq[j + 1])
                    q1_load(j + 3)
            t_l2 = [None] * 32; t_mu2 = [None] * 32; t_hp2 = [None] * 32; t_st2 = [None] * 32
            muls2 = {}
            def w2_load(c):
                q = "sync" if c % 2 == 0 else "scalar"
                if c >= 2:
                    p.wait(q, t_mu2[c - 2])
                t_l2[c] = dma(q, w4[c % 2][:, :], w2T[c * 128:(c + 1) * 128, :])
            def w2_mul(c):
                i = c % 3
                p.wait("vector", t_l2[c])
                if muls2.get(i) is not None:
                    p.wait("vector", muls2[i])
                t_mu2[c] = v_op("tensor_mul", out=ybuf[i][:, :], in0=w4[c % 2][:, :],
                                in1=rbc2[:, :])
                p.wait("gpsimd", t_mu2[c])
                t_hp2[c] = dma("gpsimd", ibuf[i][:, :], ybuf[i][:, :],
                               sem=("qG" if c % 2 == 0 else "qG2"))
                muls2[i] = t_hp2[c]
            def w2_fin(c):
                i = c % 3
                p.wait("vector", t_hp2[c])
                v_op("tensor_sub", out=ebuf[i][:, :], in0=ybuf[i][:, :], in1=ibuf[i][:, :])
                v_op("scalar_tensor_tensor", out=ybuf[i][:, :], in0=ebuf[i][:, :], scalar=0.5,
                     in1=ibuf[i][:, :], op0=ALU.is_ge, op1=ALU.add)
                v_op("tensor_single_scalar", out=ebuf[i][:, :], in_=ebuf[i][:, :],
                     scalar=-0.5, op=ALU.is_le)
                if c >= 1 and t_st2[c - 1] is not None:
                    p.wait("vector", t_st2[c - 1])
                v_op("tensor_sub", out=qc2[c % 2][:, :], in0=ybuf[i][:, :], in1=ebuf[i][:, :])
                p.wait("gpsimd", p.last_v)
                t_st2[c] = dma("gpsimd", qw2_d[c * 128:(c + 1) * 128, :], qc2[c % 2][:, :],
                               sem="qG3")
            def w2_gen():
                w2_load(0); w2_mul(0); w2_load(1); w2_mul(1); w2_load(2)
                yield
                for c in range(32):
                    w2_fin(c)
                    if c + 2 < 32:
                        w2_mul(c + 2)
                    if c + 3 < 32:
                        w2_load(c + 3)
                    yield
            gen = w2_gen()
            p.wait("tensor", p.last_v)
            p.wait("vector", p.now("qG"))
            war = {}
            ev_tok = {}
            red_tok = {}
            for tc in range(4):
                for fc in range(32):
                    k = tc * 32 + fc
                    pb = k % 6
                    if k - 6 in war:
                        p.wait("tensor", war[k - 6])
                    t_mm = None
                    for dc in range(8):
                        t_mm = mmop(ps_h[pb][:, :],
                                    qw1T[:, dc * FF + fc * 128: dc * FF + (fc + 1) * 128],
                                    q1[:, dc * TOK + tc * 512: dc * TOK + (tc + 1) * 512],
                                    dc == 0, dc == 7, inc=(dc == 7))
                    p.wait("scalar", t_mm)
                    if k >= 6:
                        p.wait("scalar", ("qA", 1))
                        p.wait("scalar", red_tok[k - 6])
                    t_e1 = a_op("activation", out=ht[k % 6][:, :], in_=ps_h[pb][:, :],
                                func=AF.Relu, bias=b1i_sb[:, fc:fc + 1], scale=1.0)
                    p.wait("vector", t_e1)
                    red_tok[k] = v_op("reduce_max", out=rmxall[:, fc:fc + 1], in_=ht[k % 6][:, :], axis=AX.X)
                    war[k] = t_e1
                    ev_tok[k] = t_e1
                    if k % 4 == 3:
                        for kk in range(k - 3, k + 1):
                            p.wait("scalar", ev_tok[kk])
                            dma("scalar",
                                h_d[(kk % 32) * 128:(kk % 32 + 1) * 128,
                                    (kk // 32) * 512:(kk // 32 + 1) * 512],
                                ht[kk % 6][:, :])
                        next(gen, None)
                p.wait("vector", p.last_v)
                v_op("tensor_mul", out=rmxall[:, :], in0=rmxall[:, :], in1=sc1_sb[:, :])
                v_op("reduce_max", out=rmx[:, :], in_=rmxall[:, :], axis=AX.X)
                v_op("tensor_max", out=hmax[:, :], in0=hmax[:, :], in1=rmx[:, :])
            for _ in range(40):
                next(gen, None)
    tok_qw2 = p.now("qG3")
    tok_h = p.now("qA")

    with p.phase("P3b_cc2"):
        p.wait("sync", p.last_v)
        t_mx = dma("sync", mx2_d, hmax[:, :])
        p.wait("gpsimd", t_mx)
        tok_c = p.inc("sC")
        p.ops["gpsimd"].append(lambda E, t=tok_c: E.collective_compute(
            "AllReduce", ALU.max, replica_groups=[list(range(NCORES))],
            ins=[mx2_d.rearrange("p o -> o p").opt()],
            outs=[cc2_out.rearrange("p o -> o p").opt()]).then_inc(p.handle(t[0]), 1))
        p.wait("sync", tok_c)
        t_g = dma("sync", grow[0:1, :], cc2_out.rearrange("p o -> o p"))
        p.wait("vector", t_g)
        v_op("reduce_max", out=gsc[0:1, 4:5], in_=grow[0:1, :], axis=AX.X)
        t_s2 = v_op("tensor_scalar", out=gsc[0:1, 4:5], in0=gsc[0:1, 4:5], scalar1=EPS_Q,
                    scalar2=1.0 / 255.0, op0=ALU.max, op1=ALU.mult)
        p.wait("tensor", t_s2)
        t_b = mmop(ps_b1[:, 0:1], ones_row_f[0:1, :], gsc[0:1, 4:5], True, True, inc=True)
        p.wait("vector", t_b)
        v_op("tensor_copy", out=scl[:, 4:5], in_=ps_b1[:, 0:1])
        v_op("reciprocal", out=scl[:, 3:4], in_=scl[:, 4:5])
        t_b2l = dma("sync", t8a[:, :], b2_rs)
        p.wait("vector", t_b2l)
        v_op("tensor_mul", out=t8a[:, :], in0=t8a[:, :], in1=rw2c[:, :])
        t_y = v_op("tensor_scalar_mul", out=t8a[:, :], in0=t8a[:, :], scalar1=scl[:, 3:4])
        round_full(t8a[:, :], t8i[:, :], t8e[:, :], b2i_sb[:, :], t_y)
        v_op("tensor_scalar_mul", out=sc2_sb[:, :], in0=sw2c[:, :], scalar1=scl[:, 4:5])
        v_op("tensor_scalar_mul", out=sc1f_sb[:, :], in0=sc1_sb[:, :], scalar1=scl[:, 3:4])
    S23.close()
    PSB.close()

    SAtt = contextlib.ExitStack()
    OT = sb("OT", [128, 8 * TOK], BF16, SAtt)
    u_all = sb("u_all", [72, SEQ], BF16, SAtt)
    u_o = sb("u_o", [8, 512], BF16, SAtt)

    KEEPS_ALL = _keep_sets()
    with contextlib.ExitStack() as S4:
        xnt = [sb(f"xnt{i}", [128, 4096], BF16, S4) for i in range(2)]
        QT = sb("QT", [128, 4 * TOK], BF16, S4)
        KT = sb("KT", [128, 4 * SEQ], BF16, S4)
        vt32 = sb("vt32", [128, 512], F32, S4)
        bvbc = sb("bvbc", [128, 512], F32, S4)
        BqT_bf = sb("BqT_bf", [8, D], BF16, S4)
        BkT_bf = sb("BkT_bf", [40, D], BF16, S4)
        BvT_bf = sb("BvT_bf", [72, D], BF16, S4)
        AT_bf = sb("AT_bf", [128, 8 * 24], BF16, S4)

        t_Bv = None
        for g in range(2):
            gsl = slice(g * 512, (g + 1) * 512)
            SWg = contextlib.ExitStack()
            keeps_g = [KEEPS_ALL[g * 8 + h] for h in range(8)]
            posof = [{kt: i for i, kt in enumerate(kp)} for kp in keeps_g]
            Vp = [sb(f"Vp{g}_{h}", [128, len(keeps_g[h]) * 65 + 128], BF16, SWg) for h in range(8)]
            SWa = contextlib.ExitStack()
            wq_bf = sb(f"wq_bf{g}", [128, 8 * 512], BF16, SWa)
            wk_bf = sb(f"wk_bf{g}", [128, 8 * 512], BF16, SWa)
            wv_bf = sb(f"wv_bf{g}", [128, 8 * 512], BF16, SWa)
            ps_p = [psum(f"ps_p{g}{i}", [128, 512], SWa) for i in range(4)]
            ps_u4 = psum(f"ps_u4{g}", [24, 512], SWa)
            ps_bv = psum(f"ps_bv{g}", [128, 512], SWa)
            with p.phase(f"P4a_proj{g}"):
                p.drain(tok_h)
                p.drain(tok_xnbf2)
                t_wv = None
                for dc in range(8):
                    rows = slice(dc * 128, (dc + 1) * 128)
                    dma("gpsimd", wq_bf[:, dc * 512:(dc + 1) * 512], wqT[rows, gsl])
                    dma("gpsimd", wk_bf[:, dc * 512:(dc + 1) * 512], wkT[rows, gsl])
                    t_wv = dma("gpsimd", wv_bf[:, dc * 512:(dc + 1) * 512], wvT[rows, gsl])
                    if g == 0:
                        dma("gpsimd", AT_bf[:, dc * 24:(dc + 1) * 24], AT[rows, :])
                if g == 0:
                    dma("gpsimd", BqT_bf[0:8, :], BqT)
                    dma("gpsimd", BkT_bf[32:40, :], BkT)
                    t_Bv = dma("gpsimd", BvT_bf[64:72, :], BvT)
                t_wload = p.now("qG")
                p.wait("tensor", p.last_v)
                t_b = mmop(ps_bv[:, :], ones_row_f[0:1, :], bv_rsb[0:1, gsl], True, True, inc=True)
                p.wait("vector", t_b)
                t_bvbc = v_op("tensor_copy", out=bvbc[:, :], in_=ps_bv[:, :])
                p.wait("vector", t_wload)
                for h in range(8):
                    nk_h = len(keeps_g[h])
                    v_op("memset", ap=Vp[h].ap()[:, nk_h * 65: nk_h * 65 + 128], constant=0.0)
                for h in range(8):
                    hh = g * 8 + h
                    nk_h = len(keeps_g[h])
                    fcol = Vp[h].ap()[:, 0:nk_h * 65].rearrange("p (k c) -> p k c", c=65)[:, :, 64]
                    for pos, kt in enumerate(keeps_g[h]):
                        v_op("tensor_copy", out=fcol[:, pos:pos + 1],
                             in_=fb_sb[:, hh * NKT + kt: hh * NKT + kt + 1])
                ev_war = [None, None, None, None]
                rotc = [0]
                uo_war = None
                prev_iter = {}
                for tc in range(8):
                    i4 = tc % 2
                    tsl = slice(tc * 512, (tc + 1) * 512)
                    if tc - 2 in prev_iter:
                        p.wait("sync", prev_iter[tc - 2])
                    xnr = xnbf_d.rearrange("(dc p) t -> p dc t", p=128)
                    t_xl = dma("sync", xnt[i4][:, :], xnr[:, :, tsl])
                    xl = [t_xl] * 8
                    p.wait("tensor", t_wload)
                    g_dma1 = g_dma2 = None
                    if g == 0:
                        if uo_war is not None:
                            p.wait("tensor", uo_war)
                        t_u = None
                        for dc in range(8):
                            p.wait("tensor", xl[dc])
                            t_u = mmop(ps_u4[:, :], AT_bf[:, dc * 24:(dc + 1) * 24],
                                       xnt[i4][:, dc * 512:(dc + 1) * 512],
                                       dc == 0, dc == 7, inc=(dc == 7))
                        p.wait("scalar", t_u)
                        t_uc = a_op("activation", out=u_all[0:24, tsl], in_=ps_u4[:, :],
                                    func=AF.Copy, bias=0.0, scale=0.125)
                        uo_war = t_uc
                        p.wait("gpsimd", t_uc)
                        g_dma1 = dma("gpsimd", u_all[32:40, tsl], u_all[8:16, tsl])
                        g_dma2 = dma("gpsimd", u_all[64:72, tsl], u_all[16:24, tsl])
                    for mc in range(4):
                        pb = rotc[0] % 4; rotc[0] += 1
                        if ev_war[pb] is not None:
                            p.wait("tensor", ev_war[pb])
                        for dc in range(8):
                            p.wait("tensor", xl[dc])
                            mmop(ps_p[pb][:, :],
                                 wk_bf[:, dc * 512 + mc * 128: dc * 512 + (mc + 1) * 128],
                                 xnt[i4][:, dc * 512:(dc + 1) * 512], dc == 0, False)
                        if g == 0:
                            p.wait("tensor", g_dma1)
                        t_k = mmop(ps_p[pb][:, :],
                                   BkT_bf[32:40, g * 512 + mc * 128: g * 512 + (mc + 1) * 128],
                                   u_all[32:40, tsl], False, True, inc=True)
                        p.wait("scalar", t_k)
                        ev_war[pb] = a_op("activation",
                                          out=KT[:, mc * SEQ + tc * 512: mc * SEQ + (tc + 1) * 512],
                                          in_=ps_p[pb][:, :], func=AF.Identity,
                                          bias=bk_sb[:, g * 4 + mc: g * 4 + mc + 1], scale=1.0)
                    if tc < 4:
                        for mc in range(4):
                            pb = rotc[0] % 4; rotc[0] += 1
                            if ev_war[pb] is not None:
                                p.wait("tensor", ev_war[pb])
                            for dc in range(8):
                                mmop(ps_p[pb][:, :],
                                     wq_bf[:, dc * 512 + mc * 128: dc * 512 + (mc + 1) * 128],
                                     xnt[i4][:, dc * 512:(dc + 1) * 512], dc == 0, False)
                            t_q = mmop(ps_p[pb][:, :],
                                       BqT_bf[0:8, g * 512 + mc * 128: g * 512 + (mc + 1) * 128],
                                       u_all[0:8, tsl], False, True, inc=True)
                            p.wait("scalar", t_q)
                            ev_war[pb] = a_op("activation",
                                              out=QT[:, mc * TOK + tc * 512: mc * TOK + (tc + 1) * 512],
                                              in_=ps_p[pb][:, :], func=AF.Identity,
                                              bias=bqs_sb[:, g * 4 + mc: g * 4 + mc + 1], scale=0.125)
                    for j in range(4):
                        tt = tc * 4 + j
                        pb = rotc[0] % 4; rotc[0] += 1
                        if ev_war[pb] is not None:
                            p.wait("tensor", ev_war[pb])
                        for dc in range(8):
                            mmop(ps_p[pb][:, :],
                                 xnt[i4][:, dc * 512 + j * 128: dc * 512 + (j + 1) * 128],
                                 wv_bf[:, dc * 512:(dc + 1) * 512], dc == 0, False)
                        if g == 0:
                            p.wait("tensor", g_dma2)
                            p.wait("tensor", t_Bv)
                        t_v = mmop(ps_p[pb][:, :], u_all[64:72, tt * 128:(tt + 1) * 128],
                                   BvT_bf[64:72, gsl], False, True, inc=True)
                        p.wait("vector", t_v)
                        p.wait("vector", t_bvbc)
                        t_vt = v_op("tensor_add", out=vt32[:, :], in0=ps_p[pb][:, :], in1=bvbc[:, :])
                        ev_war[pb] = t_vt
                        for h in range(8):
                            if tt not in posof[h]:
                                continue
                            pos = posof[h][tt]
                            v_op("tensor_scalar_mul",
                                 out=Vp[h].ap()[:, pos * 65: pos * 65 + 64],
                                 in0=vt32[:, h * 64:(h + 1) * 64],
                                 scalar1=fb_sb[:, (g * 8 + h) * NKT + tt:(g * 8 + h) * NKT + tt + 1])
                    prev_iter[tc] = t_v
            SWa.close()

            SWb = contextlib.ExitStack()
            Pt = [sb(f"Pt{g}{i}", [128, 1536], BF16, SWb) for i in range(2)]
            lrow = [sb(f"lrow{g}{i}", [128, 512], F32, SWb) for i in range(2)]
            rbx = [sb(f"rbx{g}{i}", [64, 512], F32, SWb) for i in range(2)]
            otmp = [sb(f"otmp{g}{i}", [64, 512], BF16, SWb) for i in range(2)]
            ps_S = [psum(f"ps_S{g}{i}", [128, 1536], SWb) for i in range(2)]
            ps_PV = [psum(f"ps_PV{g}{i}", [128, 512], SWb) for i in range(2)]
            with p.phase(f"P4b_attn{g}"):
                pv_ev = [None, None]
                s_free = [None, None]
                pt_free = [None, None]
                pend = None

                def emit_epi(hq, qq, pvbq, t_pvq):
                    pjq, subq = hq // 2, hq % 2
                    ot_base = (g * 4 + pjq) * TOK + qq * 512
                    p.wait("vector", t_pvq)
                    v_op("tensor_copy", out=lrow[pvbq][64:65, :], in_=ps_PV[pvbq][64:65, :])
                    t_rc = v_op("reciprocal", out=lrow[pvbq][64:65, :], in_=lrow[pvbq][64:65, :])
                    p.wait("gpsimd", t_rc)
                    t_b1 = dma("gpsimd", lr_d[pvbq], lrow[pvbq][64:65, :], sem="qG3")
                    p.wait("gpsimd", t_b1)
                    t_bc = dma("gpsimd", rbx[pvbq][:, :], lr_d[pvbq].broadcast_to((64, 512)), sem="qG3")
                    p.wait("vector", t_bc)
                    if subq == 0:
                        t_o = v_op("tensor_mul",
                                   out=OT.ap()[0:64, ot_base:ot_base + 512],
                                   in0=ps_PV[pvbq][0:64, :], in1=rbx[pvbq][:, :])
                    else:
                        t_o = v_op("tensor_mul", out=otmp[pvbq][:, :],
                                   in0=ps_PV[pvbq][0:64, :], in1=rbx[pvbq][:, :])
                        p.wait("gpsimd", t_o)
                        t_mv2 = dma("gpsimd", OT.ap()[64:128, ot_base:ot_base + 512],
                                    otmp[pvbq][:, :])
                        p.wait("vector", t_mv2)
                    pv_ev[pvbq] = t_o

                for h in range(8):
                    hh = g * 8 + h
                    pj, sub = h // 2, h % 2
                    rows = slice(64 * sub, 64 * sub + 64)
                    tp = (64 * sub, 0)
                    kept = keeps_g[h]
                    grps = [kept[i:i + 3] for i in range(0, len(kept), 3)]
                    ng = len(grps)
                    vH = Vp[h]
                    for qc in range(4):
                        qsl = slice(pj * TOK + qc * 512, pj * TOK + (qc + 1) * 512)
                        pvb = qc % 2
                        t_qk = [None] * ng

                        def emit_qk(gi, first):
                            sbk = gi % 2
                            p.wait("tensor", s_free[sbk])
                            if first:
                                p.wait("tensor", pv_ev[pvb])
                            tq = None
                            for idx, kt in enumerate(grps[gi]):
                                ksl = slice(pj * SEQ + kt * 128, pj * SEQ + (kt + 1) * 128)
                                tq = mmop(ps_S[sbk][:, idx * 512:(idx + 1) * 512],
                                          KT.ap()[rows, ksl], QT.ap()[rows, qsl],
                                          True, True, tp=tp, inc=(idx == len(grps[gi]) - 1))
                            t_qk[gi] = tq

                        emit_qk(0, True)
                        if ng > 1:
                            emit_qk(1, False)
                        t_pv = None
                        for gi, grp in enumerate(grps):
                            sbk = gi % 2
                            p.wait("scalar", t_qk[gi])
                            p.wait("scalar", pt_free[sbk])
                            n = len(grp) * 512
                            t_act = a_op_nochain("activation", out=Pt[sbk][:, 0:n], in_=ps_S[sbk][:, 0:n],
                                                 func=AF.Exp, bias=e1_sb[:, hh:hh + 1], scale=1.0)
                            s_free[sbk] = t_act
                            p.wait("tensor", t_act)
                            if gi + 2 < ng:
                                emit_qk(gi + 2, False)
                            for idx, kt in enumerate(grp):
                                pos = posof[h][kt]
                                t_pv = mmop(ps_PV[pvb][:, :], vH.ap()[:, pos * 65: pos * 65 + 128],
                                            Pt[sbk][:, idx * 512:(idx + 1) * 512],
                                            (gi == 0 and idx == 0),
                                            (gi == ng - 1 and idx == len(grp) - 1),
                                            inc=(idx == len(grp) - 1))
                            pt_free[sbk] = t_pv
                            if gi == 1 and pend is not None:
                                emit_epi(*pend)
                                pend = None
                        if pend is not None:
                            emit_epi(*pend)
                            pend = None
                        pend = (h, qc, pvb, t_pv)
                if pend is not None:
                    emit_epi(*pend)
                    pend = None
            SWb.close()
            SWg.close()
    tok_ot = p.now("qG")

    s_all = sb("s_all", [128, 8 * TOK], F32, side="right")
    with contextlib.ExitStack() as S5:
        wo_bf = sb("wo_bf", [128, 8 * 1024], BF16, S5)
        AoT_bf = sb("AoT_bf", [128, 8 * 8], BF16, S5)
        BoT_bf = sb("BoT_bf", [8, D], BF16, S5)
        x5 = [sb(f"x5{i}", [128, 512], F32, S5) for i in range(2)]
        r5 = sb("r5", [1, SEQ], F32, S5)
        mr5 = sb("mr5", [1, SEQ], F32, S5)
        rbc5 = sb("rbc5", [128, 512], F32, S5)
        mrbc5 = sb("mrbc5", [128, 512], F32, S5)
        stmp = sb("stmp", [128, 512], F32, S5)
        stmp2 = sb("stmp2", [128, 512], F32, S5)
        ps_o = [psum(f"ps_o{i}", [128, 512], S5) for i in range(2)]
        ps_uo = psum("ps_uo", [24, 512], S5)
        ps_b51 = psum("ps_b51", [128, 512], S5)
        ps_b52 = psum("ps_b52", [128, 512], S5)

        with p.phase("P5_oproj"):
            p.drain(tok_ot)
            for dc in range(8):
                rows = slice(dc * 128, (dc + 1) * 128)
                dma("gpsimd", wo_bf[:, dc * 1024:(dc + 1) * 1024], woT[rows, :])
                dma("gpsimd", AoT_bf[:, dc * 8:(dc + 1) * 8], AoT[rows, :])
            t_Bo = dma("gpsimd", BoT_bf[:, :], BoT)
            t_rr = dma("sync", r5[0:1, :], r_d)
            dma("sync", mr5[0:1, :], mr_d)
            t_wo = p.now("qG")
            p.wait("tensor", tok_ot)
            ev_war = [None, None]
            uo_war = None
            for tc in range(4):
                tsl = slice(tc * 512, (tc + 1) * 512)
                p.wait("tensor", p.last_v)
                p.wait("tensor", t_rr)
                t_b1m = mmop(ps_b51[:, :], ones_row_f[0:1, :], r5[0:1, tsl], True, True, inc=True)
                t_b2m = mmop(ps_b52[:, :], ones_row_f[0:1, :], mr5[0:1, tsl], True, True, inc=True)
                p.wait("vector", t_b1m)
                v_op("tensor_copy", out=rbc5[:, :], in_=ps_b51[:, :])
                p.wait("vector", t_b2m)
                v_op("tensor_copy", out=mrbc5[:, :], in_=ps_b52[:, :])
                p.wait("tensor", t_wo)
                if uo_war is not None:
                    p.wait("tensor", uo_war)
                t_u = None
                for dc in range(8):
                    t_u = mmop(ps_uo[0:8, :], AoT_bf[:, dc * 8:(dc + 1) * 8],
                               OT.ap()[:, dc * TOK + tc * 512: dc * TOK + (tc + 1) * 512],
                               dc == 0, dc == 7, inc=(dc == 7))
                p.wait("scalar", t_u)
                t_uo = a_op("activation", out=u_o[0:8, :], in_=ps_uo[0:8, :],
                            func=AF.Copy, bias=0.0, scale=0.125)
                uo_war = t_uo
                for mc in range(8):
                    pb = mc % 2
                    t_lx = dma("sync", x5[pb][:, :], xT[mc * 128:(mc + 1) * 128, tsl])
                    if ev_war[pb] is not None:
                        p.wait("tensor", ev_war[pb])
                    for dc in range(8):
                        mmop(ps_o[pb][:, :],
                             wo_bf[:, dc * 1024 + mc * 128: dc * 1024 + (mc + 1) * 128],
                             OT.ap()[:, dc * TOK + tc * 512: dc * TOK + (tc + 1) * 512],
                             dc == 0, False)
                    p.wait("tensor", t_uo)
                    p.wait("tensor", t_Bo)
                    t_om = mmop(ps_o[pb][:, :], BoT_bf[0:8, mc * 128:(mc + 1) * 128],
                                u_o[0:8, :], False, True, inc=True)
                    p.wait("vector", t_om)
                    v_op("tensor_scalar", out=stmp[:, :], in0=ps_o[pb][:, :],
                         scalar1=bo_sb[:, mc:mc + 1], scalar2=scl[:, 2:3],
                         op0=ALU.add, op1=ALU.mult)
                    ev_war[pb] = p.last_v
                    p.wait("vector", t_lx)
                    v_op("tensor_mul", out=stmp2[:, :], in0=x5[pb][:, :], in1=rbc5[:, :])
                    v_op("tensor_sub", out=stmp2[:, :], in0=stmp2[:, :], in1=mrbc5[:, :])
                    t_s = v_op("tensor_add",
                               out=s_all[:, mc * TOK + tc * 512: mc * TOK + (tc + 1) * 512],
                               in0=stmp[:, :], in1=stmp2[:, :])
                    p.wait("sync", t_s)
    SAtt.close()

    with contextlib.ExitStack() as S7:
        hb = [sb(f"hb{i}", [128, 512], F32, S7) for i in range(6)]
        q2b = [sb(f"q2b{i}", [128, 512], BF16, S7) for i in range(4)]
        w2b = [sb(f"w2b{i}", [128, 1024], BF16, S7) for i in range(6)]
        x7 = [sb(f"x7{i}", [128, 512], F32, S7) for i in range(8)]
        yb7 = [sb(f"yb7{i}", [128, 512], F32, S7) for i in range(4)]
        eb7 = [sb(f"eb7{i}", [128, 512], F32, S7) for i in range(4)]
        ib7 = [sb(f"ib7{i}", [128, 512], I32, S7) for i in range(4)]
        yt7 = [sb(f"yt7{i}", [128, 512], F32, S7) for i in range(8)]
        gt7 = [sb(f"gt7{i}", [128, 512], F32, S7) for i in range(8)]
        ot7 = [sb(f"ot7{i}", [128, 512], F32, S7) for i in range(8)]
        ps_y = [psum(f"ps_y{i}", [128, 512], S7) for i in range(8)]

        with p.phase("P7_ffn2"):
            p.wait("sync", tok_h)
            p.wait("sync", tok_qw2)
            t_out_final = None
            last_ev = None
            prev_tail = None
            for tc in range(4):
                tsl = slice(tc * 512, (tc + 1) * 512)
                nf = 32
                t_lh = [None] * nf; t_lw = [None] * nf
                t_y = [None] * nf; t_hop = [None] * nf; t_q = [None] * nf
                t_fc = [None] * nf
                def f_load(fc):
                    i = fc % 6
                    q = "sync" if fc % 2 == 0 else "scalar"
                    if fc >= 6:
                        p.wait(q, t_y[fc - 6])
                        p.wait(q, t_fc[fc - 6] if t_fc[fc - 6] is not None else None)
                    t_lh[fc] = dma(q, hb[i][:, :], h_d[fc * 128:(fc + 1) * 128, tsl])
                    t_lw[fc] = dma(q, w2b[i][:, :], qw2_d[fc * 128:(fc + 1) * 128, :])
                def f_mul(fc):
                    i = fc % 4
                    p.wait("scalar", t_lh[fc])
                    if fc >= 4:
                        p.wait("scalar", t_hop[fc - 4])
                        p.wait("scalar", t_q[fc - 4])
                    t_y[fc] = a_op("mul", out=yb7[i][:, :], in_=hb[fc % 6][:, :],
                                   mul=sc1f_sb[:, fc:fc + 1])
                    p.wait("gpsimd", t_y[fc])
                    t_hop[fc] = dma("gpsimd", ib7[i][:, :], yb7[i][:, :],
                                    sem=("qG" if fc % 2 == 0 else "qG2"))
                def f_fin(fc):
                    i = fc % 4
                    p.wait("vector", t_hop[fc])
                    if fc >= 4:
                        p.wait("vector", t_fc[fc - 4])
                    v_op("tensor_sub", out=eb7[i][:, :], in0=yb7[i][:, :], in1=ib7[i][:, :])
                    t_q[fc] = v_op("scalar_tensor_tensor", out=q2b[i][:, :],
                                   in0=eb7[i][:, :], scalar=0.5,
                                   in1=ib7[i][:, :], op0=ALU.is_ge, op1=ALU.add)
                def f_mm(fc):
                    i = fc % 4
                    p.wait("tensor", t_q[fc])
                    p.wait("tensor", t_lw[fc])
                    if fc == 0 and last_ev is not None:
                        p.wait("tensor", last_ev)
                    tl = None
                    for mc in range(8):
                        tl = mmop(ps_y[mc][:, :], w2b[fc % 6][:, mc * 128:(mc + 1) * 128],
                                  q2b[i][:, :], fc == 0, fc == 31,
                                  inc=(mc == 7))
                    t_fc[fc] = tl
                if tc > 0:
                    p.wait("sync", prev_fc_last)
                    p.wait("scalar", prev_fc_last)
                    p.wait("sync", prev_q_last)
                    p.wait("scalar", prev_q_last)
                    p.wait("sync", prev_y_last)
                    p.wait("scalar", prev_y_last)
                f_load(0); f_mul(0); f_load(1); f_mul(1); f_load(2); f_fin(0)
                f_mul(2); f_load(3); f_fin(1); f_mul(3); f_load(4); f_load(5)
                t_lx = [None] * 8
                for fc in range(nf):
                    f_mm(fc)
                    if fc + 2 < nf:
                        f_fin(fc + 2)
                    if fc + 4 < nf:
                        f_mul(fc + 4)
                    if fc + 6 < nf:
                        f_load(fc + 6)
                    if fc == 24:
                        for mc in range(8):
                            if prev_tail is not None:
                                p.wait("sync", prev_tail)
                            t_lx[mc] = dma("sync", x7[mc][:, :], xT[mc * 128:(mc + 1) * 128, tsl])
                prev_fc_last = t_fc[31]
                prev_q_last = t_q[31]
                prev_y_last = t_y[31]
                t_mm = p.now("sP")
                ts_tok = [None] * 8
                si_tok = [None] * 8
                p.wait("vector", t_mm)
                for mc in range(8):
                    ts_tok[mc] = v_op("tensor_scalar", out=yt7[mc][:, :], in0=ps_y[mc][:, :],
                                      scalar1=b2i_sb[:, mc:mc + 1], scalar2=sc2_sb[:, mc:mc + 1],
                                      op0=ALU.add, op1=ALU.mult)
                for mc in range(8):
                    p.wait("scalar", ts_tok[mc])
                    si_tok[mc] = a_op("activation", out=gt7[mc][:, :], in_=yt7[mc][:, :],
                                      func=AF.Silu, bias=0.0, scale=1.0)
                p.wait("vector", t_lx[0])
                p.wait("vector", p.now("qO"))
                for mc in range(8):
                    p.wait("vector", si_tok[mc])
                    v_op("tensor_mul", out=ot7[mc][:, :], in0=gt7[mc][:, :],
                         in1=s_all[:, mc * TOK + tc * 512: mc * TOK + (tc + 1) * 512])
                    t_o = v_op("tensor_add", out=ot7[mc][:, :], in0=ot7[mc][:, :],
                               in1=x7[mc][:, :])
                    p.wait("sync", t_o)
                    t_out_final = dma("sync", outT[mc * 128:(mc + 1) * 128, tsl], ot7[mc][:, :],
                                      sem="qO")
                prev_tail = t_o
                last_ev = ("sV", ts_tok[7][1])
            p.wait("sync", p.now("qO"))

    stack.close()
    return nc


def kernel(**inputs):
    x = np.ascontiguousarray(inputs["L0_attn_x"]).astype(np.float32, copy=False)
    Wq, Wk, Wv, Wo = (inputs[f"L0_attn_W{n}"] for n in "qkvo")
    bq, bk, bv, bo = (inputs[f"L0_attn_b{n}"] for n in "qkvo")
    Aq, Ak, Av, Ao = (inputs[f"L0_attn_A{n}"] for n in "qkvo")
    Bq, Bk, Bv, Bo = (inputs[f"L0_attn_B{n}"] for n in "qkvo")
    rez = inputs["L0_attn_rezero"]
    w1, b1 = inputs["L0_ffn_w1"], inputs["L0_ffn_b1"]
    w2, b2 = inputs["L0_ffn_w2"], inputs["L0_ffn_b2"]

    C = np.ascontiguousarray
    shared = {
        "wqT": C(Wq.T), "wkT": C(Wk.T), "wvT": C(Wv.T), "woT": C(Wo.T),
        "bq_rs": C(np.asarray(bq).reshape(8, 128).T), "bk_rs": C(np.asarray(bk).reshape(8, 128).T),
        "bo_rs": C(np.asarray(bo).reshape(8, 128).T), "bv_row": C(np.asarray(bv).reshape(1, D)),
        "AT": C(np.concatenate([np.asarray(Aq).T, np.asarray(Ak).T, np.asarray(Av).T], axis=1)),
        "AoT": C(np.asarray(Ao).T),
        "BqT": C(np.asarray(Bq).T), "BkT": C(np.asarray(Bk).T),
        "BvT": C(np.asarray(Bv).T), "BoT": C(np.asarray(Bo).T),
        "rez": C(np.asarray(rez).reshape(1, 1)),
        "w1": C(w1), "w1T": C(np.asarray(w1).T), "w2": C(w2), "w2T": C(np.asarray(w2).T),
        "b1_rs": C(np.asarray(b1).reshape(32, 128).T), "b2_rs": C(np.asarray(b2).reshape(8, 128).T),
        "e1tab": _e1_table(),
        "ident": np.eye(128, dtype=np.float32),
    }
    fb = [_fb_table(0), _fb_table(1)]
    in_maps = []
    for c in range(NCORES):
        b, half = c // 2, c % 2
        xb_ = x[b]
        xrot = np.concatenate([xb_[half * TOK:(half + 1) * TOK],
                               xb_[(1 - half) * TOK:(2 - half) * TOK]], axis=0)
        m = dict(shared)
        m["xT"] = C(xrot.T)
        m["fbtab"] = fb[half]
        in_maps.append(m)

    nc = bass.Bass("TRN2", target_bir_lowering=False, debug=False)
    _build(nc)
    res = run_bass_kernel_spmd(nc, in_maps, core_ids=list(range(NCORES)))
    out = np.empty((B, SEQ, D), dtype=np.float32)
    for c in range(NCORES):
        b, half = c // 2, c % 2
        out[b, half * TOK:(half + 1) * TOK, :] = np.asarray(res.results[c]["outT"]).T
    return out
